# version 03
# baseline (speedup 1.0000x reference)
import numpy as np
import ml_dtypes
from contextlib import ExitStack
import concourse.bass as bass
import concourse.mybir as mybir
from concourse.bass_utils import run_bass_kernel_spmd

F32 = mybir.dt.float32
BF16 = mybir.dt.bfloat16
AF = mybir.ActivationFunctionType
ALU = mybir.AluOpType
AX = mybir.AxisListType

ENGINES = ("sync", "tensor", "vector", "scalar", "gpsimd")
BF = ml_dtypes.bfloat16



class Sem:
    __slots__ = ("h", "n")

    def __init__(self, h):
        self.h = h
        self.n = 0

    def bump(self, k=1):
        self.n += k
        return self.n


class _Rec:
    __slots__ = ("eng", "meth", "args", "kw", "deps", "need_inc", "count",
                 "dma", "extra_inc")

    def __init__(self, eng, meth, args, kw):
        self.eng = eng
        self.meth = meth
        self.args = args
        self.kw = kw
        self.deps = []
        self.need_inc = False
        self.count = None
        self.dma = None
        self.extra_inc = None


class Prog:

    def __init__(self, nc, stack):
        self.nc = nc
        self.stack = stack
        self.scopes = [stack]
        self.q = {e: [] for e in ENGINES}
        self._si = 0
        self.esem = {e: self.sem(f"E_{e}") for e in ENGINES}
        self.dma_sems = {}
        self.sem_pool = {"sync": [], "gpsimd": []}
        self.scope_keys = [set()]
        self.key_queue = {}
        self.last_w = {}
        self.readers = {}

    def barrier(self):
        lasts = {e: (self.q[e][-1] if self.q[e] else None) for e in ENGINES}
        dmas = [(s, s.n) for s in self.dma_sems.values() if s.n > 0]
        for e in ENGINES:
            rec = _Rec(e, "__barrier__", (), {})
            for e2 in ENGINES:
                if e2 != e and lasts[e2] is not None:
                    self._dep_add_barrier(rec, lasts[e2])
            rec.deps.extend(dmas)
            self.q[e].append(rec)

    def _dep_add_barrier(self, rec, dep):
        if isinstance(dep, _Rec):
            if dep.meth == "__barrier__":
                return
            if dep.dma is None:
                dep.need_inc = True
        rec.deps.append(dep)

    class _Scope:
        def __init__(self, p):
            self.p = p
            self.st = ExitStack()

        def __enter__(self):
            self.p.scopes.append(self.st)
            self.p.scope_keys.append(set())
            return self

        def __exit__(self, *a):
            self.p.scopes.pop()
            self.p.barrier()
            for key in self.p.scope_keys.pop():
                self.p.sem_pool[self.p.key_queue[key]].append(
                    self.p.dma_sems.pop(key))
            self.st.close()
            return False

    def scope(self):
        return Prog._Scope(self)

    def sem(self, name=None):
        h = self.stack.enter_context(self.nc.semaphore(name or f"s{self._si}"))
        self._si += 1
        return Sem(h)

    def sb(self, name, shape, dtype):
        return self.scopes[-1].enter_context(
            self.nc.sbuf_tensor(name, shape, dtype))

    def ps(self, name, shape, dtype=F32):
        return self.scopes[-1].enter_context(
            self.nc.psum_tensor(name, shape, dtype))

    def dram(self, name, shape, dtype=F32, kind="Internal"):
        return self.nc.dram_tensor(name, shape, dtype, kind=kind).ap()

    def _dep_add(self, rec, dep):
        if dep is None or dep is rec:
            return
        if isinstance(dep, _Rec):
            if dep.eng == "tensor" and rec.eng == "tensor":
                return
            dep.need_inc = True
        rec.deps.append(dep)

    def op(self, eng, meth, *args, outs=(), ins=(), nodep=False, inc=None,
           k=1, okey=None, ikey=None, **kw):
        rec = _Rec(eng, meth, args, kw)
        if inc is not None:
            inc.bump(k)
            rec.extra_inc = (inc, k)
        if okey is None:
            names_o = [a.name for a in outs]
        else:
            names_o = list(okey) if isinstance(okey, tuple) else [okey]
        if ikey is None:
            names_i = [a.name for a in ins]
        else:
            names_i = list(ikey) if isinstance(ikey, tuple) else [ikey]
        if meth in ("dma_start", "dma_start_transpose"):
            key = names_o[0]
            s = self.dma_sems.get(key)
            if s is None:
                pool = self.sem_pool[eng]
                s = pool.pop() if pool else self.sem(f"DM{self._si}")
                self.dma_sems[key] = s
                self.scope_keys[-1].add(key)
                self.key_queue[key] = eng
            assert self.key_queue[key] == eng, (key, eng)
            s.bump(16)
            rec.dma = (s, s.n)
        if not nodep:
            for nm in names_i:
                self._dep_add(rec, self.last_w.get(nm))
            for nm in names_o:
                for r in self.readers.get(nm, {}).values():
                    self._dep_add(rec, r)
                self._dep_add(rec, self.last_w.get(nm))
        for nm in names_o:
            self.last_w[nm] = rec
            self.readers[nm] = {}
        rkey = ("dma", id(rec.dma[0])) if rec.dma is not None else eng
        for nm in names_i:
            self.readers.setdefault(nm, {})[rkey] = rec
        self.q[eng].append(rec)
        return rec

    def dma(self, queue, out, in_, okey=None, ikey=None, **kw):
        return self.op(queue, "dma_start", outs=(out,), ins=(in_,),
                       out=out, in_=in_, okey=okey, ikey=ikey, **kw)

    def mm(self, out, lhsT, rhs, start=True, stop=True, **kw):
        return self.op("tensor", "matmul", out, lhsT, rhs,
                       outs=(out,), ins=(lhsT, rhs), start=start, stop=stop,
                       **kw)

    def trans(self, out, in_, ident, **kw):
        return self.op("tensor", "transpose", out, in_, ident,
                       outs=(out,), ins=(in_, ident), **kw)

    def act(self, out, in_, func, bias=0.0, scale=1.0, accum_out=None, **kw):
        ins = [in_]
        outs = [out]
        if isinstance(bias, bass.AP):
            ins.append(bias)
        if isinstance(scale, bass.AP):
            ins.append(scale)
        if accum_out is not None:
            outs.append(accum_out)
            kw["accum_out"] = accum_out
        return self.op("scalar", "activation", out, in_, func,
                       outs=tuple(outs), ins=tuple(ins), bias=bias,
                       scale=scale, **kw)

    def scopy(self, out, in_, **kw):
        return self.op("scalar", "copy", out, in_, outs=(out,), ins=(in_,),
                       **kw)

    def tt(self, out, a, b, alu, eng="vector", **kw):
        return self.op(eng, "tensor_tensor", out, a, b, outs=(out,),
                       ins=(a, b), op=alu, **kw)

    def ts(self, out, a, s1, alu, s2=None, alu2=None, eng="vector", **kw):
        ins = [a] + [x for x in (s1, s2) if isinstance(x, bass.AP)]
        if alu2 is None:
            return self.op(eng, "tensor_scalar", out, a, s1, s2, alu,
                           outs=(out,), ins=tuple(ins), **kw)
        return self.op(eng, "tensor_scalar", out, a, s1, s2, alu, alu2,
                       outs=(out,), ins=tuple(ins), **kw)

    def stt(self, out, in0, scalar, in1, op0, op1, eng="vector", **kw):
        ins = [in0, in1] + ([scalar] if isinstance(scalar, bass.AP) else [])
        return self.op(eng, "scalar_tensor_tensor", out, in0, scalar, in1,
                       outs=(out,), ins=tuple(ins), op0=op0, op1=op1, **kw)

    def vcopy(self, out, in_, eng="vector", **kw):
        return self.op(eng, "tensor_copy", out, in_, outs=(out,), ins=(in_,),
                       **kw)

    def red(self, meth, out, in_, **kw):
        return self.op("vector", meth, out, in_, outs=(out,), ins=(in_,),
                       axis=AX.X, **kw)

    def recip(self, out, in_, **kw):
        return self.op("vector", "reciprocal", out, in_, outs=(out,),
                       ins=(in_,), **kw)

    def end_waits(self):
        for s in self.dma_sems.values():
            if s.n:
                rec = _Rec("sync", "__wait__", (s.h, s.n), {})
                self.q["sync"].append(rec)

    def install(self):
        for e in ENGINES:
            n = 0
            for rec in self.q[e]:
                if rec.need_inc and rec.dma is None:
                    n += 1
                    rec.count = n
            self.esem[e].n = n
        nc = self.nc
        esem = self.esem
        waited = {e: {} for e in ENGINES}

        def _wait(E, eng, sh, v):
            w = waited[eng]
            if w.get(id(sh), -1) >= v:
                return
            E.wait_ge(sh, v)
            w[id(sh)] = v

        def emit(rec, E):
            eng = rec.eng
            if rec.meth == "__wait__":
                _wait(E, eng, rec.args[0], rec.args[1])
                return
            if rec.meth == "__barrier__":
                waits = {}
                for dep in rec.deps:
                    if isinstance(dep, _Rec):
                        if dep.dma is not None:
                            sh, v = dep.dma[0].h, dep.dma[1]
                        elif dep.count is not None:
                            sh, v = esem[dep.eng].h, dep.count
                        else:
                            continue
                    else:
                        sh, v = dep[0].h, dep[1]
                    if waits.get(id(sh), (None, -1))[1] < v:
                        waits[id(sh)] = (sh, v)
                for sh, v in waits.values():
                    _wait(E, eng, sh, v)
                return
            waits = {}
            for dep in rec.deps:
                if isinstance(dep, _Rec):
                    if dep.dma is not None:
                        sh, v = dep.dma[0].h, dep.dma[1]
                    else:
                        assert dep.count is not None, (rec.eng, rec.meth)
                        sh, v = esem[dep.eng].h, dep.count
                else:
                    sh, v = dep[0].h, dep[1]
                if waits.get(id(sh), (None, -1))[1] < v:
                    waits[id(sh)] = (sh, v)
            for sh, v in waits.values():
                _wait(E, eng, sh, v)
            try:
                r = getattr(E, rec.meth)(*rec.args, **rec.kw)
            except Exception as ex:
                raise RuntimeError(
                    f"emit failed: {rec.eng}.{rec.meth} args="
                    f"{[getattr(a, 'name', a) for a in rec.args]} "
                    f"kw={ {k: getattr(v, 'name', v) for k, v in rec.kw.items()} }"
                ) from ex
            if rec.dma is not None:
                r.then_inc(rec.dma[0].h, 16)
            elif rec.count is not None:
                r.then_inc(esem[rec.eng].h, 1)
            if rec.extra_inc is not None:
                r.then_inc(rec.extra_inc[0].h, rec.extra_inc[1])

        with nc.Block() as block:
            @block.sync
            def _(e):
                for rec in self.q["sync"]:
                    emit(rec, e)

            @block.tensor
            def _(e):
                for rec in self.q["tensor"]:
                    emit(rec, e)

            @block.vector
            def _(e):
                for rec in self.q["vector"]:
                    emit(rec, e)

            @block.scalar
            def _(e):
                for rec in self.q["scalar"]:
                    emit(rec, e)

            @block.gpsimd
            def _(e):
                for rec in self.q["gpsimd"]:
                    emit(rec, e)





class LNRes:

    def __init__(self, p: Prog, tag, consts_sb):
        self.x4p = [p.sb(f"{tag}_x4p{i}", [128, TQ], F32) for i in range(3)]
        self.sq = p.sb(f"{tag}_sq", [128, TQ], F32)
        self.mu = p.sb(f"{tag}_mu", [1, TQ], F32)
        self.var = p.sb(f"{tag}_var", [1, TQ], F32)
        self.tmp1 = p.sb(f"{tag}_tmp1", [1, TQ], F32)
        self.rstd = p.sb(f"{tag}_rstd", [1, TQ], F32)
        self.rstdb = p.sb(f"{tag}_rstdb", [128, TQ], F32)
        self.yt = [p.sb(f"{tag}_yt{i}", [128, TQ], F32) for i in range(2)]
        self.ps_sum = p.ps(f"{tag}_pssum", [1, TQ])
        self.ps_sq = p.ps(f"{tag}_pssq", [1, TQ])
        self.ps_bc = p.ps(f"{tag}_psbc", [128, TQ])
        self.scr = p.dram(f"{tag}_scr", [NDT, 128, TQ], F32)
        self.ones = consts_sb["ones"]
        self.eps = consts_sb["eps"]


def emit_postln(p: Prog, R: LNRes, produce, out_view, tq, qtag):
    for dt in range(NDT):
        slot = dt % 3
        x4 = R.x4p[slot]
        produce(dt, x4)
        p.dma("sync", R.scr[dt, :, :], x4[:, :], okey=f"{qtag}/s{dt % 3}")
        p.act(R.sq[:, :], x4[:, :], AF.Square)
        p.mm(R.ps_sum[0:1, :], R.ones[:, 0:1], x4[:, :],
             start=(dt == 0), stop=(dt == NDT - 1), skip_group_check=True)
        p.mm(R.ps_sq[0:1, :], R.ones[:, 0:1], R.sq[:, :],
             start=(dt == 0), stop=(dt == NDT - 1), skip_group_check=True)
    p.ts(R.mu[0:1, :], R.ps_sum[0:1, :], 1.0 / 2048.0, ALU.mult)
    p.ts(R.var[0:1, :], R.ps_sq[0:1, :], 1.0 / 2048.0, ALU.mult)
    p.tt(R.tmp1[0:1, :], R.mu[0:1, :], R.mu[0:1, :], ALU.mult)
    p.tt(R.var[0:1, :], R.var[0:1, :], R.tmp1[0:1, :], ALU.subtract)
    p.act(R.tmp1[0:1, :], R.var[0:1, :], AF.Ln, bias=R.eps[0:1, 0:1])
    p.act(R.rstd[0:1, :], R.tmp1[0:1, :], AF.Exp, scale=-0.5)
    p.mm(R.ps_bc[:, :], R.ones[0:1, 0:128], R.rstd[0:1, :])
    p.scopy(R.rstdb[:, :], R.ps_bc[:, :])
    p.mm(R.ps_bc[:, :], R.ones[0:1, 0:128], R.mu[0:1, :])
    for dt in range(NDT):
        slot = dt % 3
        ypar = dt % 2
        p.dma("sync", R.x4p[slot][:, :], R.scr[dt, :, :],
              ikey=f"{qtag}/s{dt % 3}")
        p.tt(R.yt[ypar][:, :], R.x4p[slot][:, :], R.ps_bc[:, :],
             ALU.subtract)
        p.tt(R.yt[ypar][:, :], R.yt[ypar][:, :], R.rstdb[:, :], ALU.mult)
        p.dma("sync", out_view[:, dt, tq], R.yt[ypar][:, :],
              okey=f"{qtag}/o{dt % 2}")






def emit_moe(p: Prog, C, cfg, mk_xb, resid, out_view, postln, tag):
    R = C["ln"]
    idt, ones = C["ident"], C["ones"]
    NEr, NHT, nsub = cfg["NEr"], cfg["NHT"], cfg["nsub"]
    wgl, wul, wdl = cfg["wgl"], cfg["wul"], cfg["wdl"]
    gate_w = cfg["gate_w"]
    rb_bc = cfg.get("rb_bc")
    sig = cfg["router"] == "sigmoid"
    scal = cfg.get("scaling", 1.0)
    NGT = (NEr + nsub) * NHT

    xb = p.sb(f"{tag}_xb", [128, NDC, TQ], BF16)
    gwt = p.sb(f"{tag}_gw", [128, NDC, 8], BF16)
    rbb = p.sb(f"{tag}_rbb", [128, 8], F32)
    r_m1 = p.sb(f"{tag}_m1", [128, 1], F32)
    r_neg = p.sb(f"{tag}_neg", [128, 1], F32)
    r_sum = p.sb(f"{tag}_sum", [128, 1], F32)
    r_rinv = p.sb(f"{tag}_rinv", [128, 1], F32)
    r_m2 = p.sb(f"{tag}_m2", [128, 1], F32)
    r_lg = p.sb(f"{tag}_lg", [128, 8], F32)
    r_probs = p.sb(f"{tag}_probs", [128, 8], F32)
    r_pn = p.sb(f"{tag}_pn", [128, 8], F32)
    r_eqm = p.sb(f"{tag}_eqm", [128, 8], F32)
    r_msk = p.sb(f"{tag}_msk", [128, 8], F32)
    r_ind = p.sb(f"{tag}_ind", [128, 8], F32)
    wts = p.sb(f"{tag}_wts", [128, 4, 8], F32)
    wrow8 = p.sb(f"{tag}_wrow8", [8, TQ], F32)
    wbcraw = p.sb(f"{tag}_wbcraw", [1, 8 * TQ], F32)
    wbc = p.sb(f"{tag}_wbc", [128, NEr, TQ], BF16)
    wgb = [p.sb(f"{tag}_wgb{i}", [128, NDC, 128], BF16) for i in range(2)]
    wub = [p.sb(f"{tag}_wub{i}", [128, NDC, 128], BF16) for i in range(2)]
    gs_t = p.sb(f"{tag}_gs", [128, TQ], BF16)
    t1_t = p.sb(f"{tag}_t1", [128, TQ], BF16)
    gtil = p.sb(f"{tag}_gtil", [128, NGT, TQ], BF16)
    wdb = [p.sb(f"{tag}_wdb{i}", [128, NHT, 128], BF16) for i in range(3)]

    ps_g = p.ps(f"{tag}_psg", [128, TQ])
    ps_u = p.ps(f"{tag}_psu", [128, TQ])
    ps_dn = p.ps(f"{tag}_psdn", [128, TQ])
    ps_lg = p.ps(f"{tag}_pslg", [128, 8])
    ps_tr = p.ps(f"{tag}_pstr", [8, 128])
    ps_bc = R.ps_bc

    scr_wts = p.dram(f"{tag}_scrwts", [8, TQ], F32)

    p.dma("gpsimd", gwt[:, :, 0:NEr],
          gate_w.rearrange("(dc p) e -> p dc e", p=128))
    p.op("vector", "memset", wts[:, :, :], 0.0, outs=(wts[:, :, :],))
    if rb_bc is not None:
        p.dma("sync", rbb[:, 0:NEr], rb_bc[:, :])

    units = [(e, 0) for e in range(NEr)] + [("s", k) for k in range(nsub)]

    for q in range(4):
        tq = slice(q * TQ, (q + 1) * TQ)
        mk_xb(q, xb)

        for tsb in range(4):
            tss = slice(tsb * 128, (tsb + 1) * 128)
            for dc in range(NDC):
                p.mm(ps_lg[:, 0:NEr], xb[:, dc, tss], gwt[:, dc, 0:NEr],
                     start=(dc == 0), stop=(dc == NDC - 1))
            if sig:
                p.tt(r_lg[:, 0:NEr], ps_lg[:, 0:NEr], rbb[:, 0:NEr], ALU.add)
                p.act(r_pn[:, 0:NEr], r_lg[:, 0:NEr], AF.Sigmoid)
            else:
                p.red("reduce_max", r_m1[:, 0:1], ps_lg[:, 0:NEr])
                p.ts(r_neg[:, 0:1], r_m1[:, 0:1], -1.0, ALU.mult)
                p.act(r_probs[:, 0:NEr], ps_lg[:, 0:NEr], AF.Exp,
                      bias=r_neg[:, 0:1], accum_out=r_sum[:, 0:1])
                p.recip(r_rinv[:, 0:1], r_sum[:, 0:1])
                p.ts(r_pn[:, 0:NEr], r_probs[:, 0:NEr], r_rinv[:, 0:1],
                     ALU.mult)
            p.red("reduce_max", r_m1[:, 0:1], r_pn[:, 0:NEr])
            p.ts(r_eqm[:, 0:NEr], r_pn[:, 0:NEr], r_m1[:, 0:1], ALU.is_equal)
            p.stt(r_msk[:, 0:NEr], r_eqm[:, 0:NEr], -1e30, r_pn[:, 0:NEr],
                  ALU.mult, ALU.add)
            p.red("reduce_max", r_m2[:, 0:1], r_msk[:, 0:NEr])
            p.ts(r_ind[:, 0:NEr], r_pn[:, 0:NEr], r_m2[:, 0:1], ALU.is_ge)
            if sig:
                p.tt(r_sum[:, 0:1], r_m1[:, 0:1], r_m2[:, 0:1], ALU.add)
                p.recip(r_rinv[:, 0:1], r_sum[:, 0:1])
                p.tt(r_msk[:, 0:NEr], r_pn[:, 0:NEr], r_ind[:, 0:NEr],
                     ALU.mult)
                p.ts(wts[:, tsb, 0:NEr], r_msk[:, 0:NEr], r_rinv[:, 0:1],
                     ALU.mult, s2=1.0 / scal, alu2=ALU.mult)
            else:
                p.tt(wts[:, tsb, 0:NEr], r_pn[:, 0:NEr], r_ind[:, 0:NEr],
                     ALU.mult)
            p.trans(ps_tr[:, :], wts[:, tsb, :], idt[:])
            p.vcopy(wrow8[0:8, tss], ps_tr[0:8, 0:128])
        p.dma("sync", scr_wts[:, :], wrow8[:, :])
        p.dma("sync", wbcraw[0:1, :],
              scr_wts.rearrange("e t -> (e t)").unsqueeze(0))
        for e in range(NEr):
            p.mm(ps_bc[:, :], ones[0:1, 0:128],
                 wbcraw[0:1, e * TQ:(e + 1) * TQ])
            p.scopy(wbc[:, e, :], ps_bc[:, :])

        wload = 0
        for ui, (e, sub) in enumerate(units):
            for ht in range(NHT):
                par = wload % 2
                wload += 1
                p.dma("gpsimd", wgb[par][:, :, :], wgl[ui, ht])
                p.dma("gpsimd", wub[par][:, :, :], wul[ui, ht])
                for dc in range(NDC):
                    p.mm(ps_g[:, :], wgb[par][:, dc, :], xb[:, dc, :],
                         start=(dc == 0), stop=(dc == NDC - 1))
                for dc in range(NDC):
                    p.mm(ps_u[:, :], wub[par][:, dc, :], xb[:, dc, :],
                         start=(dc == 0), stop=(dc == NDC - 1))
                p.act(gs_t[:, :], ps_g[:, :], AF.Silu)
                gidx = (NEr * NHT + sub * NHT + ht) if e == "s"                     else e * NHT + ht
                if e == "s":
                    if scal != 1.0:
                        p.stt(gtil[:, gidx, :], ps_u[:, :], 1.0 / scal,
                              gs_t[:, :], ALU.mult, ALU.mult)
                    else:
                        p.tt(gtil[:, gidx, :], ps_u[:, :], gs_t[:, :],
                             ALU.mult)
                else:
                    p.tt(t1_t[:, :], ps_u[:, :], gs_t[:, :], ALU.mult)
                    p.tt(gtil[:, gidx, :], t1_t[:, :], wbc[:, e, :],
                         ALU.mult)

        def produce(dt, x4):
            wdi = 0
            for ui, (e, sub) in enumerate(units):
                wpar = wdi % 3
                wdi += 1
                p.dma("gpsimd", wdb[wpar][:, :, :], wdl[ui, dt])
                gbase = (NEr * NHT + sub * NHT) if e == "s" else e * NHT
                for hc in range(NHT):
                    p.mm(ps_dn[:, :], wdb[wpar][:, hc, :],
                         gtil[:, gbase + hc, :],
                         start=(ui == 0 and hc == 0),
                         stop=(ui == len(units) - 1 and hc == NHT - 1))
            resid(q, dt, x4, ps_dn)

        if postln:
            emit_postln(p, R, produce, out_view, tq, qtag=tag)
        else:
            for dt in range(NDT):
                produce(dt, R.x4p[dt % 3])
                p.dma("sync", out_view[:, dt, tq], R.x4p[dt % 3][:, :],
                      okey=f"{tag}o{dt % 3}")



TQ = 512
NDC = 16
NDT = 16


def emit_consts(p: Prog, io):
    idt = p.sb("c_id", [128, 128], F32)
    idb = p.sb("c_idb", [128, 128], BF16)
    ones = p.sb("c_ones", [128, 128], F32)
    eps = p.sb("c_eps", [1, 1], F32)
    p.dma("sync", idt[:], io["identity"][:, :])
    p.dma("gpsimd", idb[:], io["identity"][:, :])
    p.dma("sync", ones[:], io["ones"][:, :])
    p.dma("sync", eps[:], io["epsln"][:, :])
    return {"ident": idt, "identb": idb, "ones": ones, "eps": eps}


def emit_phase_d(p: Prog, io: dict, C, nq=4):
    x3T, out = io["x3T"], io["y"]
    x3T_v = x3T.rearrange("(dc p) t -> p dc t", p=128)
    out_v = out.rearrange("(dc p) t -> p dc t", p=128)
    x3f = [p.sb(f"d_x3f{i}", [128, TQ], F32) for i in range(2)]

    cfg = {
        "NEr": 8, "NHT": 11, "nsub": 2, "router": "softmax", "scaling": 1.0,
        "gate_w": io["l1_gate_w"], "wgl": io["l1_wgl"], "wul": io["l1_wul"],
        "wdl": io["l1_wdl"],
    }

    def mk_xb(q, xb):
        p.dma("gpsimd", xb[:, :, :],
              x3T_v[:, :, slice(q * TQ, (q + 1) * TQ)])

    def resid(q, dt, x4, ps):
        par = dt % 2
        p.dma("sync", x3f[par][:, :],
              x3T_v[:, dt, slice(q * TQ, (q + 1) * TQ)])
        p.tt(x4[:, :], ps[:, :], x3f[par][:, :], ALU.add)

    emit_moe(p, C, cfg, mk_xb, resid, out_v, True, "d")
    p.end_waits()



def _tile_gu(w2d):
    nh = w2d.shape[1] // 128
    return np.ascontiguousarray(
        w2d.reshape(16, 128, nh, 128).transpose(2, 1, 0, 3))


def _tile_d(w2d):
    nh = w2d.shape[0] // 128
    return np.ascontiguousarray(
        w2d.reshape(nh, 128, 16, 128).transpose(2, 1, 0, 3))


def _moe_pretile(wg, wu, wd, swg, swu, swd, NHT):
    NEr = wg.shape[0]
    H = wg.shape[2]
    nsub = swg.shape[1] // (NHT * 128)
    U = NEr + nsub

    tile_gu, tile_d = _tile_gu, _tile_d
    wgl = np.zeros((U, NHT, 128, 16, 128), dtype=np.float32)
    wul = np.zeros((U, NHT, 128, 16, 128), dtype=np.float32)
    wdl = np.zeros((U, 16, 128, NHT, 128), dtype=np.float32)
    for e in range(NEr):
        wgl[e] = tile_gu(wg[e])
        wul[e] = tile_gu(wu[e])
        wdl[e] = tile_d(wd[e])
    tg, tu, td = tile_gu(swg), tile_gu(swu), tile_d(swd)
    for k in range(nsub):
        wgl[NEr + k] = tg[k * NHT:(k + 1) * NHT]
        wul[NEr + k] = tu[k * NHT:(k + 1) * NHT]
        wdl[NEr + k] = td[:, :, k * NHT:(k + 1) * NHT, :]
    return wgl, wul, wdl


def _consts():
    return {
        "identity": np.eye(128, dtype=np.float32),
        "ones": np.ones((128, 128), dtype=np.float32),
        "epsln": np.full((1, 1), 1e-5, dtype=np.float32),
    }


def build_nc_phase_d(nq=4):
    nc = bass.Bass("TRN2", target_bir_lowering=False, debug=False)
    io = {}
    io["x3T"] = nc.dram_tensor("x3T", [2048, 2048], F32, kind="ExternalInput").ap()
    io["l1_gate_w"] = nc.dram_tensor("l1_gate_w", [2048, 8], F32, kind="ExternalInput").ap()
    io["l1_wgl"] = nc.dram_tensor("l1_wgl", [10, 11, 128, 16, 128], F32, kind="ExternalInput").ap()
    io["l1_wul"] = nc.dram_tensor("l1_wul", [10, 11, 128, 16, 128], F32, kind="ExternalInput").ap()
    io["l1_wdl"] = nc.dram_tensor("l1_wdl", [10, 16, 128, 11, 128], F32, kind="ExternalInput").ap()
    io["identity"] = nc.dram_tensor("identity", [128, 128], F32, kind="ExternalInput").ap()
    io["ones"] = nc.dram_tensor("ones", [128, 128], F32, kind="ExternalInput").ap()
    io["epsln"] = nc.dram_tensor("epsln", [1, 1], F32, kind="ExternalInput").ap()
    io["y"] = nc.dram_tensor("y", [2048, 2048], F32, kind="ExternalOutput").ap()
    with ExitStack() as stack:
        p = Prog(nc, stack)
        C = emit_consts(p, io)
        C["ln"] = LNRes(p, "ln", C)
        emit_phase_d(p, io, C, nq=nq)
        p.install()
    return nc


def _l1_common(weights):
    wgl, wul, wdl = _moe_pretile(
        weights["L1_ffn_w_gate"], weights["L1_ffn_w_up"],
        weights["L1_ffn_w_down"], weights["L1_ffn_sw_gate"],
        weights["L1_ffn_sw_up"], weights["L1_ffn_sw_down"], 11)
    return {
        "l1_gate_w": weights["L1_ffn_gate_w"],
        "l1_wgl": wgl, "l1_wul": wul, "l1_wdl": wdl,
        **_consts(),
    }


def run_phase_d(x3, weights, core_ids=None):
    if core_ids is None:
        core_ids = list(range(8))
    nc = build_nc_phase_d()
    common = _l1_common(weights)
    in_maps = []
    for c in core_ids:
        b, i = c // 2, c % 2
        x3T = np.ascontiguousarray(x3[b, i * 2048:(i + 1) * 2048, :].T)
        in_maps.append({"x3T": x3T, **common})
    res = run_bass_kernel_spmd(nc, in_maps, core_ids=core_ids)
    y = np.zeros((4, 4096, 2048), dtype=np.float32)
    for j, c in enumerate(core_ids):
        b, i = c // 2, c % 2
        y[b, i * 2048:(i + 1) * 2048, :] = res.results[j]["y"].T
    return y



NKV = 4
NKB = 20


def emit_phase_c(p: Prog, io: dict, C):
    x2m, x2s, out = io["x2m"], io["x2s"], io["x3"]
    wkT, wvT = io["wkT"], io["wvT"]
    wmask, smask = io["wmask"], io["smask"]
    R = C["ln"]
    ones = C["ones"]

    x2m_v = x2m.rearrange("(dc p) t -> p dc t", p=128)
    x2s_v = x2s.rearrange("(dc p) t -> p dc t", p=128)
    out_v = out.rearrange("(dc p) t -> p dc t", p=128)
    wkT_v = wkT.rearrange("(dc p) c -> p dc c", p=128)
    wvT_v = wvT.rearrange("(dc p) c -> p dc c", p=128)
    wqT_v = io["wql"]
    woT_v = io["wol"]

    onesb_t = p.sb("c_onesb", [128, 128], BF16)
    p.dma("gpsimd", onesb_t[:], io["ones"][:, :])

    xb = [p.sb(f"c_xb{i}", [128, NDC, 512], BF16) for i in range(1)]
    Kt = p.sb("c_K", [128, NKV, 2560], BF16)
    Ksk = p.sb("c_Ksk", [128, NKV, 128], BF16)
    Vt = p.sb("c_V", [128, NKB * NKV, 128], BF16)
    Vsk = p.sb("c_Vsk", [128, NKV, 128], BF16)
    wkt = p.sb("c_wkt", [128, NDC, 512], BF16)
    wvt = p.sb("c_wvt", [128, NDC, 512], BF16)
    wqb = [p.sb(f"c_wqb{i}", [128, NDC, 128], BF16) for i in range(2)]
    qev = p.sb("c_qev", [128, 512], BF16)
    Qh = [p.sb(f"c_Qh{i}", [128, 2048], BF16) for i in range(2)]
    sq_t = p.sb("c_sqt", [128, 512], F32)
    ktmp6 = p.sb("c_ktmp6", [1, 8], F32)
    kmaxf = p.sb("c_kmaxf", [1, 1], F32)
    cneg = p.sb("c_cneg", [1, NKV], F32)
    qnt = p.sb("c_qnt", [1, 512], F32)
    brow = p.sb("c_brow", [1, 2048], F32)
    bqb = p.sb("c_bqb", [128, 128], F32)
    mask_w = p.sb("c_mw", [128, 3, 3, 128], BF16)
    mask_s = p.sb("c_ms", [128, 4, 128], BF16)
    s_f32 = p.sb("c_sf", [128, 128], F32)
    pt_bf = p.sb("c_pt", [128, 128], BF16)
    ptm = p.sb("c_ptm", [128, 128], BF16)
    rinv = p.sb("c_rinv", [1, 128], F32)
    rinvb = p.sb("c_rinvb", [128, 128], F32)
    aout = p.sb("c_aout", [128, 128], BF16)
    att_q = [p.sb(f"c_attq{i}", [128, 512], BF16) for i in range(16)]
    wob = [p.sb(f"c_wob{i}", [128, NDC, 128], BF16) for i in range(2)]
    x2f = [p.sb(f"c_x2f{i}", [128, TQ], F32) for i in range(2)]

    ps_p = p.ps("c_psp", [128, 512])
    ps_s = p.ps("c_pss", [128, 128])
    ps_av = p.ps("c_psav", [128, 128])
    ps_rs = p.ps("c_psrs", [1, 512])
    ps_ow = p.ps("c_psow", [128, TQ])

    scr_attn = p.dram("c_scrattn", [16, 128, 2048], BF16)
    scr_q = p.dram("c_scrq", [16, 128, 2048], BF16)

    p.dma("gpsimd", mask_w[:, :, :, :],
          wmask.rearrange("a r k q -> k a r q"))
    p.dma("gpsimd", mask_s[:, :, :], smask.rearrange("a k q -> k a q"))
    p.dma("gpsimd", wkt[:, :, :], wkT_v[:, :, :])
    p.dma("gpsimd", wvt[:, :, :], wvT_v[:, :, :])

    for tt in range(6):
        par = 0
        sink = tt == 5
        if sink:
            p.dma("gpsimd", xb[par][:, :, 0:128], x2s_v[:, :, :])
            tw = 128
        else:
            p.dma("gpsimd", xb[par][:, :, :],
                  x2m_v[:, :, tt * 512:(tt + 1) * 512])
            tw = 512
        for kv in range(NKV):
            for dc in range(NDC):
                p.mm(ps_p[:, 0:tw],
                     wkt[:, dc, kv * 128:(kv + 1) * 128],
                     xb[par][:, dc, 0:tw],
                     start=(dc == 0), stop=(dc == NDC - 1))
            if sink:
                p.scopy(Ksk[:, kv, :], ps_p[:, 0:128])
            else:
                p.scopy(Kt[:, kv, tt * 512:(tt + 1) * 512], ps_p[:, 0:tw])
        nsub = 1 if sink else 4
        for sub in range(nsub):
            for dc in range(NDC):
                p.mm(ps_p[:, 0:512],
                     xb[par][:, dc, sub * 128:(sub + 1) * 128],
                     wvt[:, dc, :],
                     start=(dc == 0), stop=(dc == NDC - 1))
            for kv in range(NKV):
                if sink:
                    p.scopy(Vsk[:, kv, :], ps_p[:, kv * 128:(kv + 1) * 128])
                else:
                    kb = tt * 4 + sub
                    p.scopy(Vt[:, kb * NKV + kv, :],
                            ps_p[:, kv * 128:(kv + 1) * 128])

    for qt in range(4):
        p.dma("gpsimd", xb[0][:, :, :],
              x2m_v[:, :, 256 + qt * 512:256 + (qt + 1) * 512])
        for h in range(16):
            wpar = h % 2
            p.dma("gpsimd", wqb[wpar][:, :, :], wqT_v[h])
            for dc in range(NDC):
                p.mm(ps_p[:, 0:512], wqb[wpar][:, dc, :],
                     xb[0][:, dc, :],
                     start=(dc == 0), stop=(dc == NDC - 1))
            p.act(qev[:, :], ps_p[:, 0:512], AF.Copy,
                  scale=float(1.0 / np.sqrt(128.0)))
            p.dma("sync", scr_q[h, :, qt * 512:(qt + 1) * 512], qev[:, :],
                  okey=(f"scq/{h % 4}", "scqall"))

    for kv in range(NKV):
        for tt in range(5):
            p.act(sq_t[:, :], Kt[:, kv, tt * 512:(tt + 1) * 512], AF.Square)
            p.mm(ps_rs[0:1, :], ones[:, 0:1], sq_t[:, :])
            p.red("reduce_max", ktmp6[0:1, tt:tt + 1], ps_rs[0:1, :])
        p.act(sq_t[:, 0:128], Ksk[:, kv, :], AF.Square)
        p.mm(ps_rs[0:1, 0:128], ones[:, 0:1], sq_t[:, 0:128])
        p.red("reduce_max", ktmp6[0:1, 5:6], ps_rs[0:1, 0:128])
        p.red("reduce_max", kmaxf[0:1, 0:1], ktmp6[0:1, 0:6])
        p.act(kmaxf[0:1, 0:1], kmaxf[0:1, 0:1], AF.Ln)
        p.act(kmaxf[0:1, 0:1], kmaxf[0:1, 0:1], AF.Exp, scale=0.5)
        p.ts(cneg[0:1, kv:kv + 1], kmaxf[0:1, 0:1], -1.0001, ALU.mult)

    for h in range(16):
        kv = h // 4
        qpar = h % 2
        p.dma("sync", Qh[qpar][:, :], scr_q[h, :, :], ikey="scqall")
        for qt in range(4):
            p.act(sq_t[:, :], Qh[qpar][:, qt * 512:(qt + 1) * 512], AF.Square)
            p.mm(ps_rs[0:1, :], ones[:, 0:1], sq_t[:, :])
            p.act(qnt[0:1, :], ps_rs[0:1, :], AF.Ln)
            p.act(qnt[0:1, :], qnt[0:1, :], AF.Exp, scale=0.5)
            p.ts(brow[0:1, qt * 512:(qt + 1) * 512], qnt[0:1, :],
                 cneg[0:1, kv:kv + 1], ALU.mult)
        for j in range(16):
            qc = slice(j * 128, (j + 1) * 128)
            jj = min(j, 2)
            jjs = min(j, 3)
            p.mm(ps_s[:, :], ones[0:1, 0:128], brow[0:1, qc])
            p.scopy(bqb[:, :], ps_s[:, :])
            for vis in range(4):
                if vis < 3:
                    kb = j + vis
                    lk = Kt[:, kv, kb * 128:(kb + 1) * 128]
                    lv = Vt[:, kb * NKV + kv, :]
                    msk = mask_w[:, jj, vis, :]
                else:
                    lk = Ksk[:, kv, :]
                    lv = Vsk[:, kv, :]
                    msk = mask_s[:, jjs, :]
                p.mm(ps_s[:, :], lk, Qh[qpar][:, qc])
                p.tt(s_f32[:, :], ps_s[:, :], bqb[:, :], ALU.subtract)
                p.act(pt_bf[:, :], s_f32[:, :], AF.Exp)
                p.tt(ptm[:, :], pt_bf[:, :], msk, ALU.mult)
                p.mm(ps_av[:, :], lv, ptm[:, :],
                     start=(vis == 0), stop=(vis == 3),
                     skip_group_check=True)
                p.mm(ps_rs[0:1, 0:128], onesb_t[:, 0:1], ptm[:, :],
                     start=(vis == 0), stop=(vis == 3),
                     skip_group_check=True)
            p.recip(rinv[0:1, :], ps_rs[0:1, 0:128])
            p.mm(ps_s[:, :], ones[0:1, 0:128], rinv[0:1, :])
            p.scopy(rinvb[:, :], ps_s[:, :])
            p.tt(aout[:, :], ps_av[:, :], rinvb[:, :], ALU.mult)
            p.dma("sync", scr_attn[h, :, qc], aout[:, :],
                  okey=(f"scat/{j % 4}", "scatall"))

    for q4 in range(4):
        tq = slice(q4 * TQ, (q4 + 1) * TQ)
        for hc in range(16):
            p.dma("sync", att_q[hc][:, :], scr_attn[hc, :, tq],
                  ikey="scatall", okey=(f"cattq{hc % 4}", f"c_attq{hc}"))

        def produce(dt, x4):
            par = dt % 2
            p.dma("gpsimd", wob[par][:, :, :], woT_v[dt])
            for hc in range(NDC):
                p.mm(ps_ow[:, :], wob[par][:, hc, :], att_q[hc][:, :],
                     start=(hc == 0), stop=(hc == NDC - 1))
            p.dma("sync", x2f[par][:, :],
                  x2m_v[:, dt, 256 + q4 * TQ:256 + (q4 + 1) * TQ])
            p.tt(x4[:, :], ps_ow[:, :], x2f[par][:, :], ALU.add)

        emit_postln(p, R, produce, out_v, tq, qtag="c")

    p.end_waits()


def make_masks_c(core_i):
    wm = np.zeros((3, 3, 128, 128), dtype=np.float32)
    kk = np.arange(128)
    qq = np.arange(128)
    for jj in range(3):
        for r in range(3):
            d = 256 - 128 * r + qq[None, :] - kk[:, None]
            m = (d >= 0) & (d <= 255)
            if core_i == 0:
                kloc = 128 * (jj + r) + kk[:, None]
                m = m & ((kloc >= 256) | (jj + r >= 2))
            wm[jj, r] = m
    sm = np.zeros((4, 128, 128), dtype=np.float32)
    for jjs in range(4):
        gq = 2048 * core_i + 128 * jjs + qq[None, :]
        gk = kk[:, None]
        m = (gk < 4) & (gq - gk > 255) & (gk <= gq)
        if jjs == 3 and not (core_i == 0):
            m = (gk < 4)
        sm[jjs] = m
    if core_i == 1:
        for jj in range(3):
            for r in range(3):
                d = 256 - 128 * r + qq[None, :] - kk[:, None]
                wm[jj, r] = (d >= 0) & (d <= 255)
        for jjs in range(4):
            sm[jjs] = (kk[:, None] < 4)
    return wm.astype(BF), sm.astype(BF)


def build_nc_phase_c():
    nc = bass.Bass("TRN2", target_bir_lowering=False, debug=False)
    io = {}
    io["x2m"] = nc.dram_tensor("x2m", [2048, 2560], F32, kind="ExternalInput").ap()
    io["x2s"] = nc.dram_tensor("x2s", [2048, 128], F32, kind="ExternalInput").ap()
    io["wql"] = nc.dram_tensor("wql", [16, 128, 16, 128], F32, kind="ExternalInput").ap()
    io["wkT"] = nc.dram_tensor("wkT", [2048, 512], F32, kind="ExternalInput").ap()
    io["wvT"] = nc.dram_tensor("wvT", [2048, 512], F32, kind="ExternalInput").ap()
    io["wol"] = nc.dram_tensor("wol", [16, 128, 16, 128], F32, kind="ExternalInput").ap()
    io["wmask"] = nc.dram_tensor("wmask", [3, 3, 128, 128], BF16, kind="ExternalInput").ap()
    io["smask"] = nc.dram_tensor("smask", [4, 128, 128], BF16, kind="ExternalInput").ap()
    io["identity"] = nc.dram_tensor("identity", [128, 128], F32, kind="ExternalInput").ap()
    io["ones"] = nc.dram_tensor("ones", [128, 128], F32, kind="ExternalInput").ap()
    io["epsln"] = nc.dram_tensor("epsln", [1, 1], F32, kind="ExternalInput").ap()
    io["x3"] = nc.dram_tensor("x3", [2048, 2048], F32, kind="ExternalOutput").ap()
    with ExitStack() as stack:
        p = Prog(nc, stack)
        C = emit_consts(p, io)
        C["ln"] = LNRes(p, "ln", C)
        emit_phase_c(p, io, C)
        p.install()
    return nc


def _phase_c_core_inputs(x2, weights, b, i):
    lo = 2048 * i - 256
    x2m = np.zeros((2560, 2048), dtype=np.float32)
    src_lo = max(lo, 0)
    src_hi = min(2048 * i + 2304, 4096)
    x2m[src_lo - lo:src_hi - lo, :] = x2[b, src_lo:src_hi, :]
    wm, sm = make_masks_c(i)
    return {
        "x2m": np.ascontiguousarray(x2m.T),
        "x2s": np.ascontiguousarray(x2[b, 0:128, :].T),
        "wql": _tile_gu(np.ascontiguousarray(weights["L1_attn_wq"].T)),
        "wkT": np.ascontiguousarray(weights["L1_attn_wk"].T),
        "wvT": np.ascontiguousarray(weights["L1_attn_wv"].T),
        "wol": _tile_d(np.ascontiguousarray(weights["L1_attn_wo"].T)),
        "wmask": wm,
        "smask": sm,
        **_consts(),
    }


def run_phase_c(x2, weights, core_ids=None):
    if core_ids is None:
        core_ids = list(range(8))
    nc = build_nc_phase_c()
    in_maps = []
    for c in core_ids:
        b, i = c // 2, c % 2
        in_maps.append(_phase_c_core_inputs(x2, weights, b, i))
    res = run_bass_kernel_spmd(nc, in_maps, core_ids=core_ids)
    x3 = np.zeros((4, 4096, 2048), dtype=np.float32)
    for j, c in enumerate(core_ids):
        b, i = c // 2, c % 2
        x3[b, i * 2048:(i + 1) * 2048, :] = res.results[j]["x3"].T
    return x3



RD2 = 64


def emit_phase_b(p: Prog, io: dict, C):
    x1T, out = io["x1T"], io["x2"]
    R = C["ln"]
    ones = C["ones"]
    x1T_v = x1T.rearrange("(dc p) t -> p dc t", p=128)
    x1own_v = io["x1ownT"].rearrange("(dc p) t -> p dc t", p=128)
    out_v = out.rearrange("(dc p) t -> p dc t", p=128)

    scr_k = p.dram("b_scrk", [16, 128, 4096], BF16)
    scr_v = p.dram("b_scrv", [4096, 2048], BF16)
    scr_q = p.dram("b_scrq", [16, 128, 2048], BF16)
    scr_at = p.dram("b_scrat", [16, 128, 2048], BF16)
    scr_h = p.dram("b_scrh", [2048, 2048], F32)
    scr_brow = p.dram("b_scrbrow", [1, 2048], F32)

    QSC = float(1.0 / np.sqrt(128.0))

    with p.scope():
        kvdw_t = p.sb("b_kvdw", [128, NDC, 512], BF16)
        qdw_t = p.sb("b_qdw", [128, NDC, 512], BF16)
        kuw_t = p.sb("b_kuw", [128, 4, 1024], BF16)
        vuw_t = p.sb("b_vuw", [128, 4, 2048], BF16)
        quw_t = p.sb("b_quw", [128, 4, 1024], BF16)
        rqw_t = p.sb("b_rqw", [128, 4, 1024], BF16)
        rkw_b = [p.sb(f"b_rkw{i}", [128, NDC, 128], BF16) for i in range(2)]
        cs_c = p.sb("b_csc", [128, 4096], BF16)
        cs_s = p.sb("b_css", [128, 4096], BF16)
        pswT = p.sb("b_psw", [128, 128], BF16)
        ln1c = p.sb("b_ln1", [128, NDC], F32)
        xf = [p.sb(f"b_xf{i}", [128, 512], F32) for i in range(2)]
        sqx = p.sb("b_sqx", [128, 512], F32)
        msr = p.sb("b_msr", [1, 512], F32)
        rstdb = p.sb("b_rstdb", [128, 512], F32)
        xr = p.sb("b_xr", [128, 512], F32)
        xb1 = p.sb("b_xb1", [128, NDC, 512], BF16)
        kvd = p.sb("b_kvd", [128, 4, 512], BF16)
        qd = p.sb("b_qd", [128, 4, 2048], BF16)
        krb = p.sb("b_krb", [128, 512], BF16)
        tc_t = p.sb("b_tc", [128, 512], F32)
        ts_t = p.sb("b_ts", [128, 512], F32)
        kro = p.sb("b_kro", [128, 512], BF16)
        kpb = p.sb("b_kpb", [128, 512], BF16)
        vev = p.sb("b_vev", [128, 512], BF16)

        ps_p = p.ps("b_psp", [128, 512])
        ps_n = p.ps("b_psn", [1, 512])
        ps_b = p.ps("b_psb", [128, 512])
        ps_2 = p.ps("b_ps2", [128, 512])

        p.dma("gpsimd", kvdw_t[:, :, :],
              io["kvdw"].rearrange("(dc p) c -> p dc c", p=128))
        p.dma("gpsimd", qdw_t[:, :, :],
              io["qdw"].rearrange("(dc p) c -> p dc c", p=128))
        p.dma("gpsimd", kuw_t[:, :, :],
              io["kuw"].rearrange("(lt p) c -> p lt c", p=128))
        p.dma("gpsimd", vuw_t[:, :, :],
              io["vuw"].rearrange("(lt p) c -> p lt c", p=128))
        p.dma("gpsimd", quw_t[:, :, :],
              io["quw"].rearrange("(lt p) c -> p lt c", p=128))
        p.dma("gpsimd", rqw_t[:, :, :],
              io["ropeqw"].rearrange("(lt p) c -> p lt c", p=128))
        p.dma("gpsimd", cs_c[:, :], io["cs_cos"][:, :])
        p.dma("gpsimd", cs_s[:, :], io["cs_sin"][:, :])
        p.dma("gpsimd", pswT[:, :], io["pswapT"][:, :])
        p.dma("sync", ln1c[:, :], io["ln1w"][:, :])

        csc_own = p.sb("b_cscown", [128, 2048], BF16)
        css_own = p.sb("b_cssown", [128, 2048], BF16)
        p.dma("gpsimd", csc_own[:, :], io["cs_cos_own"][:, :])
        p.dma("gpsimd", css_own[:, :], io["cs_sin_own"][:, :])

        ropekw_v = io["ropekl"]

        for tt in range(8):
            tcol = slice(tt * 512, (tt + 1) * 512)
            for dc in range(NDC):
                par = dc % 2
                p.dma("sync", xf[par][:, :], x1T_v[:, dc, tcol])
                p.act(sqx[:, :], xf[par][:, :], AF.Square)
                p.mm(ps_n[0:1, :], ones[:, 0:1], sqx[:, :],
                     start=(dc == 0), stop=(dc == NDC - 1))
            p.ts(msr[0:1, :], ps_n[0:1, :], 1.0 / 2048.0, ALU.mult)
            p.act(msr[0:1, :], msr[0:1, :], AF.Ln, bias=C["eps"][0:1, 0:1])
            p.act(msr[0:1, :], msr[0:1, :], AF.Exp, scale=-0.5)
            p.mm(ps_b[:, :], ones[0:1, 0:128], msr[0:1, :])
            p.scopy(rstdb[:, :], ps_b[:, :])
            for dc in range(NDC):
                par = dc % 2
                p.dma("sync", xf[par][:, :], x1T_v[:, dc, tcol])
                p.tt(xr[:, :], xf[par][:, :], rstdb[:, :], ALU.mult)
                p.ts(xb1[:, dc, :], xr[:, :], ln1c[:, dc:dc + 1], ALU.mult)
            for lt in range(4):
                for dc in range(NDC):
                    p.mm(ps_p[:, :], kvdw_t[:, dc, lt * 128:(lt + 1) * 128],
                         xb1[:, dc, :], start=(dc == 0), stop=(dc == NDC - 1))
                p.scopy(kvd[:, lt, :], ps_p[:, :])
            for cht in range(8):
                par = cht % 2
                p.dma("gpsimd", rkw_b[par][:, :, :], ropekw_v[cht])
                for dc in range(NDC):
                    p.mm(ps_p[:, :], rkw_b[par][:, dc, :], xb1[:, dc, :],
                         start=(dc == 0), stop=(dc == NDC - 1))
                p.scopy(krb[:, :], ps_p[:, :])
                p.mm(ps_2[:, :], pswT[:, :], krb[:, :])
                p.tt(tc_t[:, :], krb[:, :], cs_c[:, tcol], ALU.mult)
                p.tt(ts_t[:, :], ps_2[:, :], cs_s[:, tcol], ALU.mult)
                p.tt(kro[:, :], tc_t[:, :], ts_t[:, :], ALU.add)
                p.dma("sync", scr_k[2 * cht, 64:128, tcol], kro[0:64, :],
                      okey=f"bsk{cht % 2}a")
                p.dma("sync", scr_k[2 * cht + 1, 64:128, tcol], kro[64:128, :],
                      okey=f"bsk{cht % 2}b")
            for cht in range(8):
                for lt in range(4):
                    p.mm(ps_p[:, :], kuw_t[:, lt, cht * 128:(cht + 1) * 128],
                         kvd[:, lt, :], start=(lt == 0), stop=(lt == 3))
                p.scopy(kpb[:, :], ps_p[:, :])
                p.dma("sync", scr_k[2 * cht, 0:64, tcol], kpb[0:64, :],
                      okey=f"bsk{cht % 2}c")
                p.dma("sync", scr_k[2 * cht + 1, 0:64, tcol], kpb[64:128, :],
                      okey=f"bsk{cht % 2}d")
            for sub in range(4):
                tb = 4 * tt + sub
                for cg in range(4):
                    for lt in range(4):
                        p.mm(ps_p[:, :],
                             kvd[:, lt, sub * 128:(sub + 1) * 128],
                             vuw_t[:, lt, cg * 512:(cg + 1) * 512],
                             start=(lt == 0), stop=(lt == 3))
                    p.scopy(vev[:, :], ps_p[:, :])
                    p.dma("sync", scr_v[tb * 128:(tb + 1) * 128,
                                        cg * 512:(cg + 1) * 512],
                          vev[:, :], okey=f"bsv{(sub * 4 + cg) % 4}")
        for qt in range(4):
            qcol = slice(qt * 512, (qt + 1) * 512)
            for dc in range(NDC):
                par = dc % 2
                p.dma("sync", xf[par][:, :], x1own_v[:, dc, qcol])
                p.act(sqx[:, :], xf[par][:, :], AF.Square)
                p.mm(ps_n[0:1, :], ones[:, 0:1], sqx[:, :],
                     start=(dc == 0), stop=(dc == NDC - 1))
            p.ts(msr[0:1, :], ps_n[0:1, :], 1.0 / 2048.0, ALU.mult)
            p.act(msr[0:1, :], msr[0:1, :], AF.Ln, bias=C["eps"][0:1, 0:1])
            p.act(msr[0:1, :], msr[0:1, :], AF.Exp, scale=-0.5)
            p.mm(ps_b[:, :], ones[0:1, 0:128], msr[0:1, :])
            p.scopy(rstdb[:, :], ps_b[:, :])
            for dc in range(NDC):
                par = dc % 2
                p.dma("sync", xf[par][:, :], x1own_v[:, dc, qcol])
                p.tt(xr[:, :], xf[par][:, :], rstdb[:, :], ALU.mult)
                p.ts(xb1[:, dc, :], xr[:, :], ln1c[:, dc:dc + 1], ALU.mult)
            for lt in range(4):
                for dc in range(NDC):
                    p.mm(ps_p[:, :], qdw_t[:, dc, lt * 128:(lt + 1) * 128],
                         xb1[:, dc, :], start=(dc == 0), stop=(dc == NDC - 1))
                p.scopy(qd[:, lt, qcol], ps_p[:, :])

        for cht in range(8):
            for qt in range(4):
                qcol = slice(qt * 512, (qt + 1) * 512)
                for lt in range(4):
                    p.mm(ps_p[:, :], quw_t[:, lt, cht * 128:(cht + 1) * 128],
                         qd[:, lt, qcol], start=(lt == 0), stop=(lt == 3))
                p.act(kpb[:, :], ps_p[:, :], AF.Copy, scale=QSC)
                p.dma("sync", scr_q[2 * cht, 0:64, qcol], kpb[0:64, :],
                      okey="bsq0")
                p.dma("sync", scr_q[2 * cht + 1, 0:64, qcol], kpb[64:128, :],
                      okey="bsq1")
                for lt in range(4):
                    p.mm(ps_p[:, :], rqw_t[:, lt, cht * 128:(cht + 1) * 128],
                         qd[:, lt, qcol], start=(lt == 0), stop=(lt == 3))
                p.scopy(krb[:, :], ps_p[:, :])
                p.mm(ps_2[:, :], pswT[:, :], krb[:, :])
                qcv = csc_own[:, qcol]
                qsv = css_own[:, qcol]
                p.tt(tc_t[:, :], krb[:, :], qcv, ALU.mult)
                p.tt(ts_t[:, :], ps_2[:, :], qsv, ALU.mult)
                p.tt(xr[:, :], tc_t[:, :], ts_t[:, :], ALU.add)
                p.ts(kro[:, :], xr[:, :], QSC, ALU.mult)
                p.dma("sync", scr_q[2 * cht, 64:128, qcol], kro[0:64, :],
                      okey="bsq2")
                p.dma("sync", scr_q[2 * cht + 1, 64:128, qcol], kro[64:128, :],
                      okey="bsq3")

    with p.scope():
        Khn = [p.sb(f"b_khn{i}", [64, 4096], BF16) for i in range(2)]
        Kha = [p.sb(f"b_kha{i}", [65, 4096], BF16) for i in range(2)]
        Vh = [p.sb(f"b_vh{i}", [128, 32, 128], BF16) for i in range(2)]
        Qhn = [p.sb(f"b_qhn{i}", [64, 2048], BF16) for i in range(2)]
        Qha = [p.sb(f"b_qha{i}", [65, 2048], BF16) for i in range(2)]
        bmask_t = p.sb("b_bmask", [128, 8, 512], BF16)
        onesb_t = p.sb("b_onesb", [128, 128], BF16)
        sqa = p.sb("b_sqa", [64, 2048], F32)
        sqb = p.sb("b_sqb", [64, 2048], F32)
        ktmp = p.sb("b_ktmp", [1, 8], F32)
        kmaxf = p.sb("b_kmaxf", [1, 1], F32)
        qnt = p.sb("b_qnt", [1, 512], F32)
        brow = p.sb("b_brow", [1, 2048], F32)
        pt = p.sb("b_pt", [128, 512], BF16)
        ptm = p.sb("b_ptm", [128, 512], BF16)
        rinv = p.sb("b_rinv", [1, 512], F32)
        rinvb = p.sb("b_rinvb", [128, 512], F32)
        aout = p.sb("b_aout", [128, 512], BF16)

        ps_s = p.ps("b_pss", [128, 512])
        ps_av = p.ps("b_psav", [128, 512])
        ps_rs = p.ps("b_psrs", [1, 512])

        p.dma("gpsimd", bmask_t[:, :, :],
              io["bmask"].rearrange("r k q -> k r q"))
        p.dma("gpsimd", onesb_t[:, :], io["ones"][:, :])
        for i in range(2):
            p.op("vector", "memset", Kha[i][64:65, :], 1.0,
                 outs=(Kha[i][64:65, :],))

        for h in range(16):
            par = h % 2
            p.dma("sync", Khn[par][0:64, :], scr_k[h, 0:64, :])
            p.dma("sync", Kha[par][0:64, :], scr_k[h, 64:128, :])
            p.dma("sync", Vh[par][:, :, :],
                  scr_v.rearrange("(b p) c -> p b c", p=128)[
                      :, :, h * 128:(h + 1) * 128])
            p.dma("sync", Qhn[par][0:64, :], scr_q[h, 0:64, :])
            p.dma("sync", Qha[par][0:64, :], scr_q[h, 64:128, :])
            p.act(sqa[:, :], Khn[par][0:64, :].rearrange(
                "p (a c) -> p a c", c=2048)[:, 0, :], AF.Square)
            p.act(sqb[:, :], Kha[par][0:64, :].rearrange(
                "p (a c) -> p a c", c=2048)[:, 0, :], AF.Square)
            for c2 in range(4):
                cs_ = slice(c2 * 512, (c2 + 1) * 512)
                p.mm(ps_rs[0:1, :], ones[0:64, 0:1], sqa[:, cs_],
                     start=True, stop=False, skip_group_check=True)
                p.mm(ps_rs[0:1, :], ones[0:64, 0:1], sqb[:, cs_],
                     start=False, stop=True, skip_group_check=True)
                p.red("reduce_max", ktmp[0:1, c2:c2 + 1], ps_rs[0:1, :])
            p.act(sqa[:, :], Khn[par][0:64, :].rearrange(
                "p (a c) -> p a c", c=2048)[:, 1, :], AF.Square)
            p.act(sqb[:, :], Kha[par][0:64, :].rearrange(
                "p (a c) -> p a c", c=2048)[:, 1, :], AF.Square)
            for c2 in range(4):
                cs_ = slice(c2 * 512, (c2 + 1) * 512)
                p.mm(ps_rs[0:1, :], ones[0:64, 0:1], sqa[:, cs_],
                     start=True, stop=False, skip_group_check=True)
                p.mm(ps_rs[0:1, :], ones[0:64, 0:1], sqb[:, cs_],
                     start=False, stop=True, skip_group_check=True)
                p.red("reduce_max", ktmp[0:1, 4 + c2:5 + c2], ps_rs[0:1, :])
            p.red("reduce_max", kmaxf[0:1, 0:1], ktmp[0:1, 0:8])
            p.act(kmaxf[0:1, 0:1], kmaxf[0:1, 0:1], AF.Ln)
            p.act(kmaxf[0:1, 0:1], kmaxf[0:1, 0:1], AF.Exp, scale=0.5)
            p.act(sqa[:, :], Qhn[par][0:64, :], AF.Square)
            p.act(sqb[:, :], Qha[par][0:64, :], AF.Square)
            for c2 in range(4):
                cs_ = slice(c2 * 512, (c2 + 1) * 512)
                p.mm(ps_rs[0:1, :], ones[0:64, 0:1], sqa[:, cs_],
                     start=True, stop=False, skip_group_check=True)
                p.mm(ps_rs[0:1, :], ones[0:64, 0:1], sqb[:, cs_],
                     start=False, stop=True, skip_group_check=True)
                p.act(qnt[0:1, :], ps_rs[0:1, :], AF.Ln)
                p.act(qnt[0:1, :], qnt[0:1, :], AF.Exp, scale=0.5)
                p.ts(qnt[0:1, :], qnt[0:1, :], kmaxf[0:1, 0:1], ALU.mult,
                     s2=-1.01, alu2=ALU.mult)
                p.vcopy(brow[0:1, cs_], qnt[0:1, :])
            p.dma("sync", scr_brow[0:1, :], brow[0:1, :])
            p.dma("gpsimd", Qha[par][64:65, :], scr_brow[0:1, :],
                  okey=(f"bqrow{par}", f"b_qha{par}"))
            for J in range(4):
                qcol = slice(J * 512, (J + 1) * 512)
                nkb = 8 * J + 8
                for kb in range(nkb):
                    kcol = slice(kb * 128, (kb + 1) * 128)
                    p.mm(ps_s[:, :], Khn[par][0:64, kcol],
                         Qhn[par][0:64, qcol], start=True, stop=False)
                    p.mm(ps_s[:, :], Kha[par][:, kcol], Qha[par][:, qcol],
                         start=False, stop=True)
                    r = kb - 8 * J
                    p.act(pt[:, :], ps_s[:, :], AF.Exp)
                    src = pt
                    if r >= 0:
                        p.tt(ptm[:, :], pt[:, :], bmask_t[:, r, :], ALU.mult)
                        src = ptm
                    p.mm(ps_av[:, :], Vh[par][:, kb, :], src[:, :],
                         start=(kb == 0), stop=(kb == nkb - 1),
                         skip_group_check=True)
                    p.mm(ps_rs[0:1, :], onesb_t[:, 0:1], src[:, :],
                         start=(kb == 0), stop=(kb == nkb - 1),
                         skip_group_check=True)
                p.recip(rinv[0:1, :], ps_rs[0:1, :])
                p.mm(ps_s[:, :], ones[0:1, 0:128], rinv[0:1, :])
                p.scopy(rinvb[:, :], ps_s[:, :])
                p.tt(aout[:, :], ps_av[:, :], rinvb[:, :], ALU.mult)
                p.dma("sync", scr_at[h, :, qcol], aout[:, :],
                      okey=(f"bat{J % 2}", "batall"))

    with p.scope():
        att_q = [p.sb(f"b_attq{i}", [128, 512], BF16) for i in range(16)]
        wob = [p.sb(f"b_wob{i}", [128, NDC, 128], BF16) for i in range(2)]
        x1f = [p.sb(f"b_x1f{i}", [128, TQ], F32) for i in range(2)]
        hq = [p.sb(f"b_hq{i}", [128, TQ], F32) for i in range(2)]
        ps_ow = p.ps("b_psow", [128, TQ])
        ow_v = io["owl"]
        for q4 in range(4):
            tq = slice(q4 * TQ, (q4 + 1) * TQ)
            for hc in range(16):
                p.dma("sync", att_q[hc][:, :], scr_at[hc, :, tq],
                      ikey="batall",
                      okey=(f"battq{hc % 4}", f"b_attq{hc}"))
            for dt in range(NDT):
                par = dt % 2
                p.dma("gpsimd", wob[par][:, :, :], ow_v[dt])
                for hc in range(NDC):
                    p.mm(ps_ow[:, :], wob[par][:, hc, :], att_q[hc][:, :],
                         start=(hc == 0), stop=(hc == NDC - 1))
                p.dma("sync", x1f[par][:, :], x1own_v[:, dt, tq])
                p.tt(hq[par][:, :], ps_ow[:, :], x1f[par][:, :], ALU.add)
                p.dma("sync", scr_h.rearrange("(dc p) t -> p dc t", p=128)[
                    :, dt, tq], hq[par][:, :], okey=(f"bh{dt % 3}", "bhall"))

    with p.scope():
        ln2c = p.sb("b_ln2", [128, NDC], F32)
        hf = [p.sb(f"b_hf{i}", [128, 512], F32) for i in range(2)]
        sqm = p.sb("b_sqm", [128, 512], F32)
        msr2 = p.sb("b_msr2", [1, 512], F32)
        rstdb2 = p.sb("b_rstdb2", [128, 512], F32)
        hr = p.sb("b_hr", [128, 512], F32)
        x1g = [p.sb(f"b_x1g{i}", [128, TQ], F32) for i in range(2)]
        ps_nm = R.ps_sum
        ps_bm = R.ps_bc
        p.dma("sync", ln2c[:, :], io["ln2w"][:, :])
        scrh_v = scr_h.rearrange("(dc p) t -> p dc t", p=128)

        cfg = {
            "NEr": 7, "NHT": 8, "nsub": 1, "router": "sigmoid",
            "scaling": 8.0, "gate_w": io["router_w"], "rb_bc": io["rb_bc"],
            "wgl": io["r_wgl"], "wul": io["r_wul"], "wdl": io["r_wdl"],
        }

        def mk_xb(q, xb):
            tq = slice(q * TQ, (q + 1) * TQ)
            for dc in range(NDC):
                par = dc % 2
                p.dma("sync", hf[par][:, :], scrh_v[:, dc, tq], ikey="bhall")
                p.act(sqm[:, :], hf[par][:, :], AF.Square)
                p.mm(ps_nm[0:1, :], ones[:, 0:1], sqm[:, :],
                     start=(dc == 0), stop=(dc == NDC - 1))
            p.ts(msr2[0:1, :], ps_nm[0:1, :], 1.0 / 2048.0, ALU.mult)
            p.act(msr2[0:1, :], msr2[0:1, :], AF.Ln, bias=C["eps"][0:1, 0:1])
            p.act(msr2[0:1, :], msr2[0:1, :], AF.Exp, scale=-0.5)
            p.mm(ps_bm[:, :], ones[0:1, 0:128], msr2[0:1, :])
            p.scopy(rstdb2[:, :], ps_bm[:, :])
            for dc in range(NDC):
                par = dc % 2
                p.dma("sync", hf[par][:, :], scrh_v[:, dc, tq], ikey="bhall")
                p.tt(hr[:, :], hf[par][:, :], rstdb2[:, :], ALU.mult)
                p.ts(xb[:, dc, :], hr[:, :], ln2c[:, dc:dc + 1], ALU.mult)

        def resid(q, dt, x4, ps):
            tq = slice(q * TQ, (q + 1) * TQ)
            par = dt % 2
            p.dma("sync", x1g[par][:, :],
                  x1own_v[:, dt, slice(q * TQ, (q + 1) * TQ)])
            p.dma("sync", hf[par][:, :], scrh_v[:, dt, tq], ikey="bhall")
            p.tt(x4[:, :], ps[:, :], x1g[par][:, :], ALU.add)
            p.tt(x4[:, :], x4[:, :], hf[par][:, :], ALU.add)

        emit_moe(p, C, cfg, mk_xb, resid, out_v, True, "bm")

    p.end_waits()


def _rope_tables():
    inv = 1.0 / 10000.0 ** (np.arange(0, RD2, 2, dtype=np.float64) / RD2)
    ang = np.outer(np.arange(4096, dtype=np.float64), inv)
    emb = np.concatenate([ang, ang], axis=-1)
    cos = np.tile(np.cos(emb).T, (2, 1))
    sin = np.tile(np.sin(emb).T, (2, 1))
    return cos.astype(BF), sin.astype(BF)


def _pswapT():
    P = np.zeros((128, 128), dtype=np.float32)
    for base in (0, 64):
        for d in range(32):
            P[base + d, base + d + 32] = -1.0
            P[base + d + 32, base + d] = 1.0
    return np.ascontiguousarray(P.T).astype(BF)


def _masks_b(core_i):
    bm = np.zeros((8, 128, 512), dtype=np.float32)
    kk = np.arange(128)[:, None]
    qq = np.arange(128)[None, :]
    for r in range(8):
        for sblk in range(4):
            rel = 2 * sblk + core_i
            if r < rel:
                bm[r, :, sblk * 128:(sblk + 1) * 128] = 1.0
            elif r == rel:
                bm[r, :, sblk * 128:(sblk + 1) * 128] = (kk <= qq)
    return bm.astype(BF)


def build_nc_phase_b():
    nc = bass.Bass("TRN2", target_bir_lowering=False, debug=False)
    io = {}

    def inp(name, shape, dt=F32):
        io[name] = nc.dram_tensor(name, shape, dt, kind="ExternalInput").ap()
    inp("x1T", [2048, 4096])
    inp("x1ownT", [2048, 2048])
    inp("ln1w", [128, 16])
    inp("ln2w", [128, 16])
    inp("kvdw", [2048, 512])
    inp("qdw", [2048, 512])
    inp("kuw", [512, 1024])
    inp("vuw", [512, 2048])
    inp("quw", [512, 1024])
    inp("ropekl", [8, 128, 16, 128])
    inp("ropeqw", [512, 1024])
    inp("owl", [16, 128, 16, 128])
    inp("cs_cos", [128, 4096], BF16)
    inp("cs_sin", [128, 4096], BF16)
    inp("cs_cos_own", [128, 2048], BF16)
    inp("cs_sin_own", [128, 2048], BF16)
    inp("pswapT", [128, 128], BF16)
    inp("bmask", [8, 128, 512], BF16)
    inp("router_w", [2048, 7])
    inp("rb_bc", [128, 7])
    inp("r_wgl", [8, 8, 128, 16, 128])
    inp("r_wul", [8, 8, 128, 16, 128])
    inp("r_wdl", [8, 16, 128, 8, 128])
    inp("identity", [128, 128])
    inp("ones", [128, 128])
    inp("epsln", [1, 1])
    io["x2"] = nc.dram_tensor("x2", [2048, 2048], F32, kind="ExternalOutput").ap()
    with ExitStack() as stack:
        p = Prog(nc, stack)
        C = emit_consts(p, io)
        C["ln"] = LNRes(p, "ln", C)
        emit_phase_b(p, io, C)
        p.install()
    return nc


_B_MOE_TILED = None


def _phase_b_prep(weights):
    global _B_MOE_TILED
    if _B_MOE_TILED is None:
        _B_MOE_TILED = _moe_pretile(
            weights["L0_ffn_r_gate"], weights["L0_ffn_r_up"],
            weights["L0_ffn_r_down"], weights["L0_ffn_sh_gate"],
            weights["L0_ffn_sh_up"], weights["L0_ffn_sh_down"], 8)


def _phase_b_core_inputs(x1, weights, b, i):
    _phase_b_prep(weights)
    cos, sin = _rope_tables()
    x1b = x1[b]
    own = x1b.reshape(32, 128, 2048)[i::2].reshape(2048, 2048)
    colsel = np.arange(4096).reshape(32, 128)[i::2].reshape(-1)
    return {
        "x1T": np.ascontiguousarray(x1b.T),
        "x1ownT": np.ascontiguousarray(own.T),
        "ln1w": np.ascontiguousarray(
            weights["L0_ffn_ln1_w"].reshape(16, 128).T),
        "ln2w": np.ascontiguousarray(
            weights["L0_ffn_ln2_w"].reshape(16, 128).T),
        "kvdw": weights["L0_ffn_kv_d_w"],
        "qdw": weights["L0_ffn_q_d_w"],
        "kuw": weights["L0_ffn_k_u_w"],
        "vuw": weights["L0_ffn_v_u_w"],
        "quw": weights["L0_ffn_q_u_w"],
        "ropekl": _tile_gu(weights["L0_ffn_rope_k_w"]),
        "ropeqw": weights["L0_ffn_rope_q_w"],
        "owl": _tile_d(weights["L0_ffn_o_w"]),
        "cs_cos": cos, "cs_sin": sin,
        "cs_cos_own": np.ascontiguousarray(cos[:, colsel]),
        "cs_sin_own": np.ascontiguousarray(sin[:, colsel]),
        "pswapT": _pswapT(),
        "bmask": _masks_b(i),
        "router_w": weights["L0_ffn_router_w"],
        "rb_bc": np.tile(weights["L0_ffn_router_b"][None, :], (128, 1)),
        "r_wgl": _B_MOE_TILED[0], "r_wul": _B_MOE_TILED[1],
        "r_wdl": _B_MOE_TILED[2],
        **_consts(),
    }


def run_phase_b(x1, weights, core_ids=None):
    if core_ids is None:
        core_ids = list(range(8))
    nc = build_nc_phase_b()
    in_maps = [
        _phase_b_core_inputs(x1, weights, c // 2, c % 2) for c in core_ids]
    res = run_bass_kernel_spmd(nc, in_maps, core_ids=core_ids)
    x2 = np.zeros((4, 4096, 2048), dtype=np.float32)
    for j, c in enumerate(core_ids):
        b, i = c // 2, c % 2
        loc = res.results[j]["x2"].T.reshape(16, 128, 2048)
        x2[b].reshape(32, 128, 2048)[i::2] = loc
    return x2



def emit_phase_a2(p: Prog, io: dict, C):
    R = C["ln"]
    og_v = io["ogT"].rearrange("(hc p) t -> p hc t", p=128)
    x0_v = io["x0ownT"].rearrange("(dc p) t -> p dc t", p=128)
    out_v = io["x1own"].rearrange("(dc p) t -> p dc t", p=128)
    wo_v = io["woAl"]

    with p.scope():
        att_q = [p.sb(f"a2_og{i}", [128, 512], BF16) for i in range(16)]
        wob = [p.sb(f"a2_wob{i}", [128, NDC, 128], BF16) for i in range(2)]
        x0f = [p.sb(f"a2_x0f{i}", [128, TQ], F32) for i in range(2)]
        ps_ow = p.ps("a2_psow", [128, TQ])
        for q4 in range(4):
            tq = slice(q4 * TQ, (q4 + 1) * TQ)
            for hc in range(16):
                p.dma("sync", att_q[hc][:, :], og_v[:, hc, tq],
                      okey=(f"a2q{hc % 4}", f"a2_og{hc}"))

            def produce(dt, x4):
                par = dt % 2
                p.dma("gpsimd", wob[par][:, :, :], wo_v[dt])
                for hc in range(NDC):
                    p.mm(ps_ow[:, :], wob[par][:, hc, :], att_q[hc][:, :],
                         start=(hc == 0), stop=(hc == NDC - 1))
                p.dma("sync", x0f[par][:, :], x0_v[:, dt, tq])
                p.tt(x4[:, :], ps_ow[:, :], x0f[par][:, :], ALU.add)

            emit_postln(p, R, produce, out_v, tq, qtag="a2")
    p.end_waits()


def build_nc_phase_a2():
    nc = bass.Bass("TRN2", target_bir_lowering=False, debug=False)
    io = {}
    io["ogT"] = nc.dram_tensor("ogT", [2048, 2048], BF16, kind="ExternalInput").ap()
    io["x0ownT"] = nc.dram_tensor("x0ownT", [2048, 2048], F32, kind="ExternalInput").ap()
    io["woAl"] = nc.dram_tensor("woAl", [16, 128, 16, 128], F32, kind="ExternalInput").ap()
    io["identity"] = nc.dram_tensor("identity", [128, 128], F32, kind="ExternalInput").ap()
    io["ones"] = nc.dram_tensor("ones", [128, 128], F32, kind="ExternalInput").ap()
    io["epsln"] = nc.dram_tensor("epsln", [1, 1], F32, kind="ExternalInput").ap()
    io["x1own"] = nc.dram_tensor("x1own", [2048, 2048], F32, kind="ExternalOutput").ap()
    with ExitStack() as stack:
        p = Prog(nc, stack)
        C = emit_consts(p, io)
        C["ln"] = LNRes(p, "ln", C)
        emit_phase_a2(p, io, C)
        p.install()
    return nc



CA = 32
NCH = 128


def emit_phase_a1(p: Prog, io: dict, C):
    ones = C["ones"]
    x0_v = io["x0T"].rearrange("(dc p) t -> p dc t", p=128)

    scr_q = p.dram("a_scrq", [8, 128, 4096], BF16)
    scr_k = p.dram("a_scrk", [8, 4, 128, 4096], BF16)
    scr_w = p.dram("a_scrw", [8, 4, 128, 4096], BF16)
    scr_v = p.dram("a_scrv", [8, 4, 128, 4096], BF16)
    scr_A = p.dram("a_scrA", [8, 128, 4096], F32)
    scr_g = p.dram("a_scrg", [8, 128, 4096], BF16)
    scr_b = p.dram("a_scrb", [32, 4096], F32)
    scr_o = p.dram("a_scro", [8, 128, 4096], F32)

    f1b = p.sb("a_f1b", [128, 4096], BF16)
    g1b = p.sb("a_g1b", [128, 4096], BF16)
    with p.scope():
        xb = p.sb("a_xb", [128, NDC, 512], BF16)
        wbs_t = p.sb("a_wbs", [128, NDC, 32], BF16)
        wf1_t = p.sb("a_wf1", [128, NDC, 128], BF16)
        wg1_t = p.sb("a_wg1", [128, NDC, 128], BF16)
        bg1c = p.sb("a_bg1", [128, 1], F32)
        bev = p.sb("a_bev", [32, 512], F32)
        ps_a = p.ps("a1_psa", [128, 512])
        ps_b = p.ps("a1_psb", [32, 512])
        p.dma("gpsimd", wbs_t[:, :, :],
              io["wbs"].rearrange("(dc p) c -> p dc c", p=128))
        p.dma("gpsimd", wf1_t[:, :, :],
              io["wf1T"].rearrange("(dc p) c -> p dc c", p=128))
        p.dma("gpsimd", wg1_t[:, :, :],
              io["wg1T"].rearrange("(dc p) c -> p dc c", p=128))
        p.dma("sync", bg1c[:, :], io["bg1c"][:, :])
        for tt in range(8):
            tcol = slice(tt * 512, (tt + 1) * 512)
            p.dma("gpsimd", xb[:, :, :], x0_v[:, :, tcol])
            for dc in range(NDC):
                p.mm(ps_b[:, :], wbs_t[:, dc, :], xb[:, dc, :],
                     start=(dc == 0), stop=(dc == NDC - 1))
            p.act(bev[:, :], ps_b[:, :], AF.Sigmoid)
            p.dma("sync", scr_b[:, tcol], bev[:, :], okey="s1b")
            for dc in range(NDC):
                p.mm(ps_a[:, :], wf1_t[:, dc, :], xb[:, dc, :],
                     start=(dc == 0), stop=(dc == NDC - 1))
            p.scopy(f1b[:, tcol], ps_a[:, :])
            for dc in range(NDC):
                p.mm(ps_a[:, :], wg1_t[:, dc, :], xb[:, dc, :],
                     start=(dc == 0), stop=(dc == NDC - 1))
            p.ts(g1b[:, tcol], ps_a[:, :], bg1c[:, 0:1], ALU.add)

    with p.scope():
        wf2_t = p.sb("a_wf2", [128, 8, 128], BF16)
        wg2_t = p.sb("a_wg2", [128, 8, 128], BF16)
        dtb_t = p.sb("a_dtb", [128, 8], F32)
        bg2_t = p.sb("a_bg2", [128, 8], F32)
        alog_t = p.sb("a_alog", [1, 8], F32)
        nega_t = p.sb("a_nega", [1, 8], F32)
        negab = p.sb("a_negab", [128, 8], F32)
        u_t = p.sb("a_u", [128, 512], F32)
        a_t = p.sb("a_a", [128, 512], F32)
        e_t = p.sb("a_e", [128, 512], F32)
        r_t = p.sb("a_r", [128, 512], F32)
        sp_t = p.sb("a_sp", [128, 512], F32)
        la_t = [p.sb(f"a_la{i}", [128, 512], F32) for i in range(2)]
        gev = p.sb("a_gev", [128, 512], BF16)
        ps_a = p.ps("a2_psa", [128, 512])
        ps_c = p.ps("a2_psc", [128, 8])
        p.dma("gpsimd", wf2_t[:, :, :],
              io["wf2s"].rearrange("p (h c) -> p h c", c=128))
        p.dma("gpsimd", wg2_t[:, :, :],
              io["wg2s"].rearrange("p (h c) -> p h c", c=128))
        p.dma("sync", dtb_t[:, :], io["dtb"][:, :])
        p.dma("sync", bg2_t[:, :], io["bg2c"][:, :])
        p.dma("sync", alog_t[:, :], io["alog"][:, :])
        p.act(nega_t[0:1, :], alog_t[0:1, :], AF.Exp, scale=1.0)
        p.ts(nega_t[0:1, :], nega_t[0:1, :], -1.0, ALU.mult)
        p.mm(ps_c[:, :], ones[0:1, 0:128], nega_t[0:1, :])
        p.vcopy(negab[:, :], ps_c[:, :])
        for ht in range(8):
            for tt in range(8):
                tcol = slice(tt * 512, (tt + 1) * 512)
                p.mm(ps_a[:, :], wf2_t[:, ht, :], f1b[:, tcol])
                p.ts(u_t[:, :], ps_a[:, :], dtb_t[:, ht:ht + 1], ALU.add)
                p.op("scalar", "activation", a_t[:, :], u_t[:, :], AF.Abs,
                     outs=(a_t[:, :],), ins=(u_t[:, :],))
                p.act(e_t[:, :], a_t[:, :], AF.Exp, scale=-1.0)
                p.act(e_t[:, :], e_t[:, :], AF.Ln, bias=ones[:, 0:1])
                p.op("vector", "tensor_relu", r_t[:, :], u_t[:, :],
                     outs=(r_t[:, :],), ins=(u_t[:, :],))
                p.tt(sp_t[:, :], r_t[:, :], e_t[:, :], ALU.add)
                p.ts(la_t[0][:, :], sp_t[:, :], negab[:, ht:ht + 1], ALU.mult)
                src, dst = la_t[0], la_t[1]
                for sh in (1, 2, 4, 8, 16):
                    sv = src[:, :].rearrange("p (c t) -> p c t", t=32)
                    dv = dst[:, :].rearrange("p (c t) -> p c t", t=32)
                    p.tt(dv[:, :, sh:32], sv[:, :, sh:32], sv[:, :, 0:32 - sh],
                         ALU.add)
                    p.vcopy(dv[:, :, 0:sh], sv[:, :, 0:sh], eng="gpsimd")
                    src, dst = dst, src
                p.dma("sync", scr_A[ht, :, tcol], src[:, :],
                      okey=f"s2A{tt % 2}")
                p.mm(ps_a[:, :], wg2_t[:, ht, :], g1b[:, tcol])
                p.ts(gev[:, :], ps_a[:, :], bg2_t[:, ht:ht + 1], ALU.add)
                p.dma("sync", scr_g[ht, :, tcol], gev[:, :],
                      okey=f"s2g{tt % 2}")

    with p.scope():
        xb = p.sb("a3_xb", [128, NDC, 512], BF16)
        wsl = [p.sb(f"a3_w{g}", [128, NDC, 128], BF16) for g in range(18)]
        cw_t = p.sb("a3_cw", [128, 72, 4], F32)
        tails = p.sb("a3_tails", [128, 72, 3], F32)
        zf = p.sb("a3_zf", [128, 515], F32)
        c1 = p.sb("a3_c1", [128, 512], F32)
        sil = p.sb("a3_sil", [128, 512], F32)
        sq2 = p.sb("a3_sq2", [128, 512], F32)
        nrm = p.sb("a3_nrm", [1, 512], F32)
        nrb = p.sb("a3_nrb", [128, 512], F32)
        outb = p.sb("a3_outb", [128, 512], BF16)
        brow = [p.sb(f"a3_brow{i}", [1, 512], F32) for i in range(2)]
        bbc = p.sb("a3_bbc", [128, 512], F32)
        wev = p.sb("a3_wev", [128, 512], BF16)
        ps_a = p.ps("a3_psa", [128, 512])
        ps_n = p.ps("a3_psn", [1, 512])
        ps_b = p.ps("a3_psb", [128, 512])
        p.dma("sync", cw_t[:, :, :], io["convw"][:, :, :])
        p.op("vector", "memset", tails[:, :, :], 0.0,
             outs=(tails[:, :, :],))

        def cht_kind(cht):
            if cht < 8:
                return "q", None, cht
            if cht < 40:
                return "k", None, cht - 8
            return "v", None, cht - 40

        for grp in range(4):
            chts = list(range(grp * 18, (grp + 1) * 18))
            for gi, cht in enumerate(chts):
                p.dma("gpsimd", wsl[gi][:, :, :], io["wqkvl"][cht])
            for tt in range(8):
                tcol = slice(tt * 512, (tt + 1) * 512)
                p.dma("gpsimd", xb[:, :, :], x0_v[:, :, tcol])
                for gi, cht in enumerate(chts):
                    kind, wsrc, off = cht_kind(cht)
                    for dc in range(NDC):
                        p.mm(ps_a[:, :], wsl[gi][:, dc, :], xb[:, dc, :],
                             start=(dc == 0), stop=(dc == NDC - 1))
                    p.vcopy(zf[:, 0:3], tails[:, cht, :])
                    p.scopy(zf[:, 3:515], ps_a[:, :])
                    p.vcopy(tails[:, cht, :], zf[:, 512:515], eng="gpsimd")
                    p.ts(c1[:, :], zf[:, 0:512], cw_t[:, cht, 0:1], ALU.mult)
                    p.stt(c1[:, :], zf[:, 1:513], cw_t[:, cht, 1:2],
                          c1[:, :], ALU.mult, ALU.add)
                    p.stt(c1[:, :], zf[:, 2:514], cw_t[:, cht, 2:3],
                          c1[:, :], ALU.mult, ALU.add)
                    p.stt(c1[:, :], zf[:, 3:515], cw_t[:, cht, 3:4],
                          c1[:, :], ALU.mult, ALU.add)
                    if kind == "v":
                        p.act(outb[:, :], c1[:, :], AF.Silu)
                        h8, r4 = off // 4, off % 4
                        p.dma("sync", scr_v[h8, r4, :, tcol], outb[:, :],
                              okey=f"s3v{tt % 2}")
                        continue
                    p.act(sil[:, :], c1[:, :], AF.Silu)
                    p.act(sq2[:, :], sil[:, :], AF.Square)
                    p.mm(ps_n[0:1, :], ones[:, 0:1], sq2[:, :])
                    p.act(nrm[0:1, :], ps_n[0:1, :], AF.Ln)
                    p.act(nrm[0:1, :], nrm[0:1, :], AF.Exp, scale=0.5)
                    p.ts(nrm[0:1, :], nrm[0:1, :], 1e-12, ALU.max)
                    p.recip(nrm[0:1, :], nrm[0:1, :])
                    if kind == "q":
                        p.ts(nrm[0:1, :], nrm[0:1, :],
                             float(1.0 / np.sqrt(128.0)), ALU.mult)
                    p.mm(ps_b[:, :], ones[0:1, 0:128], nrm[0:1, :])
                    p.scopy(nrb[:, :], ps_b[:, :])
                    p.tt(outb[:, :], sil[:, :], nrb[:, :], ALU.mult)
                    if kind == "q":
                        p.dma("sync", scr_q[off, :, tcol], outb[:, :],
                              okey=f"s3q{tt % 2}")
                    else:
                        h8, r4 = off // 4, off % 4
                        p.dma("sync", scr_k[h8, r4, :, tcol], outb[:, :],
                              okey=f"s3k{tt % 2}")
                        bpar = tt % 2
                        p.dma("sync", brow[bpar][0:1, :],
                              scr_b[off:off + 1, tcol], ikey="s1b")
                        p.mm(ps_b[:, :], ones[0:1, 0:128], brow[bpar][0:1, :])
                        p.scopy(bbc[:, :], ps_b[:, :])
                        p.tt(wev[:, :], outb[:, :], bbc[:, :], ALU.mult)
                        p.dma("sync", scr_w[h8, r4, :, tcol], wev[:, :],
                              okey=f"s3w{tt % 2}")
    return dict(scr_q=scr_q, scr_k=scr_k, scr_w=scr_w, scr_v=scr_v,
                scr_A=scr_A, scr_g=scr_g, scr_o=scr_o)


def emit_phase_a1_mkda(p: Prog, io: dict, C, scr, nch=NCH):
    ones = C["ones"]
    scr_q, scr_k, scr_w = scr["scr_q"], scr["scr_k"], scr["scr_w"]
    scr_v, scr_A, scr_o = scr["scr_v"], scr["scr_A"], scr["scr_o"]

    with p.scope():
        q8 = [p.sb(f"am_q{i}", [128, 8, 32], BF16) for i in range(2)]
        k8 = [p.sb(f"am_k{i}", [128, 8, 4, 32], BF16) for i in range(2)]
        w8 = [p.sb(f"am_w{i}", [128, 8, 4, 32], BF16) for i in range(2)]
        vT8 = [p.sb(f"am_v{i}", [128, 8, 4, 32], BF16) for i in range(2)]
        A8 = [p.sb(f"am_A{i}", [128, 8, 32], F32) for i in range(2)]
        Arel = p.sb("am_Arel", [128, 8, 32], F32)
        EK = p.sb("am_EK", [128, 8, 32], F32)
        EW = p.sb("am_EW", [128, 8, 32], F32)
        eAc = p.sb("am_eAc", [128, 8], F32)
        eA31 = p.sb("am_eA31", [128, 8], F32)
        f31 = p.sb("am_f31", [128, 8], F32)
        Khat = p.sb("am_Khat", [128, 8, 128], BF16)
        What = p.sb("am_What", [128, 8, 128], BF16)
        kexp = p.sb("am_kexp", [128, 8, 128], BF16)
        kdec = p.sb("am_kdec", [128, 8, 128], BF16)
        qhc = p.sb("am_qhc", [128, 8, 32], BF16)
        qhf = p.sb("am_qhf", [128, 8, 32], BF16)
        Nt = p.sb("am_Nt", [128, 8, 128], BF16)
        vS = p.sb("am_vS", [128, 8, 128], BF16)
        kdT = p.sb("am_kdT", [128, 8, 128], BF16)
        attnT = p.sb("am_attnT", [128, 8, 32], BF16)
        Tp = [p.sb(f"am_T{l}", [128, 8, 128], BF16) for l in range(7)]
        Up = [p.sb(f"am_U{i}", [128, 8, 128], BF16) for i in range(2)]
        yv = [p.sb(f"am_y{i}", [128, 8, 128], BF16) for i in range(2)]
        Sst = p.sb("am_S", [128, 8, 128], F32)
        Sbf = p.sb("am_Sbf", [128, 8, 128], BF16)
        Sdec = p.sb("am_Sdec", [128, 8, 128], F32)
        o8 = p.sb("am_o8", [128, 8, 32], F32)
        mmask = p.sb("am_mmask", [128, 128], BF16)
        amask = p.sb("am_amask", [128, 32], BF16)

        psf = [p.ps(f"am_psf{i}", [128, 512]) for i in range(4)]
        psb = [p.ps(f"am_psb{i}", [128, 1024], BF16) for i in range(2)]
        ps_o = p.ps("am_pso", [128, 256])

        p.dma("gpsimd", mmask[:, :], io["mmask"][:, :])
        p.dma("gpsimd", amask[:, :], io["amask"][:, :])
        p.op("vector", "memset", Sst[:, :, :], 0.0, outs=(Sst[:, :, :],))
        p.op("gpsimd", "memset", Sbf[:, :, :], 0.0, outs=(Sbf[:, :, :],))

        def h4(j):
            return slice(j * 4, (j + 1) * 4)

        for n in range(nch):
            par = n % 2
            tcol = slice(n * CA, (n + 1) * CA)
            p.dma("sync", q8[par][:, :, :],
                  scr_q.rearrange("h p t -> p h t")[:, :, tcol])
            p.dma("sync", k8[par][:, :, :, :],
                  scr_k.rearrange("h i p t -> p h i t")[:, :, :, tcol])
            p.dma("sync", w8[par][:, :, :, :],
                  scr_w.rearrange("h i p t -> p h i t")[:, :, :, tcol])
            p.dma("sync", vT8[par][:, :, :, :],
                  scr_v.rearrange("h i p t -> p h i t")[:, :, :, tcol])
            p.dma("sync", A8[par][:, :, :],
                  scr_A.rearrange("h p t -> p h t")[:, :, tcol])
            A = A8[par]
            kv = k8[par].rearrange("p h i t -> p h (i t)")
            wv = w8[par].rearrange("p h i t -> p h (i t)")

            p.tt(Arel[:, :, :], A[:, :, :],
                 A[:, :, 15:16].broadcast_to([128, 8, 32]), ALU.subtract)
            p.ts(Arel[:, :, :], Arel[:, :, :], 85.0, ALU.min)
            p.ts(Arel[:, :, :], Arel[:, :, :], -85.0, ALU.max)
            p.act(EK[:, :, :], Arel[:, :, :], AF.Exp)
            p.act(EW[:, :, :], Arel[:, :, :], AF.Exp, scale=-1.0)
            a15 = A[:, :, 15:16].rearrange("p h o -> p (h o)")
            a31 = A[:, :, 31:32].rearrange("p h o -> p (h o)")
            p.act(eAc[:, :], a15, AF.Exp)
            p.act(eA31[:, :], a31, AF.Exp)
            p.tt(f31[:, :], a31, a15, ALU.subtract)
            p.act(f31[:, :], f31[:, :], AF.Exp)
            ek4 = EK[:, :, :].unsqueeze(2).broadcast_to([128, 8, 4, 32])
            ew4 = EW[:, :, :].unsqueeze(2).broadcast_to([128, 8, 4, 32])
            kh4 = Khat.rearrange("p h (i t) -> p h i t", t=32)
            wh4 = What.rearrange("p h (i t) -> p h i t", t=32)
            p.tt(kh4, k8[par][:, :, :, :], ek4, ALU.mult)
            p.tt(wh4, w8[par][:, :, :, :], ew4, ALU.mult)
            p.tt(kexp[:, :, :], Khat[:, :, :],
                 eAc[:, :].unsqueeze(2).broadcast_to([128, 8, 128]), ALU.mult)
            p.tt(kdec[:, :, :], What[:, :, :],
                 f31[:, :].unsqueeze(2).broadcast_to([128, 8, 128]), ALU.mult,
                 eng="gpsimd")
            p.tt(qhc[:, :, :], q8[par][:, :, :], EK[:, :, :], ALU.mult)
            p.tt(qhf[:, :, :], qhc[:, :, :],
                 eAc[:, :].unsqueeze(2).broadcast_to([128, 8, 32]), ALU.mult,
                 eng="gpsimd")

            for h in range(8):
                p.mm(psf[h // 4][:, (h % 4) * 128:(h % 4 + 1) * 128],
                     Khat[:, h, :], What[:, h, :])
            for j in range(2):
                p.tt(Nt[:, j * 4:(j + 1) * 4, :],
                     psf[j][:, :].rearrange("p (a c) -> p a c", c=128),
                     mmask[:, :].unsqueeze(1).broadcast_to([128, 4, 128]),
                     ALU.mult)
            for h in range(8):
                p.trans(psb[h // 4][:, (h % 4) * 128:(h % 4 + 1) * 128],
                        Nt[:, h, :], C["identb"][:, :])
            for j in range(2):
                p.scopy(Tp[0].rearrange("p h c -> p (h c)")[
                    :, j * 512:(j + 1) * 512], psb[j][:, 0:512])
            for h in range(8):
                p.trans(psb[h // 4][:, (h % 4) * 128:(h % 4 + 1) * 128],
                        vT8[par].rearrange("p h i t -> p h (i t)")[:, h, :],
                        C["identb"][:, :])
            for j in range(2):
                p.scopy(vS.rearrange("p h c -> p (h c)")[
                    :, j * 512:(j + 1) * 512], psb[j][:, 0:512])
            for h in range(8):
                p.mm(psf[h // 4][:, (h % 4) * 128:(h % 4 + 1) * 128],
                     kexp[:, h, :], Sbf[:, h, :])
            for j in range(2):
                p.tt(yv[0].rearrange("p h c -> p (h c)")[
                    :, j * 512:(j + 1) * 512],
                    vS.rearrange("p h c -> p (h c)")[:, j * 512:(j + 1) * 512],
                    psf[j][:, :], ALU.subtract)

            for lvl in range(6):
                ucur = Up[lvl % 2] if lvl > 0 else Nt
                tcur = Tp[lvl]
                unxt = Up[(lvl + 1) % 2]
                for h in range(8):
                    p.mm(psf[h // 4][:, (h % 4) * 128:(h % 4 + 1) * 128],
                         tcur[:, h, :], ucur[:, h, :])
                for h in range(8):
                    p.mm(psf[2 + h // 4][:, (h % 4) * 128:(h % 4 + 1) * 128],
                         ucur[:, h, :], tcur[:, h, :])
                for j in range(2):
                    p.scopy(unxt.rearrange("p h c -> p (h c)")[
                        :, j * 512:(j + 1) * 512], psf[j][:, :])
                for j in range(2):
                    p.scopy(Tp[lvl + 1].rearrange("p h c -> p (h c)")[
                        :, j * 512:(j + 1) * 512], psf[2 + j][:, :])
            ycur, ynxt = yv[0], yv[1]
            for lvl in range(6, -1, -1):
                for h in range(8):
                    p.mm(psf[h // 4][:, (h % 4) * 128:(h % 4 + 1) * 128],
                         Tp[lvl][:, h, :], ycur[:, h, :])
                alu = ALU.add if lvl > 0 else ALU.subtract
                for j in range(2):
                    p.tt(ynxt.rearrange("p h c -> p (h c)")[
                        :, j * 512:(j + 1) * 512],
                        ycur.rearrange("p h c -> p (h c)")[
                            :, j * 512:(j + 1) * 512],
                        psf[j][:, :], alu)
                ycur, ynxt = ynxt, ycur
            u_f = ycur

            for h in range(8):
                p.mm(psf[2 + h // 4][:, (h % 4) * 32:(h % 4 + 1) * 32],
                     What[:, h, :], qhc[:, h, :])
            for j in range(2):
                p.tt(attnT[:, j * 4:(j + 1) * 4, :],
                    psf[2 + j][:, 0:128].rearrange("p (a c) -> p a c", c=32),
                    amask[:, :].unsqueeze(1).broadcast_to([128, 4, 32]),
                    ALU.mult)
            for h in range(8):
                p.mm(ps_o[:, h * 32:(h + 1) * 32], Sbf[:, h, :],
                     qhf[:, h, :], start=True, stop=False,
                     skip_group_check=True)
                p.mm(ps_o[:, h * 32:(h + 1) * 32], u_f[:, h, :],
                     attnT[:, h, :], start=False, stop=True,
                     skip_group_check=True)
            p.vcopy(o8[:, :, :].rearrange("p h t -> p (h t)"), ps_o[:, :])
            p.dma("sync", scr_o.rearrange("h p t -> p h t")[:, :, tcol],
                  o8[:, :, :], okey=f"amo{n % 3}")

            for h in range(8):
                p.trans(psb[h // 4][:, (h % 4) * 128:(h % 4 + 1) * 128],
                        kdec[:, h, :], C["identb"][:, :])
            for j in range(2):
                p.scopy(kdT.rearrange("p h c -> p (h c)")[
                    :, j * 512:(j + 1) * 512], psb[j][:, 0:512])
            for h in range(8):
                p.mm(psf[h // 4][:, (h % 4) * 128:(h % 4 + 1) * 128],
                     kdT[:, h, :], u_f[:, h, :])
            p.tt(Sdec[:, :, :], Sst[:, :, :],
                 eA31[:, :].unsqueeze(2).broadcast_to([128, 8, 128]),
                 ALU.mult)
            for j in range(2):
                p.tt(Sst.rearrange("p h c -> p (h c)")[
                    :, j * 512:(j + 1) * 512],
                    Sdec.rearrange("p h c -> p (h c)")[
                        :, j * 512:(j + 1) * 512],
                    psf[j][:, :], ALU.add)
            p.scopy(Sbf.rearrange("p h c -> p (h c)"),
                    Sst.rearrange("p h c -> p (h c)"))


def emit_phase_a1_gate(p: Prog, io: dict, C, scr):
    ones = C["ones"]
    scr_o, scr_g = scr["scr_o"], scr["scr_g"]
    og_v = io["og"].rearrange("(h p) t -> p h t", p=128)
    with p.scope():
        of = [p.sb(f"ag_of{i}", [128, 512], F32) for i in range(2)]
        gf = [p.sb(f"ag_gf{i}", [128, 512], BF16) for i in range(2)]
        sq2 = p.sb("ag_sq", [128, 512], F32)
        ms = p.sb("ag_ms", [1, 512], F32)
        rsb = p.sb("ag_rsb", [128, 512], F32)
        sg = p.sb("ag_sg", [128, 512], F32)
        onc = p.sb("ag_onc", [128, 1], F32)
        ogo = p.sb("ag_ogo", [128, 512], BF16)
        ps_n = p.ps("ag_psn", [1, 512])
        ps_b = p.ps("ag_psb", [128, 512])
        p.dma("sync", onc[:, :], io["onormc"][:, :])
        for h in range(8):
            for tt in range(8):
                par = tt % 2
                tcol = slice(tt * 512, (tt + 1) * 512)
                p.dma("sync", of[par][:, :], scr_o[h, :, tcol], ikey="amoall")
                p.dma("gpsimd", gf[par][:, :], scr_g[h, :, tcol])
                p.act(sq2[:, :], of[par][:, :], AF.Square)
                p.mm(ps_n[0:1, :], ones[:, 0:1], sq2[:, :])
                p.ts(ms[0:1, :], ps_n[0:1, :], 1.0 / 128.0, ALU.mult)
                p.act(ms[0:1, :], ms[0:1, :], AF.Ln, bias=C["eps"][0:1, 0:1])
                p.act(ms[0:1, :], ms[0:1, :], AF.Exp, scale=-0.5)
                p.mm(ps_b[:, :], ones[0:1, 0:128], ms[0:1, :])
                p.scopy(rsb[:, :], ps_b[:, :])
                p.act(sg[:, :], gf[par][:, :], AF.Sigmoid)
                p.tt(ogo[:, :], of[par][:, :], rsb[:, :], ALU.mult)
                p.ts(ogo[:, :], ogo[:, :], onc[:, 0:1], ALU.mult)
                p.tt(ogo[:, :], ogo[:, :], sg[:, :], ALU.mult)
                p.dma("sync", og_v[:, h, tcol], ogo[:, :],
                      okey=f"ago{tt % 2}")


def _mkda_masks():
    ii = np.arange(128) // 32
    tt_ = np.arange(128) % 32
    t1 = tt_[:, None]; i1 = ii[:, None]
    t2 = tt_[None, :]; i2 = ii[None, :]
    mm_ = (t1 > t2) | ((t1 == t2) & (i1 > i2))
    am = (tt_[:, None] <= np.arange(32)[None, :])
    return mm_.astype(BF), am.astype(BF)


def build_nc_phase_a1(nch=NCH):
    nc = bass.Bass("TRN2", target_bir_lowering=False, debug=False)
    io = {}

    def inp(name, shape, dt=F32):
        io[name] = nc.dram_tensor(name, shape, dt, kind="ExternalInput").ap()
    inp("x0T", [2048, 4096])
    inp("wqkvl", [72, 128, 16, 128])
    inp("wbs", [2048, 32])
    inp("wf1T", [2048, 128])
    inp("wf2s", [128, 1024])
    inp("wg1T", [2048, 128])
    inp("wg2s", [128, 1024])
    inp("convw", [128, 72, 4])
    inp("alog", [1, 8])
    inp("dtb", [128, 8])
    inp("bg1c", [128, 1])
    inp("bg2c", [128, 8])
    inp("onormc", [128, 1])
    inp("mmask", [128, 128], BF16)
    inp("amask", [128, 32], BF16)
    inp("identity", [128, 128])
    inp("ones", [128, 128])
    inp("epsln", [1, 1])
    io["og"] = nc.dram_tensor("og", [1024, 4096], BF16, kind="ExternalOutput").ap()
    with ExitStack() as stack:
        p = Prog(nc, stack)
        C = emit_consts(p, io)
        scr = emit_phase_a1(p, io, C)
        emit_phase_a1_mkda(p, io, C, scr, nch=nch)
        emit_phase_a1_gate(p, io, C, scr)
        p.end_waits()
        p.install()
    return nc


def _phase_a1_core_inputs(x0, weights, b, hg):
    hs = slice(hg * 8, (hg + 1) * 8)
    chs_q = slice(hg * 1024, (hg + 1) * 1024)
    chs_kv = slice(hg * 4096, (hg + 1) * 4096)
    cw = np.zeros((72, 128, 4), dtype=np.float32)
    cw[0:8] = weights["L0_attn_conv_q"][chs_q].reshape(8, 128, 4)
    cw[8:40] = weights["L0_attn_conv_k"][chs_kv].reshape(32, 128, 4)
    cw[40:72] = weights["L0_attn_conv_v"][chs_kv].reshape(32, 128, 4)
    mm_, am = _mkda_masks()
    return {
        "x0T": np.ascontiguousarray(x0[b].T),
        "wqkvl": np.concatenate([
            _tile_gu(np.ascontiguousarray(weights["L0_attn_Wq"][chs_q].T)),
            _tile_gu(np.ascontiguousarray(weights["L0_attn_Wk"][chs_kv].T)),
            _tile_gu(np.ascontiguousarray(weights["L0_attn_Wv"][chs_kv].T)),
        ], axis=0),
        "wbs": np.ascontiguousarray(
            weights["L0_attn_Wb"][hg * 32:(hg + 1) * 32].T),
        "wf1T": np.ascontiguousarray(weights["L0_attn_Wf1"].T),
        "wf2s": np.ascontiguousarray(weights["L0_attn_Wf2"][chs_q].T),
        "wg1T": np.ascontiguousarray(weights["L0_attn_Wg1"].T),
        "wg2s": np.ascontiguousarray(weights["L0_attn_Wg2"][chs_q].T),
        "convw": np.ascontiguousarray(cw.transpose(1, 0, 2)),
        "alog": np.ascontiguousarray(
            weights["L0_attn_A_log"][hs][None, :]),
        "dtb": np.ascontiguousarray(
            weights["L0_attn_dt_bias"][chs_q].reshape(8, 128).T),
        "bg1c": np.ascontiguousarray(weights["L0_attn_bg1"][:, None]),
        "bg2c": np.ascontiguousarray(
            weights["L0_attn_bg2"][chs_q].reshape(8, 128).T),
        "onormc": np.ascontiguousarray(weights["L0_attn_onorm_w"][:, None]),
        "mmask": mm_, "amask": am,
        **_consts(),
    }



def _run(nc, in_maps):
    return run_bass_kernel_spmd(nc, in_maps, core_ids=list(range(8)))


def kernel(**inputs):
    z = inputs
    x0 = np.asarray(z["L0_attn_hidden_states"])

    nc = build_nc_phase_a1()
    in_maps = [_phase_a1_core_inputs(x0, z, c // 2, c % 2) for c in range(8)]
    res = _run(nc, in_maps)
    og = [res.results[c]["og"] for c in range(8)]

    nc = build_nc_phase_a2()
    woAl = _tile_d(np.ascontiguousarray(z["L0_attn_Wo"].T))
    consts = _consts()
    in_maps = []
    for c in range(8):
        b, i = c // 2, c % 2
        ogf = np.concatenate([og[2 * b], og[2 * b + 1]], axis=0)
        colsel = np.arange(4096).reshape(32, 128)[i::2].reshape(-1)
        x0own = x0[b].reshape(32, 128, 2048)[i::2].reshape(2048, 2048)
        in_maps.append({
            "ogT": np.ascontiguousarray(ogf[:, colsel]),
            "x0ownT": np.ascontiguousarray(x0own.T),
            "woAl": woAl, **consts,
        })
    res = _run(nc, in_maps)
    x1 = np.zeros((4, 4096, 2048), dtype=np.float32)
    for c in range(8):
        b, i = c // 2, c % 2
        x1[b].reshape(32, 128, 2048)[i::2] = \
            res.results[c]["x1own"].T.reshape(16, 128, 2048)

    x2 = run_phase_b(x1, z)

    x3 = run_phase_c(x2, z)

    y = run_phase_d(x3, z)
    return y


# version 06
# speedup vs baseline: 1.0329x; 1.0329x over previous
import numpy as np
import ml_dtypes
from contextlib import ExitStack
import concourse.bass as bass
import concourse.mybir as mybir
from concourse.bass_utils import run_bass_kernel_spmd

F32 = mybir.dt.float32
BF16 = mybir.dt.bfloat16
AF = mybir.ActivationFunctionType
ALU = mybir.AluOpType
AX = mybir.AxisListType

ENGINES = ("sync", "tensor", "vector", "scalar", "gpsimd")
BF = ml_dtypes.bfloat16



class Sem:
    __slots__ = ("h", "n")

    def __init__(self, h):
        self.h = h
        self.n = 0

    def bump(self, k=1):
        self.n += k
        return self.n


class _Rec:
    __slots__ = ("eng", "meth", "args", "kw", "deps", "need_inc", "count",
                 "dma", "extra_inc")

    def __init__(self, eng, meth, args, kw):
        self.eng = eng
        self.meth = meth
        self.args = args
        self.kw = kw
        self.deps = []
        self.need_inc = False
        self.count = None
        self.dma = None
        self.extra_inc = None


class Prog:

    def __init__(self, nc, stack):
        self.nc = nc
        self.stack = stack
        self.scopes = [stack]
        self.q = {e: [] for e in ENGINES}
        self._si = 0
        self.esem = {e: self.sem(f"E_{e}") for e in ENGINES}
        self.dma_sems = {}
        self.sem_pool = {"sync": [], "gpsimd": [], "scalar": []}
        self.scope_keys = [set()]
        self.key_queue = {}
        self.last_w = {}
        self.readers = {}

    def barrier(self):
        lasts = {e: (self.q[e][-1] if self.q[e] else None) for e in ENGINES}
        dmas = [(s, s.n) for s in self.dma_sems.values() if s.n > 0]
        for e in ENGINES:
            rec = _Rec(e, "__barrier__", (), {})
            for e2 in ENGINES:
                if e2 != e and lasts[e2] is not None:
                    self._dep_add_barrier(rec, lasts[e2])
            rec.deps.extend(dmas)
            self.q[e].append(rec)

    def _dep_add_barrier(self, rec, dep):
        if isinstance(dep, _Rec):
            if dep.meth == "__barrier__":
                return
            if dep.dma is None:
                dep.need_inc = True
        rec.deps.append(dep)

    class _Scope:
        def __init__(self, p):
            self.p = p
            self.st = ExitStack()

        def __enter__(self):
            self.p.scopes.append(self.st)
            self.p.scope_keys.append(set())
            return self

        def __exit__(self, *a):
            self.p.scopes.pop()
            self.p.barrier()
            for key in self.p.scope_keys.pop():
                self.p.sem_pool[self.p.key_queue[key]].append(
                    self.p.dma_sems.pop(key))
            self.st.close()
            return False

    def scope(self):
        return Prog._Scope(self)

    def sem(self, name=None):
        h = self.stack.enter_context(self.nc.semaphore(name or f"s{self._si}"))
        self._si += 1
        return Sem(h)

    def sb(self, name, shape, dtype):
        return self.scopes[-1].enter_context(
            self.nc.sbuf_tensor(name, shape, dtype))

    def ps(self, name, shape, dtype=F32):
        return self.scopes[-1].enter_context(
            self.nc.psum_tensor(name, shape, dtype))

    def dram(self, name, shape, dtype=F32, kind="Internal"):
        return self.nc.dram_tensor(name, shape, dtype, kind=kind).ap()

    def _dep_add(self, rec, dep):
        if dep is None or dep is rec:
            return
        if isinstance(dep, _Rec):
            if dep.eng == "tensor" and rec.eng == "tensor":
                return
            dep.need_inc = True
        rec.deps.append(dep)

    def op(self, eng, meth, *args, outs=(), ins=(), nodep=False, inc=None,
           k=1, okey=None, ikey=None, **kw):
        rec = _Rec(eng, meth, args, kw)
        if inc is not None:
            inc.bump(k)
            rec.extra_inc = (inc, k)
        if okey is None:
            names_o = [a.name for a in outs]
        else:
            names_o = list(okey) if isinstance(okey, tuple) else [okey]
        if ikey is None:
            names_i = [a.name for a in ins]
        else:
            names_i = list(ikey) if isinstance(ikey, tuple) else [ikey]
        if meth in ("dma_start", "dma_start_transpose"):
            key = names_o[0]
            s = self.dma_sems.get(key)
            if s is None:
                pool = self.sem_pool[eng]
                s = pool.pop() if pool else self.sem(f"DM{self._si}")
                self.dma_sems[key] = s
                self.scope_keys[-1].add(key)
                self.key_queue[key] = eng
            assert self.key_queue[key] == eng, (key, eng)
            s.bump(16)
            rec.dma = (s, s.n)
        if not nodep:
            for nm in names_i:
                self._dep_add(rec, self.last_w.get(nm))
            for nm in names_o:
                for r in self.readers.get(nm, {}).values():
                    self._dep_add(rec, r)
                self._dep_add(rec, self.last_w.get(nm))
        for nm in names_o:
            self.last_w[nm] = rec
            self.readers[nm] = {}
        rkey = ("dma", id(rec.dma[0])) if rec.dma is not None else eng
        for nm in names_i:
            self.readers.setdefault(nm, {})[rkey] = rec
        self.q[eng].append(rec)
        return rec

    def dma(self, queue, out, in_, okey=None, ikey=None, **kw):
        return self.op(queue, "dma_start", outs=(out,), ins=(in_,),
                       out=out, in_=in_, okey=okey, ikey=ikey, **kw)

    def mm(self, out, lhsT, rhs, start=True, stop=True, **kw):
        return self.op("tensor", "matmul", out, lhsT, rhs,
                       outs=(out,), ins=(lhsT, rhs), start=start, stop=stop,
                       **kw)

    def trans(self, out, in_, ident, **kw):
        return self.op("tensor", "transpose", out, in_, ident,
                       outs=(out,), ins=(in_, ident), **kw)

    def act(self, out, in_, func, bias=0.0, scale=1.0, accum_out=None, **kw):
        ins = [in_]
        outs = [out]
        if isinstance(bias, bass.AP):
            ins.append(bias)
        if isinstance(scale, bass.AP):
            ins.append(scale)
        if accum_out is not None:
            outs.append(accum_out)
            kw["accum_out"] = accum_out
        return self.op("scalar", "activation", out, in_, func,
                       outs=tuple(outs), ins=tuple(ins), bias=bias,
                       scale=scale, **kw)

    def scopy(self, out, in_, **kw):
        return self.op("scalar", "copy", out, in_, outs=(out,), ins=(in_,),
                       **kw)

    def tt(self, out, a, b, alu, eng="vector", **kw):
        return self.op(eng, "tensor_tensor", out, a, b, outs=(out,),
                       ins=(a, b), op=alu, **kw)

    def ts(self, out, a, s1, alu, s2=None, alu2=None, eng="vector", **kw):
        ins = [a] + [x for x in (s1, s2) if isinstance(x, bass.AP)]
        if alu2 is None:
            return self.op(eng, "tensor_scalar", out, a, s1, s2, alu,
                           outs=(out,), ins=tuple(ins), **kw)
        return self.op(eng, "tensor_scalar", out, a, s1, s2, alu, alu2,
                       outs=(out,), ins=tuple(ins), **kw)

    def stt(self, out, in0, scalar, in1, op0, op1, eng="vector", **kw):
        ins = [in0, in1] + ([scalar] if isinstance(scalar, bass.AP) else [])
        return self.op(eng, "scalar_tensor_tensor", out, in0, scalar, in1,
                       outs=(out,), ins=tuple(ins), op0=op0, op1=op1, **kw)

    def vcopy(self, out, in_, eng="vector", **kw):
        return self.op(eng, "tensor_copy", out, in_, outs=(out,), ins=(in_,),
                       **kw)

    def red(self, meth, out, in_, **kw):
        return self.op("vector", meth, out, in_, outs=(out,), ins=(in_,),
                       axis=AX.X, **kw)

    def recip(self, out, in_, **kw):
        return self.op("vector", "reciprocal", out, in_, outs=(out,),
                       ins=(in_,), **kw)

    def end_waits(self):
        for s in self.dma_sems.values():
            if s.n:
                rec = _Rec("sync", "__wait__", (s.h, s.n), {})
                self.q["sync"].append(rec)

    def install(self):
        for e in ENGINES:
            n = 0
            for rec in self.q[e]:
                if rec.need_inc and rec.dma is None:
                    n += 1
                    rec.count = n
            self.esem[e].n = n
        nc = self.nc
        esem = self.esem
        waited = {e: {} for e in ENGINES}

        def _wait(E, eng, sh, v):
            w = waited[eng]
            if w.get(id(sh), -1) >= v:
                return
            E.wait_ge(sh, v)
            w[id(sh)] = v

        def emit(rec, E):
            eng = rec.eng
            if rec.meth == "__wait__":
                _wait(E, eng, rec.args[0], rec.args[1])
                return
            if rec.meth == "__barrier__":
                waits = {}
                for dep in rec.deps:
                    if isinstance(dep, _Rec):
                        if dep.dma is not None:
                            sh, v = dep.dma[0].h, dep.dma[1]
                        elif dep.count is not None:
                            sh, v = esem[dep.eng].h, dep.count
                        else:
                            continue
                    else:
                        sh, v = dep[0].h, dep[1]
                    if waits.get(id(sh), (None, -1))[1] < v:
                        waits[id(sh)] = (sh, v)
                for sh, v in waits.values():
                    _wait(E, eng, sh, v)
                return
            waits = {}
            for dep in rec.deps:
                if isinstance(dep, _Rec):
                    if dep.dma is not None:
                        sh, v = dep.dma[0].h, dep.dma[1]
                    else:
                        assert dep.count is not None, (rec.eng, rec.meth)
                        sh, v = esem[dep.eng].h, dep.count
                else:
                    sh, v = dep[0].h, dep[1]
                if waits.get(id(sh), (None, -1))[1] < v:
                    waits[id(sh)] = (sh, v)
            for sh, v in waits.values():
                _wait(E, eng, sh, v)
            try:
                r = getattr(E, rec.meth)(*rec.args, **rec.kw)
            except Exception as ex:
                raise RuntimeError(
                    f"emit failed: {rec.eng}.{rec.meth} args="
                    f"{[getattr(a, 'name', a) for a in rec.args]} "
                    f"kw={ {k: getattr(v, 'name', v) for k, v in rec.kw.items()} }"
                ) from ex
            if rec.dma is not None:
                r.then_inc(rec.dma[0].h, 16)
            elif rec.count is not None:
                r.then_inc(esem[rec.eng].h, 1)
            if rec.extra_inc is not None:
                r.then_inc(rec.extra_inc[0].h, rec.extra_inc[1])

        with nc.Block() as block:
            @block.sync
            def _(e):
                for rec in self.q["sync"]:
                    emit(rec, e)

            @block.tensor
            def _(e):
                for rec in self.q["tensor"]:
                    emit(rec, e)

            @block.vector
            def _(e):
                for rec in self.q["vector"]:
                    emit(rec, e)

            @block.scalar
            def _(e):
                for rec in self.q["scalar"]:
                    emit(rec, e)

            @block.gpsimd
            def _(e):
                for rec in self.q["gpsimd"]:
                    emit(rec, e)





class LNRes:

    def __init__(self, p: Prog, tag, consts_sb):
        self.x4p = [p.sb(f"{tag}_x4p{i}", [128, TQ], F32) for i in range(3)]
        self.sq = p.sb(f"{tag}_sq", [128, TQ], F32)
        self.mu = p.sb(f"{tag}_mu", [1, TQ], F32)
        self.var = p.sb(f"{tag}_var", [1, TQ], F32)
        self.tmp1 = p.sb(f"{tag}_tmp1", [1, TQ], F32)
        self.rstd = p.sb(f"{tag}_rstd", [1, TQ], F32)
        self.rstdb = p.sb(f"{tag}_rstdb", [128, TQ], F32)
        self.yt = [p.sb(f"{tag}_yt{i}", [128, TQ], F32) for i in range(2)]
        self.ps_sum = p.ps(f"{tag}_pssum", [1, TQ])
        self.ps_sq = p.ps(f"{tag}_pssq", [1, TQ])
        self.ps_bc = p.ps(f"{tag}_psbc", [128, TQ])
        self.scr = p.dram(f"{tag}_scr", [NDT, 128, TQ], F32)
        self.ones = consts_sb["ones"]
        self.eps = consts_sb["eps"]


def emit_postln(p: Prog, R: LNRes, produce, out_view, tq, qtag):
    for dt in range(NDT):
        slot = dt % 3
        x4 = R.x4p[slot]
        produce(dt, x4)
        p.dma("sync", R.scr[dt, :, :], x4[:, :], okey=f"{qtag}/s{dt % 3}")
        p.act(R.sq[:, :], x4[:, :], AF.Square)
        p.mm(R.ps_sum[0:1, :], R.ones[:, 0:1], x4[:, :],
             start=(dt == 0), stop=(dt == NDT - 1), skip_group_check=True)
        p.mm(R.ps_sq[0:1, :], R.ones[:, 0:1], R.sq[:, :],
             start=(dt == 0), stop=(dt == NDT - 1), skip_group_check=True)
    p.ts(R.mu[0:1, :], R.ps_sum[0:1, :], 1.0 / 2048.0, ALU.mult)
    p.ts(R.var[0:1, :], R.ps_sq[0:1, :], 1.0 / 2048.0, ALU.mult)
    p.tt(R.tmp1[0:1, :], R.mu[0:1, :], R.mu[0:1, :], ALU.mult)
    p.tt(R.var[0:1, :], R.var[0:1, :], R.tmp1[0:1, :], ALU.subtract)
    p.act(R.tmp1[0:1, :], R.var[0:1, :], AF.Ln, bias=R.eps[0:1, 0:1])
    p.act(R.rstd[0:1, :], R.tmp1[0:1, :], AF.Exp, scale=-0.5)
    p.mm(R.ps_bc[:, :], R.ones[0:1, 0:128], R.rstd[0:1, :])
    p.scopy(R.rstdb[:, :], R.ps_bc[:, :])
    p.mm(R.ps_bc[:, :], R.ones[0:1, 0:128], R.mu[0:1, :])
    for dt in range(NDT):
        slot = dt % 3
        ypar = dt % 2
        p.dma("sync", R.x4p[slot][:, :], R.scr[dt, :, :],
              ikey=f"{qtag}/s{dt % 3}")
        p.tt(R.yt[ypar][:, :], R.x4p[slot][:, :], R.ps_bc[:, :],
             ALU.subtract)
        p.tt(R.yt[ypar][:, :], R.yt[ypar][:, :], R.rstdb[:, :], ALU.mult)
        p.dma("sync", out_view[:, dt, tq], R.yt[ypar][:, :],
              okey=f"{qtag}/o{dt % 2}")






def emit_moe(p: Prog, C, cfg, mk_xb, resid, out_view, postln, tag):
    R = C["ln"]
    idt, ones = C["ident"], C["ones"]
    NEr, NHT, nsub = cfg["NEr"], cfg["NHT"], cfg["nsub"]
    wgl, wul, wdl = cfg["wgl"], cfg["wul"], cfg["wdl"]
    gate_w = cfg["gate_w"]
    rb_bc = cfg.get("rb_bc")
    sig = cfg["router"] == "sigmoid"
    scal = cfg.get("scaling", 1.0)
    NGT = (NEr + nsub) * NHT

    xb = p.sb(f"{tag}_xb", [128, NDC, TQ], BF16)
    gwt = p.sb(f"{tag}_gw", [128, NDC, 8], BF16)
    rbb = p.sb(f"{tag}_rbb", [128, 8], F32)
    r_m1 = p.sb(f"{tag}_m1", [128, 1], F32)
    r_neg = p.sb(f"{tag}_neg", [128, 1], F32)
    r_sum = p.sb(f"{tag}_sum", [128, 1], F32)
    r_rinv = p.sb(f"{tag}_rinv", [128, 1], F32)
    r_m2 = p.sb(f"{tag}_m2", [128, 1], F32)
    r_lg = p.sb(f"{tag}_lg", [128, 8], F32)
    r_probs = p.sb(f"{tag}_probs", [128, 8], F32)
    r_pn = p.sb(f"{tag}_pn", [128, 8], F32)
    r_eqm = p.sb(f"{tag}_eqm", [128, 8], F32)
    r_msk = p.sb(f"{tag}_msk", [128, 8], F32)
    r_ind = p.sb(f"{tag}_ind", [128, 8], F32)
    wts = p.sb(f"{tag}_wts", [128, 4, 8], F32)
    wrow8 = p.sb(f"{tag}_wrow8", [8, TQ], F32)
    wbcraw = p.sb(f"{tag}_wbcraw", [1, 8 * TQ], F32)
    wbc = p.sb(f"{tag}_wbc", [128, NEr, TQ], BF16)
    wgb = [p.sb(f"{tag}_wgb{i}", [128, NDC, 128], BF16) for i in range(2)]
    wub = [p.sb(f"{tag}_wub{i}", [128, NDC, 128], BF16) for i in range(2)]
    gs_t = p.sb(f"{tag}_gs", [128, TQ], BF16)
    t1_t = p.sb(f"{tag}_t1", [128, TQ], BF16)
    gtil = p.sb(f"{tag}_gtil", [128, NGT, TQ], BF16)
    wdb = [p.sb(f"{tag}_wdb{i}", [128, NHT, 128], BF16) for i in range(3)]

    ps_g = p.ps(f"{tag}_psg", [128, TQ])
    ps_u = p.ps(f"{tag}_psu", [128, TQ])
    ps_dn = p.ps(f"{tag}_psdn", [128, TQ])
    ps_lg = p.ps(f"{tag}_pslg", [128, 8])
    ps_tr = p.ps(f"{tag}_pstr", [8, 128])
    ps_bc = R.ps_bc

    scr_wts = p.dram(f"{tag}_scrwts", [8, TQ], F32)

    p.dma("sync", gwt[:, :, 0:NEr],
          gate_w.rearrange("(dc p) e -> p dc e", p=128))
    p.op("vector", "memset", wts[:, :, :], 0.0, outs=(wts[:, :, :],))
    if rb_bc is not None:
        p.dma("sync", rbb[:, 0:NEr], rb_bc[:, :])

    units = [(e, 0) for e in range(NEr)] + [("s", k) for k in range(nsub)]

    for q in range(4):
        tq = slice(q * TQ, (q + 1) * TQ)
        mk_xb(q, xb)

        for tsb in range(4):
            tss = slice(tsb * 128, (tsb + 1) * 128)
            for dc in range(NDC):
                p.mm(ps_lg[:, 0:NEr], xb[:, dc, tss], gwt[:, dc, 0:NEr],
                     start=(dc == 0), stop=(dc == NDC - 1))
            if sig:
                p.tt(r_lg[:, 0:NEr], ps_lg[:, 0:NEr], rbb[:, 0:NEr], ALU.add)
                p.act(r_pn[:, 0:NEr], r_lg[:, 0:NEr], AF.Sigmoid)
            else:
                p.red("reduce_max", r_m1[:, 0:1], ps_lg[:, 0:NEr])
                p.ts(r_neg[:, 0:1], r_m1[:, 0:1], -1.0, ALU.mult)
                p.act(r_probs[:, 0:NEr], ps_lg[:, 0:NEr], AF.Exp,
                      bias=r_neg[:, 0:1], accum_out=r_sum[:, 0:1])
                p.recip(r_rinv[:, 0:1], r_sum[:, 0:1])
                p.ts(r_pn[:, 0:NEr], r_probs[:, 0:NEr], r_rinv[:, 0:1],
                     ALU.mult)
            p.red("reduce_max", r_m1[:, 0:1], r_pn[:, 0:NEr])
            p.ts(r_eqm[:, 0:NEr], r_pn[:, 0:NEr], r_m1[:, 0:1], ALU.is_equal)
            p.stt(r_msk[:, 0:NEr], r_eqm[:, 0:NEr], -1e30, r_pn[:, 0:NEr],
                  ALU.mult, ALU.add)
            p.red("reduce_max", r_m2[:, 0:1], r_msk[:, 0:NEr])
            p.ts(r_ind[:, 0:NEr], r_pn[:, 0:NEr], r_m2[:, 0:1], ALU.is_ge)
            if sig:
                p.tt(r_sum[:, 0:1], r_m1[:, 0:1], r_m2[:, 0:1], ALU.add)
                p.recip(r_rinv[:, 0:1], r_sum[:, 0:1])
                p.tt(r_msk[:, 0:NEr], r_pn[:, 0:NEr], r_ind[:, 0:NEr],
                     ALU.mult)
                p.ts(wts[:, tsb, 0:NEr], r_msk[:, 0:NEr], r_rinv[:, 0:1],
                     ALU.mult, s2=1.0 / scal, alu2=ALU.mult)
            else:
                p.tt(wts[:, tsb, 0:NEr], r_pn[:, 0:NEr], r_ind[:, 0:NEr],
                     ALU.mult)
            p.trans(ps_tr[:, :], wts[:, tsb, :], idt[:])
            p.vcopy(wrow8[0:8, tss], ps_tr[0:8, 0:128])
        p.dma("sync", scr_wts[:, :], wrow8[:, :])
        p.dma("sync", wbcraw[0:1, :],
              scr_wts.rearrange("e t -> (e t)").unsqueeze(0))
        for e in range(NEr):
            p.mm(ps_bc[:, :], ones[0:1, 0:128],
                 wbcraw[0:1, e * TQ:(e + 1) * TQ])
            p.scopy(wbc[:, e, :], ps_bc[:, :])

        wload = 0
        for ui, (e, sub) in enumerate(units):
            for ht in range(NHT):
                par = wload % 2
                wload += 1
                p.dma("sync", wgb[par][:, :, :], wgl[ui, ht])
                p.dma("sync", wub[par][:, :, :], wul[ui, ht])
                for dc in range(NDC):
                    p.mm(ps_g[:, :], wgb[par][:, dc, :], xb[:, dc, :],
                         start=(dc == 0), stop=(dc == NDC - 1))
                for dc in range(NDC):
                    p.mm(ps_u[:, :], wub[par][:, dc, :], xb[:, dc, :],
                         start=(dc == 0), stop=(dc == NDC - 1))
                p.act(gs_t[:, :], ps_g[:, :], AF.Silu)
                gidx = (NEr * NHT + sub * NHT + ht) if e == "s"                     else e * NHT + ht
                if e == "s":
                    if scal != 1.0:
                        p.stt(gtil[:, gidx, :], ps_u[:, :], 1.0 / scal,
                              gs_t[:, :], ALU.mult, ALU.mult)
                    else:
                        p.tt(gtil[:, gidx, :], ps_u[:, :], gs_t[:, :],
                             ALU.mult)
                else:
                    p.tt(t1_t[:, :], ps_u[:, :], gs_t[:, :], ALU.mult)
                    p.tt(gtil[:, gidx, :], t1_t[:, :], wbc[:, e, :],
                         ALU.mult)

        def produce(dt, x4):
            wdi = 0
            for ui, (e, sub) in enumerate(units):
                wpar = wdi % 3
                wdi += 1
                p.dma("sync", wdb[wpar][:, :, :], wdl[ui, dt])
                gbase = (NEr * NHT + sub * NHT) if e == "s" else e * NHT
                for hc in range(NHT):
                    p.mm(ps_dn[:, :], wdb[wpar][:, hc, :],
                         gtil[:, gbase + hc, :],
                         start=(ui == 0 and hc == 0),
                         stop=(ui == len(units) - 1 and hc == NHT - 1))
            resid(q, dt, x4, ps_dn)

        if postln:
            emit_postln(p, R, produce, out_view, tq, qtag=tag)
        else:
            for dt in range(NDT):
                produce(dt, R.x4p[dt % 3])
                p.dma("sync", out_view[:, dt, tq], R.x4p[dt % 3][:, :],
                      okey=f"{tag}o{dt % 3}")



TQ = 512
NDC = 16
NDT = 16


def emit_consts(p: Prog, io):
    idt = p.sb("c_id", [128, 128], F32)
    idb = p.sb("c_idb", [128, 128], BF16)
    ones = p.sb("c_ones", [128, 128], F32)
    eps = p.sb("c_eps", [1, 1], F32)
    p.dma("sync", idt[:], io["identity"][:, :])
    p.dma("gpsimd", idb[:], io["identity"][:, :])
    p.dma("sync", ones[:], io["ones"][:, :])
    p.dma("sync", eps[:], io["epsln"][:, :])
    return {"ident": idt, "identb": idb, "ones": ones, "eps": eps}


def emit_phase_d(p: Prog, io: dict, C, nq=4):
    x3T, out = io["x3T"], io["y"]
    x3T_v = x3T.rearrange("(dc p) t -> p dc t", p=128)
    out_v = out.rearrange("(dc p) t -> p dc t", p=128)
    x3f = [p.sb(f"d_x3f{i}", [128, TQ], F32) for i in range(2)]

    cfg = {
        "NEr": 8, "NHT": 11, "nsub": 2, "router": "softmax", "scaling": 1.0,
        "gate_w": io["l1_gate_w"], "wgl": io["l1_wgl"], "wul": io["l1_wul"],
        "wdl": io["l1_wdl"],
    }

    xcast = [p.sb("d_xc0", [128, TQ // 2], F32)]

    def mk_xb(q, xb):
        for dc in range(NDC):
            for hlf in range(2):
                par = 0
                cs = slice(q * TQ + hlf * (TQ // 2),
                           q * TQ + (hlf + 1) * (TQ // 2))
                p.dma("sync", xcast[par][:, :], x3T_v[:, dc, cs])
                p.scopy(xb[:, dc, hlf * (TQ // 2):(hlf + 1) * (TQ // 2)],
                        xcast[par][:, :])

    def resid(q, dt, x4, ps):
        par = dt % 2
        p.dma("sync", x3f[par][:, :],
              x3T_v[:, dt, slice(q * TQ, (q + 1) * TQ)])
        p.tt(x4[:, :], ps[:, :], x3f[par][:, :], ALU.add)

    emit_moe(p, C, cfg, mk_xb, resid, out_v, True, "d")
    p.end_waits()



def _tile_gu(w2d):
    nh = w2d.shape[1] // 128
    return np.ascontiguousarray(
        w2d.reshape(16, 128, nh, 128).transpose(2, 1, 0, 3))


def _tile_d(w2d):
    nh = w2d.shape[0] // 128
    return np.ascontiguousarray(
        w2d.reshape(nh, 128, 16, 128).transpose(2, 1, 0, 3))


def _moe_pretile(wg, wu, wd, swg, swu, swd, NHT):
    NEr = wg.shape[0]
    H = wg.shape[2]
    nsub = swg.shape[1] // (NHT * 128)
    U = NEr + nsub

    tile_gu, tile_d = _tile_gu, _tile_d
    wgl = np.zeros((U, NHT, 128, 16, 128), dtype=np.float32)
    wul = np.zeros((U, NHT, 128, 16, 128), dtype=np.float32)
    wdl = np.zeros((U, 16, 128, NHT, 128), dtype=np.float32)
    for e in range(NEr):
        wgl[e] = tile_gu(wg[e])
        wul[e] = tile_gu(wu[e])
        wdl[e] = tile_d(wd[e])
    tg, tu, td = tile_gu(swg), tile_gu(swu), tile_d(swd)
    for k in range(nsub):
        wgl[NEr + k] = tg[k * NHT:(k + 1) * NHT]
        wul[NEr + k] = tu[k * NHT:(k + 1) * NHT]
        wdl[NEr + k] = td[:, :, k * NHT:(k + 1) * NHT, :]
    return wgl, wul, wdl


def _consts():
    return {
        "identity": np.eye(128, dtype=np.float32),
        "ones": np.ones((128, 128), dtype=np.float32),
        "epsln": np.full((1, 1), 1e-5, dtype=np.float32),
    }


def build_nc_phase_d(nq=4):
    nc = bass.Bass("TRN2", target_bir_lowering=False, debug=False)
    io = {}
    io["x3T"] = nc.dram_tensor("x3T", [2048, 2048], F32, kind="ExternalInput").ap()
    io["l1_gate_w"] = nc.dram_tensor("l1_gate_w", [2048, 8], BF16, kind="ExternalInput").ap()
    io["l1_wgl"] = nc.dram_tensor("l1_wgl", [10, 11, 128, 16, 128], BF16, kind="ExternalInput").ap()
    io["l1_wul"] = nc.dram_tensor("l1_wul", [10, 11, 128, 16, 128], BF16, kind="ExternalInput").ap()
    io["l1_wdl"] = nc.dram_tensor("l1_wdl", [10, 16, 128, 11, 128], BF16, kind="ExternalInput").ap()
    io["identity"] = nc.dram_tensor("identity", [128, 128], F32, kind="ExternalInput").ap()
    io["ones"] = nc.dram_tensor("ones", [128, 128], F32, kind="ExternalInput").ap()
    io["epsln"] = nc.dram_tensor("epsln", [1, 1], F32, kind="ExternalInput").ap()
    io["y"] = nc.dram_tensor("y", [2048, 2048], F32, kind="ExternalOutput").ap()
    with ExitStack() as stack:
        p = Prog(nc, stack)
        C = emit_consts(p, io)
        C["ln"] = LNRes(p, "ln", C)
        emit_phase_d(p, io, C, nq=nq)
        p.install()
    return nc


def _l1_common(weights):
    wgl, wul, wdl = _moe_pretile(
        weights["L1_ffn_w_gate"], weights["L1_ffn_w_up"],
        weights["L1_ffn_w_down"], weights["L1_ffn_sw_gate"],
        weights["L1_ffn_sw_up"], weights["L1_ffn_sw_down"], 11)
    return {
        "l1_gate_w": weights["L1_ffn_gate_w"].astype(BF),
        "l1_wgl": wgl.astype(BF), "l1_wul": wul.astype(BF),
        "l1_wdl": wdl.astype(BF),
        **_consts(),
    }


def run_phase_d(x3, weights, core_ids=None):
    if core_ids is None:
        core_ids = list(range(8))
    nc = build_nc_phase_d()
    common = _l1_common(weights)
    in_maps = []
    for c in core_ids:
        b, i = c // 2, c % 2
        x3T = np.ascontiguousarray(x3[b, i * 2048:(i + 1) * 2048, :].T)
        in_maps.append({"x3T": x3T, **common})
    res = run_bass_kernel_spmd(nc, in_maps, core_ids=core_ids)
    y = np.zeros((4, 4096, 2048), dtype=np.float32)
    for j, c in enumerate(core_ids):
        b, i = c // 2, c % 2
        y[b, i * 2048:(i + 1) * 2048, :] = res.results[j]["y"].T
    return y



NKV = 4
NKB = 20


def emit_phase_c(p: Prog, io: dict, C):
    x2m, x2s, out = io["x2m"], io["x2s"], io["x3"]
    wkT, wvT = io["wkT"], io["wvT"]
    wmask, smask = io["wmask"], io["smask"]
    R = C["ln"]
    ones = C["ones"]

    x2m_v = x2m.rearrange("(dc p) t -> p dc t", p=128)
    x2s_v = x2s.rearrange("(dc p) t -> p dc t", p=128)
    out_v = out.rearrange("(dc p) t -> p dc t", p=128)
    wkT_v = wkT.rearrange("(dc p) c -> p dc c", p=128)
    wvT_v = wvT.rearrange("(dc p) c -> p dc c", p=128)
    wqT_v = io["wql"]
    woT_v = io["wol"]

    onesb_t = p.sb("c_onesb", [128, 128], BF16)
    p.dma("gpsimd", onesb_t[:], io["ones"][:, :])

    xb = [p.sb(f"c_xb{i}", [128, NDC, 512], BF16) for i in range(1)]
    xcastc = [p.sb(f"c_xc{i}", [128, 512], F32) for i in range(2)]
    Kt = p.sb("c_K", [128, NKV, 2560], BF16)
    Ksk = p.sb("c_Ksk", [128, NKV, 128], BF16)
    Vt = p.sb("c_V", [128, NKB * NKV, 128], BF16)
    Vsk = p.sb("c_Vsk", [128, NKV, 128], BF16)
    wkt = p.sb("c_wkt", [128, NDC, 512], BF16)
    wvt = p.sb("c_wvt", [128, NDC, 512], BF16)
    wqb = [p.sb(f"c_wqb{i}", [128, NDC, 128], BF16) for i in range(2)]
    qev = p.sb("c_qev", [128, 512], BF16)
    Qh = [p.sb(f"c_Qh{i}", [128, 2048], BF16) for i in range(2)]
    sq_t = p.sb("c_sqt", [128, 512], F32)
    ktmp6 = p.sb("c_ktmp6", [1, 8], F32)
    kmaxf = p.sb("c_kmaxf", [1, 1], F32)
    cneg = p.sb("c_cneg", [1, NKV], F32)
    qnt = p.sb("c_qnt", [1, 512], F32)
    brow = p.sb("c_brow", [1, 2048], F32)
    bqb = p.sb("c_bqb", [128, 128], F32)
    mask_w = p.sb("c_mw", [128, 3, 3, 128], BF16)
    mask_s = p.sb("c_ms", [128, 4, 128], BF16)
    s_f32 = p.sb("c_sf", [128, 128], F32)
    pt_bf = p.sb("c_pt", [128, 128], BF16)
    ptm = p.sb("c_ptm", [128, 128], BF16)
    rinv = p.sb("c_rinv", [1, 128], F32)
    rinvb = p.sb("c_rinvb", [128, 128], F32)
    aout = p.sb("c_aout", [128, 128], BF16)
    att_q = [p.sb(f"c_attq{i}", [128, 512], BF16) for i in range(16)]
    wob = [p.sb(f"c_wob{i}", [128, NDC, 128], BF16) for i in range(2)]
    x2f = [p.sb(f"c_x2f{i}", [128, TQ], F32) for i in range(2)]

    ps_p = p.ps("c_psp", [128, 512])
    ps_s = p.ps("c_pss", [128, 128])
    ps_av = p.ps("c_psav", [128, 128])
    ps_rs = p.ps("c_psrs", [1, 512])
    ps_ow = p.ps("c_psow", [128, TQ])

    scr_attn = p.dram("c_scrattn", [16, 128, 2048], BF16)
    scr_q = p.dram("c_scrq", [16, 128, 2048], BF16)

    p.dma("sync", mask_w[:, :, :, :],
          wmask.rearrange("a r k q -> k a r q"))
    p.dma("sync", mask_s[:, :, :], smask.rearrange("a k q -> k a q"))
    p.dma("sync", wkt[:, :, :], wkT_v[:, :, :])
    p.dma("sync", wvt[:, :, :], wvT_v[:, :, :])

    for tt in range(6):
        par = 0
        sink = tt == 5
        if sink:
            for dc in range(NDC):
                pc = dc % 2
                p.dma("sync", xcastc[pc][:, 0:128], x2s_v[:, dc, :])
                p.scopy(xb[par][:, dc, 0:128], xcastc[pc][:, 0:128])
            tw = 128
        else:
            for dc in range(NDC):
                pc = dc % 2
                p.dma("sync", xcastc[pc][:, :],
                      x2m_v[:, dc, tt * 512:(tt + 1) * 512])
                p.scopy(xb[par][:, dc, :], xcastc[pc][:, :])
            tw = 512
        for kv in range(NKV):
            for dc in range(NDC):
                p.mm(ps_p[:, 0:tw],
                     wkt[:, dc, kv * 128:(kv + 1) * 128],
                     xb[par][:, dc, 0:tw],
                     start=(dc == 0), stop=(dc == NDC - 1))
            if sink:
                p.scopy(Ksk[:, kv, :], ps_p[:, 0:128])
            else:
                p.scopy(Kt[:, kv, tt * 512:(tt + 1) * 512], ps_p[:, 0:tw])
        nsub = 1 if sink else 4
        for sub in range(nsub):
            for dc in range(NDC):
                p.mm(ps_p[:, 0:512],
                     xb[par][:, dc, sub * 128:(sub + 1) * 128],
                     wvt[:, dc, :],
                     start=(dc == 0), stop=(dc == NDC - 1))
            for kv in range(NKV):
                if sink:
                    p.scopy(Vsk[:, kv, :], ps_p[:, kv * 128:(kv + 1) * 128])
                else:
                    kb = tt * 4 + sub
                    p.scopy(Vt[:, kb * NKV + kv, :],
                            ps_p[:, kv * 128:(kv + 1) * 128])

    for qt in range(4):
        for dc in range(NDC):
            pc = dc % 2
            p.dma("sync", xcastc[pc][:, :],
                  x2m_v[:, dc, 256 + qt * 512:256 + (qt + 1) * 512])
            p.scopy(xb[0][:, dc, :], xcastc[pc][:, :])
        for h in range(16):
            wpar = h % 2
            p.dma("sync", wqb[wpar][:, :, :], wqT_v[h])
            for dc in range(NDC):
                p.mm(ps_p[:, 0:512], wqb[wpar][:, dc, :],
                     xb[0][:, dc, :],
                     start=(dc == 0), stop=(dc == NDC - 1))
            p.act(qev[:, :], ps_p[:, 0:512], AF.Copy,
                  scale=float(1.0 / np.sqrt(128.0)))
            p.dma("sync", scr_q[h, :, qt * 512:(qt + 1) * 512], qev[:, :],
                  okey=(f"scq/{h % 4}", "scqall"))

    for kv in range(NKV):
        for tt in range(5):
            p.act(sq_t[:, :], Kt[:, kv, tt * 512:(tt + 1) * 512], AF.Square)
            p.mm(ps_rs[0:1, :], ones[:, 0:1], sq_t[:, :])
            p.red("reduce_max", ktmp6[0:1, tt:tt + 1], ps_rs[0:1, :])
        p.act(sq_t[:, 0:128], Ksk[:, kv, :], AF.Square)
        p.mm(ps_rs[0:1, 0:128], ones[:, 0:1], sq_t[:, 0:128])
        p.red("reduce_max", ktmp6[0:1, 5:6], ps_rs[0:1, 0:128])
        p.red("reduce_max", kmaxf[0:1, 0:1], ktmp6[0:1, 0:6])
        p.act(kmaxf[0:1, 0:1], kmaxf[0:1, 0:1], AF.Ln)
        p.act(kmaxf[0:1, 0:1], kmaxf[0:1, 0:1], AF.Exp, scale=0.5)
        p.ts(cneg[0:1, kv:kv + 1], kmaxf[0:1, 0:1], -1.0001, ALU.mult)

    for h in range(16):
        kv = h // 4
        qpar = h % 2
        p.dma("sync", Qh[qpar][:, :], scr_q[h, :, :], ikey="scqall")
        for qt in range(4):
            p.act(sq_t[:, :], Qh[qpar][:, qt * 512:(qt + 1) * 512], AF.Square)
            p.mm(ps_rs[0:1, :], ones[:, 0:1], sq_t[:, :])
            p.act(qnt[0:1, :], ps_rs[0:1, :], AF.Ln)
            p.act(qnt[0:1, :], qnt[0:1, :], AF.Exp, scale=0.5)
            p.ts(brow[0:1, qt * 512:(qt + 1) * 512], qnt[0:1, :],
                 cneg[0:1, kv:kv + 1], ALU.mult)
        for j in range(16):
            qc = slice(j * 128, (j + 1) * 128)
            jj = min(j, 2)
            jjs = min(j, 3)
            p.mm(ps_s[:, :], ones[0:1, 0:128], brow[0:1, qc])
            p.scopy(bqb[:, :], ps_s[:, :])
            for vis in range(4):
                if vis < 3:
                    kb = j + vis
                    lk = Kt[:, kv, kb * 128:(kb + 1) * 128]
                    lv = Vt[:, kb * NKV + kv, :]
                    msk = mask_w[:, jj, vis, :]
                else:
                    lk = Ksk[:, kv, :]
                    lv = Vsk[:, kv, :]
                    msk = mask_s[:, jjs, :]
                p.mm(ps_s[:, :], lk, Qh[qpar][:, qc])
                p.tt(s_f32[:, :], ps_s[:, :], bqb[:, :], ALU.subtract)
                p.act(pt_bf[:, :], s_f32[:, :], AF.Exp)
                p.tt(ptm[:, :], pt_bf[:, :], msk, ALU.mult)
                p.mm(ps_av[:, :], lv, ptm[:, :],
                     start=(vis == 0), stop=(vis == 3),
                     skip_group_check=True)
                p.mm(ps_rs[0:1, 0:128], onesb_t[:, 0:1], ptm[:, :],
                     start=(vis == 0), stop=(vis == 3),
                     skip_group_check=True)
            p.recip(rinv[0:1, :], ps_rs[0:1, 0:128])
            p.mm(ps_s[:, :], ones[0:1, 0:128], rinv[0:1, :])
            p.scopy(rinvb[:, :], ps_s[:, :])
            p.tt(aout[:, :], ps_av[:, :], rinvb[:, :], ALU.mult)
            p.dma("sync", scr_attn[h, :, qc], aout[:, :],
                  okey=(f"scat/{j % 4}", "scatall"))

    for q4 in range(4):
        tq = slice(q4 * TQ, (q4 + 1) * TQ)
        for hc in range(16):
            p.dma("sync", att_q[hc][:, :], scr_attn[hc, :, tq],
                  ikey="scatall", okey=(f"cattq{hc % 4}", f"c_attq{hc}"))

        def produce(dt, x4):
            par = dt % 2
            p.dma("sync", wob[par][:, :, :], woT_v[dt])
            for hc in range(NDC):
                p.mm(ps_ow[:, :], wob[par][:, hc, :], att_q[hc][:, :],
                     start=(hc == 0), stop=(hc == NDC - 1))
            p.dma("sync", x2f[par][:, :],
                  x2m_v[:, dt, 256 + q4 * TQ:256 + (q4 + 1) * TQ])
            p.tt(x4[:, :], ps_ow[:, :], x2f[par][:, :], ALU.add)

        emit_postln(p, R, produce, out_v, tq, qtag="c")

    p.end_waits()


def make_masks_c(core_i):
    wm = np.zeros((3, 3, 128, 128), dtype=np.float32)
    kk = np.arange(128)
    qq = np.arange(128)
    for jj in range(3):
        for r in range(3):
            d = 256 - 128 * r + qq[None, :] - kk[:, None]
            m = (d >= 0) & (d <= 255)
            if core_i == 0:
                kloc = 128 * (jj + r) + kk[:, None]
                m = m & ((kloc >= 256) | (jj + r >= 2))
            wm[jj, r] = m
    sm = np.zeros((4, 128, 128), dtype=np.float32)
    for jjs in range(4):
        gq = 2048 * core_i + 128 * jjs + qq[None, :]
        gk = kk[:, None]
        m = (gk < 4) & (gq - gk > 255) & (gk <= gq)
        if jjs == 3 and not (core_i == 0):
            m = (gk < 4)
        sm[jjs] = m
    if core_i == 1:
        for jj in range(3):
            for r in range(3):
                d = 256 - 128 * r + qq[None, :] - kk[:, None]
                wm[jj, r] = (d >= 0) & (d <= 255)
        for jjs in range(4):
            sm[jjs] = (kk[:, None] < 4)
    return wm.astype(BF), sm.astype(BF)


def build_nc_phase_c():
    nc = bass.Bass("TRN2", target_bir_lowering=False, debug=False)
    io = {}
    io["x2m"] = nc.dram_tensor("x2m", [2048, 2560], F32, kind="ExternalInput").ap()
    io["x2s"] = nc.dram_tensor("x2s", [2048, 128], F32, kind="ExternalInput").ap()
    io["wql"] = nc.dram_tensor("wql", [16, 128, 16, 128], BF16, kind="ExternalInput").ap()
    io["wkT"] = nc.dram_tensor("wkT", [2048, 512], BF16, kind="ExternalInput").ap()
    io["wvT"] = nc.dram_tensor("wvT", [2048, 512], BF16, kind="ExternalInput").ap()
    io["wol"] = nc.dram_tensor("wol", [16, 128, 16, 128], BF16, kind="ExternalInput").ap()
    io["wmask"] = nc.dram_tensor("wmask", [3, 3, 128, 128], BF16, kind="ExternalInput").ap()
    io["smask"] = nc.dram_tensor("smask", [4, 128, 128], BF16, kind="ExternalInput").ap()
    io["identity"] = nc.dram_tensor("identity", [128, 128], F32, kind="ExternalInput").ap()
    io["ones"] = nc.dram_tensor("ones", [128, 128], F32, kind="ExternalInput").ap()
    io["epsln"] = nc.dram_tensor("epsln", [1, 1], F32, kind="ExternalInput").ap()
    io["x3"] = nc.dram_tensor("x3", [2048, 2048], F32, kind="ExternalOutput").ap()
    with ExitStack() as stack:
        p = Prog(nc, stack)
        C = emit_consts(p, io)
        C["ln"] = LNRes(p, "ln", C)
        emit_phase_c(p, io, C)
        p.install()
    return nc


def _phase_c_core_inputs(x2, weights, b, i):
    lo = 2048 * i - 256
    x2m = np.zeros((2560, 2048), dtype=np.float32)
    src_lo = max(lo, 0)
    src_hi = min(2048 * i + 2304, 4096)
    x2m[src_lo - lo:src_hi - lo, :] = x2[b, src_lo:src_hi, :]
    wm, sm = make_masks_c(i)
    return {
        "x2m": np.ascontiguousarray(x2m.T),
        "x2s": np.ascontiguousarray(x2[b, 0:128, :].T),
        "wql": _tile_gu(np.ascontiguousarray(
            weights["L1_attn_wq"].T)).astype(BF),
        "wkT": np.ascontiguousarray(weights["L1_attn_wk"].T).astype(BF),
        "wvT": np.ascontiguousarray(weights["L1_attn_wv"].T).astype(BF),
        "wol": _tile_d(np.ascontiguousarray(
            weights["L1_attn_wo"].T)).astype(BF),
        "wmask": wm,
        "smask": sm,
        **_consts(),
    }


def run_phase_c(x2, weights, core_ids=None):
    if core_ids is None:
        core_ids = list(range(8))
    nc = build_nc_phase_c()
    in_maps = []
    for c in core_ids:
        b, i = c // 2, c % 2
        in_maps.append(_phase_c_core_inputs(x2, weights, b, i))
    res = run_bass_kernel_spmd(nc, in_maps, core_ids=core_ids)
    x3 = np.zeros((4, 4096, 2048), dtype=np.float32)
    for j, c in enumerate(core_ids):
        b, i = c // 2, c % 2
        x3[b, i * 2048:(i + 1) * 2048, :] = res.results[j]["x3"].T
    return x3



RD2 = 64


def emit_phase_b(p: Prog, io: dict, C):
    x1T, out = io["x1T"], io["x2"]
    R = C["ln"]
    ones = C["ones"]
    x1T_v = x1T.rearrange("(dc p) t -> p dc t", p=128)
    x1own_v = io["x1ownT"].rearrange("(dc p) t -> p dc t", p=128)
    out_v = out.rearrange("(dc p) t -> p dc t", p=128)

    scr_k = p.dram("b_scrk", [16, 128, 4096], BF16)
    scr_v = p.dram("b_scrv", [4096, 2048], BF16)
    scr_q = p.dram("b_scrq", [16, 128, 2048], BF16)
    scr_at = p.dram("b_scrat", [16, 128, 2048], BF16)
    scr_h = p.dram("b_scrh", [2048, 2048], F32)
    scr_brow = p.dram("b_scrbrow", [1, 2048], F32)

    QSC = float(1.0 / np.sqrt(128.0))

    with p.scope():
        kvdw_t = p.sb("b_kvdw", [128, NDC, 512], BF16)
        qdw_t = p.sb("b_qdw", [128, NDC, 512], BF16)
        kuw_t = p.sb("b_kuw", [128, 4, 1024], BF16)
        vuw_t = p.sb("b_vuw", [128, 4, 2048], BF16)
        quw_t = p.sb("b_quw", [128, 4, 1024], BF16)
        rqw_t = p.sb("b_rqw", [128, 4, 1024], BF16)
        rkw_b = [p.sb(f"b_rkw{i}", [128, NDC, 128], BF16) for i in range(2)]
        cs_c = p.sb("b_csc", [128, 4096], BF16)
        cs_s = p.sb("b_css", [128, 4096], BF16)
        pswT = p.sb("b_psw", [128, 128], BF16)
        ln1c = p.sb("b_ln1", [128, NDC], F32)
        xf = [p.sb(f"b_xf{i}", [128, 512], F32) for i in range(2)]
        sqx = p.sb("b_sqx", [128, 512], F32)
        msr = p.sb("b_msr", [1, 512], F32)
        rstdb = p.sb("b_rstdb", [128, 512], F32)
        xr = p.sb("b_xr", [128, 512], F32)
        xb1 = p.sb("b_xb1", [128, NDC, 512], BF16)
        kvd = p.sb("b_kvd", [128, 4, 512], BF16)
        qd = p.sb("b_qd", [128, 4, 2048], BF16)
        krb = p.sb("b_krb", [128, 512], BF16)
        tc_t = p.sb("b_tc", [128, 512], F32)
        ts_t = p.sb("b_ts", [128, 512], F32)
        kro = p.sb("b_kro", [128, 512], BF16)
        kpb = p.sb("b_kpb", [128, 512], BF16)
        vev = p.sb("b_vev", [128, 512], BF16)

        ps_p = p.ps("b_psp", [128, 512])
        ps_n = p.ps("b_psn", [1, 512])
        ps_b = p.ps("b_psb", [128, 512])
        ps_2 = p.ps("b_ps2", [128, 512])

        p.dma("sync", kvdw_t[:, :, :],
              io["kvdw"].rearrange("(dc p) c -> p dc c", p=128))
        p.dma("sync", qdw_t[:, :, :],
              io["qdw"].rearrange("(dc p) c -> p dc c", p=128))
        p.dma("sync", kuw_t[:, :, :],
              io["kuw"].rearrange("(lt p) c -> p lt c", p=128))
        p.dma("sync", vuw_t[:, :, :],
              io["vuw"].rearrange("(lt p) c -> p lt c", p=128))
        p.dma("sync", quw_t[:, :, :],
              io["quw"].rearrange("(lt p) c -> p lt c", p=128))
        p.dma("sync", rqw_t[:, :, :],
              io["ropeqw"].rearrange("(lt p) c -> p lt c", p=128))
        p.dma("sync", cs_c[:, :], io["cs_cos"][:, :])
        p.dma("sync", cs_s[:, :], io["cs_sin"][:, :])
        p.dma("sync", pswT[:, :], io["pswapT"][:, :])
        p.dma("sync", ln1c[:, :], io["ln1w"][:, :])

        csc_own = p.sb("b_cscown", [128, 2048], BF16)
        css_own = p.sb("b_cssown", [128, 2048], BF16)
        p.dma("sync", csc_own[:, :], io["cs_cos_own"][:, :])
        p.dma("sync", css_own[:, :], io["cs_sin_own"][:, :])

        ropekw_v = io["ropekl"]

        for tt in range(8):
            tcol = slice(tt * 512, (tt + 1) * 512)
            for dc in range(NDC):
                par = dc % 2
                p.dma("sync", xf[par][:, :], x1T_v[:, dc, tcol])
                p.act(sqx[:, :], xf[par][:, :], AF.Square)
                p.mm(ps_n[0:1, :], ones[:, 0:1], sqx[:, :],
                     start=(dc == 0), stop=(dc == NDC - 1))
            p.ts(msr[0:1, :], ps_n[0:1, :], 1.0 / 2048.0, ALU.mult)
            p.act(msr[0:1, :], msr[0:1, :], AF.Ln, bias=C["eps"][0:1, 0:1])
            p.act(msr[0:1, :], msr[0:1, :], AF.Exp, scale=-0.5)
            p.mm(ps_b[:, :], ones[0:1, 0:128], msr[0:1, :])
            p.scopy(rstdb[:, :], ps_b[:, :])
            for dc in range(NDC):
                par = dc % 2
                p.dma("sync", xf[par][:, :], x1T_v[:, dc, tcol])
                p.tt(xr[:, :], xf[par][:, :], rstdb[:, :], ALU.mult)
                p.ts(xb1[:, dc, :], xr[:, :], ln1c[:, dc:dc + 1], ALU.mult)
            for lt in range(4):
                for dc in range(NDC):
                    p.mm(ps_p[:, :], kvdw_t[:, dc, lt * 128:(lt + 1) * 128],
                         xb1[:, dc, :], start=(dc == 0), stop=(dc == NDC - 1))
                p.scopy(kvd[:, lt, :], ps_p[:, :])
            for cht in range(8):
                par = cht % 2
                p.dma("sync", rkw_b[par][:, :, :], ropekw_v[cht])
                for dc in range(NDC):
                    p.mm(ps_p[:, :], rkw_b[par][:, dc, :], xb1[:, dc, :],
                         start=(dc == 0), stop=(dc == NDC - 1))
                p.scopy(krb[:, :], ps_p[:, :])
                p.mm(ps_2[:, :], pswT[:, :], krb[:, :])
                p.tt(tc_t[:, :], krb[:, :], cs_c[:, tcol], ALU.mult)
                p.tt(ts_t[:, :], ps_2[:, :], cs_s[:, tcol], ALU.mult)
                p.tt(kro[:, :], tc_t[:, :], ts_t[:, :], ALU.add)
                p.dma("sync", scr_k[2 * cht, 64:128, tcol], kro[0:64, :],
                      okey=f"bsk{cht % 2}a")
                p.dma("sync", scr_k[2 * cht + 1, 64:128, tcol], kro[64:128, :],
                      okey=f"bsk{cht % 2}b")
            for cht in range(8):
                for lt in range(4):
                    p.mm(ps_p[:, :], kuw_t[:, lt, cht * 128:(cht + 1) * 128],
                         kvd[:, lt, :], start=(lt == 0), stop=(lt == 3))
                p.scopy(kpb[:, :], ps_p[:, :])
                p.dma("sync", scr_k[2 * cht, 0:64, tcol], kpb[0:64, :],
                      okey=f"bsk{cht % 2}c")
                p.dma("sync", scr_k[2 * cht + 1, 0:64, tcol], kpb[64:128, :],
                      okey=f"bsk{cht % 2}d")
            for sub in range(4):
                tb = 4 * tt + sub
                for cg in range(4):
                    for lt in range(4):
                        p.mm(ps_p[:, :],
                             kvd[:, lt, sub * 128:(sub + 1) * 128],
                             vuw_t[:, lt, cg * 512:(cg + 1) * 512],
                             start=(lt == 0), stop=(lt == 3))
                    p.scopy(vev[:, :], ps_p[:, :])
                    p.dma("sync", scr_v[tb * 128:(tb + 1) * 128,
                                        cg * 512:(cg + 1) * 512],
                          vev[:, :], okey=f"bsv{(sub * 4 + cg) % 4}")
        for qt in range(4):
            qcol = slice(qt * 512, (qt + 1) * 512)
            for dc in range(NDC):
                par = dc % 2
                p.dma("sync", xf[par][:, :], x1own_v[:, dc, qcol])
                p.act(sqx[:, :], xf[par][:, :], AF.Square)
                p.mm(ps_n[0:1, :], ones[:, 0:1], sqx[:, :],
                     start=(dc == 0), stop=(dc == NDC - 1))
            p.ts(msr[0:1, :], ps_n[0:1, :], 1.0 / 2048.0, ALU.mult)
            p.act(msr[0:1, :], msr[0:1, :], AF.Ln, bias=C["eps"][0:1, 0:1])
            p.act(msr[0:1, :], msr[0:1, :], AF.Exp, scale=-0.5)
            p.mm(ps_b[:, :], ones[0:1, 0:128], msr[0:1, :])
            p.scopy(rstdb[:, :], ps_b[:, :])
            for dc in range(NDC):
                par = dc % 2
                p.dma("sync", xf[par][:, :], x1own_v[:, dc, qcol])
                p.tt(xr[:, :], xf[par][:, :], rstdb[:, :], ALU.mult)
                p.ts(xb1[:, dc, :], xr[:, :], ln1c[:, dc:dc + 1], ALU.mult)
            for lt in range(4):
                for dc in range(NDC):
                    p.mm(ps_p[:, :], qdw_t[:, dc, lt * 128:(lt + 1) * 128],
                         xb1[:, dc, :], start=(dc == 0), stop=(dc == NDC - 1))
                p.scopy(qd[:, lt, qcol], ps_p[:, :])

        for cht in range(8):
            for qt in range(4):
                qcol = slice(qt * 512, (qt + 1) * 512)
                for lt in range(4):
                    p.mm(ps_p[:, :], quw_t[:, lt, cht * 128:(cht + 1) * 128],
                         qd[:, lt, qcol], start=(lt == 0), stop=(lt == 3))
                p.act(kpb[:, :], ps_p[:, :], AF.Copy, scale=QSC)
                p.dma("sync", scr_q[2 * cht, 0:64, qcol], kpb[0:64, :],
                      okey="bsq0")
                p.dma("sync", scr_q[2 * cht + 1, 0:64, qcol], kpb[64:128, :],
                      okey="bsq1")
                for lt in range(4):
                    p.mm(ps_p[:, :], rqw_t[:, lt, cht * 128:(cht + 1) * 128],
                         qd[:, lt, qcol], start=(lt == 0), stop=(lt == 3))
                p.scopy(krb[:, :], ps_p[:, :])
                p.mm(ps_2[:, :], pswT[:, :], krb[:, :])
                qcv = csc_own[:, qcol]
                qsv = css_own[:, qcol]
                p.tt(tc_t[:, :], krb[:, :], qcv, ALU.mult)
                p.tt(ts_t[:, :], ps_2[:, :], qsv, ALU.mult)
                p.tt(xr[:, :], tc_t[:, :], ts_t[:, :], ALU.add)
                p.ts(kro[:, :], xr[:, :], QSC, ALU.mult)
                p.dma("sync", scr_q[2 * cht, 64:128, qcol], kro[0:64, :],
                      okey="bsq2")
                p.dma("sync", scr_q[2 * cht + 1, 64:128, qcol], kro[64:128, :],
                      okey="bsq3")

    with p.scope():
        Khn = [p.sb(f"b_khn{i}", [64, 4096], BF16) for i in range(2)]
        Kha = [p.sb(f"b_kha{i}", [65, 4096], BF16) for i in range(2)]
        Vh = [p.sb(f"b_vh{i}", [128, 32, 128], BF16) for i in range(2)]
        Qhn = [p.sb(f"b_qhn{i}", [64, 2048], BF16) for i in range(2)]
        Qha = [p.sb(f"b_qha{i}", [65, 2048], BF16) for i in range(2)]
        bmask_t = p.sb("b_bmask", [128, 8, 512], BF16)
        onesb_t = p.sb("b_onesb", [128, 128], BF16)
        sqa = p.sb("b_sqa", [64, 2048], F32)
        sqb = p.sb("b_sqb", [64, 2048], F32)
        ktmp = p.sb("b_ktmp", [1, 8], F32)
        kmaxf = p.sb("b_kmaxf", [1, 1], F32)
        qnt = p.sb("b_qnt", [1, 512], F32)
        brow = p.sb("b_brow", [1, 2048], F32)
        pt = p.sb("b_pt", [128, 512], BF16)
        ptm = p.sb("b_ptm", [128, 512], BF16)
        rinv = p.sb("b_rinv", [1, 512], F32)
        rinvb = p.sb("b_rinvb", [128, 512], F32)
        aout = p.sb("b_aout", [128, 512], BF16)

        ps_s = p.ps("b_pss", [128, 512])
        ps_av = p.ps("b_psav", [128, 512])
        ps_rs = p.ps("b_psrs", [1, 512])

        p.dma("sync", bmask_t[:, :, :],
              io["bmask"].rearrange("r k q -> k r q"))
        p.dma("gpsimd", onesb_t[:, :], io["ones"][:, :])
        for i in range(2):
            p.op("vector", "memset", Kha[i][64:65, :], 1.0,
                 outs=(Kha[i][64:65, :],))

        for h in range(16):
            par = h % 2
            p.dma("sync", Khn[par][0:64, :], scr_k[h, 0:64, :])
            p.dma("sync", Kha[par][0:64, :], scr_k[h, 64:128, :])
            p.dma("sync", Vh[par][:, :, :],
                  scr_v.rearrange("(b p) c -> p b c", p=128)[
                      :, :, h * 128:(h + 1) * 128])
            p.dma("sync", Qhn[par][0:64, :], scr_q[h, 0:64, :])
            p.dma("sync", Qha[par][0:64, :], scr_q[h, 64:128, :])
            p.act(sqa[:, :], Khn[par][0:64, :].rearrange(
                "p (a c) -> p a c", c=2048)[:, 0, :], AF.Square)
            p.act(sqb[:, :], Kha[par][0:64, :].rearrange(
                "p (a c) -> p a c", c=2048)[:, 0, :], AF.Square)
            for c2 in range(4):
                cs_ = slice(c2 * 512, (c2 + 1) * 512)
                p.mm(ps_rs[0:1, :], ones[0:64, 0:1], sqa[:, cs_],
                     start=True, stop=False, skip_group_check=True)
                p.mm(ps_rs[0:1, :], ones[0:64, 0:1], sqb[:, cs_],
                     start=False, stop=True, skip_group_check=True)
                p.red("reduce_max", ktmp[0:1, c2:c2 + 1], ps_rs[0:1, :])
            p.act(sqa[:, :], Khn[par][0:64, :].rearrange(
                "p (a c) -> p a c", c=2048)[:, 1, :], AF.Square)
            p.act(sqb[:, :], Kha[par][0:64, :].rearrange(
                "p (a c) -> p a c", c=2048)[:, 1, :], AF.Square)
            for c2 in range(4):
                cs_ = slice(c2 * 512, (c2 + 1) * 512)
                p.mm(ps_rs[0:1, :], ones[0:64, 0:1], sqa[:, cs_],
                     start=True, stop=False, skip_group_check=True)
                p.mm(ps_rs[0:1, :], ones[0:64, 0:1], sqb[:, cs_],
                     start=False, stop=True, skip_group_check=True)
                p.red("reduce_max", ktmp[0:1, 4 + c2:5 + c2], ps_rs[0:1, :])
            p.red("reduce_max", kmaxf[0:1, 0:1], ktmp[0:1, 0:8])
            p.act(kmaxf[0:1, 0:1], kmaxf[0:1, 0:1], AF.Ln)
            p.act(kmaxf[0:1, 0:1], kmaxf[0:1, 0:1], AF.Exp, scale=0.5)
            p.act(sqa[:, :], Qhn[par][0:64, :], AF.Square)
            p.act(sqb[:, :], Qha[par][0:64, :], AF.Square)
            for c2 in range(4):
                cs_ = slice(c2 * 512, (c2 + 1) * 512)
                p.mm(ps_rs[0:1, :], ones[0:64, 0:1], sqa[:, cs_],
                     start=True, stop=False, skip_group_check=True)
                p.mm(ps_rs[0:1, :], ones[0:64, 0:1], sqb[:, cs_],
                     start=False, stop=True, skip_group_check=True)
                p.act(qnt[0:1, :], ps_rs[0:1, :], AF.Ln)
                p.act(qnt[0:1, :], qnt[0:1, :], AF.Exp, scale=0.5)
                p.ts(qnt[0:1, :], qnt[0:1, :], kmaxf[0:1, 0:1], ALU.mult,
                     s2=-1.01, alu2=ALU.mult)
                p.vcopy(brow[0:1, cs_], qnt[0:1, :])
            p.dma("sync", scr_brow[0:1, :], brow[0:1, :])
            p.dma("gpsimd", Qha[par][64:65, :], scr_brow[0:1, :],
                  okey=(f"bqrow{par}", f"b_qha{par}"))
            for J in range(4):
                qcol = slice(J * 512, (J + 1) * 512)
                nkb = 8 * J + 8
                for kb in range(nkb):
                    kcol = slice(kb * 128, (kb + 1) * 128)
                    p.mm(ps_s[:, :], Khn[par][0:64, kcol],
                         Qhn[par][0:64, qcol], start=True, stop=False)
                    p.mm(ps_s[:, :], Kha[par][:, kcol], Qha[par][:, qcol],
                         start=False, stop=True)
                    r = kb - 8 * J
                    p.act(pt[:, :], ps_s[:, :], AF.Exp)
                    src = pt
                    if r >= 0:
                        p.tt(ptm[:, :], pt[:, :], bmask_t[:, r, :], ALU.mult)
                        src = ptm
                    p.mm(ps_av[:, :], Vh[par][:, kb, :], src[:, :],
                         start=(kb == 0), stop=(kb == nkb - 1),
                         skip_group_check=True)
                    p.mm(ps_rs[0:1, :], onesb_t[:, 0:1], src[:, :],
                         start=(kb == 0), stop=(kb == nkb - 1),
                         skip_group_check=True)
                p.recip(rinv[0:1, :], ps_rs[0:1, :])
                p.mm(ps_s[:, :], ones[0:1, 0:128], rinv[0:1, :])
                p.scopy(rinvb[:, :], ps_s[:, :])
                p.tt(aout[:, :], ps_av[:, :], rinvb[:, :], ALU.mult)
                p.dma("sync", scr_at[h, :, qcol], aout[:, :],
                      okey=(f"bat{J % 2}", "batall"))

    with p.scope():
        att_q = [p.sb(f"b_attq{i}", [128, 512], BF16) for i in range(16)]
        wob = [p.sb(f"b_wob{i}", [128, NDC, 128], BF16) for i in range(2)]
        x1f = [p.sb(f"b_x1f{i}", [128, TQ], F32) for i in range(2)]
        hq = [p.sb(f"b_hq{i}", [128, TQ], F32) for i in range(2)]
        ps_ow = p.ps("b_psow", [128, TQ])
        ow_v = io["owl"]
        for q4 in range(4):
            tq = slice(q4 * TQ, (q4 + 1) * TQ)
            for hc in range(16):
                p.dma("sync", att_q[hc][:, :], scr_at[hc, :, tq],
                      ikey="batall",
                      okey=(f"battq{hc % 4}", f"b_attq{hc}"))
            for dt in range(NDT):
                par = dt % 2
                p.dma("sync", wob[par][:, :, :], ow_v[dt])
                for hc in range(NDC):
                    p.mm(ps_ow[:, :], wob[par][:, hc, :], att_q[hc][:, :],
                         start=(hc == 0), stop=(hc == NDC - 1))
                p.dma("sync", x1f[par][:, :], x1own_v[:, dt, tq])
                p.tt(hq[par][:, :], ps_ow[:, :], x1f[par][:, :], ALU.add)
                p.dma("sync", scr_h.rearrange("(dc p) t -> p dc t", p=128)[
                    :, dt, tq], hq[par][:, :], okey=(f"bh{dt % 3}", "bhall"))

    with p.scope():
        ln2c = p.sb("b_ln2", [128, NDC], F32)
        hf = [p.sb(f"b_hf{i}", [128, 512], F32) for i in range(2)]
        sqm = p.sb("b_sqm", [128, 512], F32)
        msr2 = p.sb("b_msr2", [1, 512], F32)
        rstdb2 = p.sb("b_rstdb2", [128, 512], F32)
        hr = p.sb("b_hr", [128, 512], F32)
        x1g = [p.sb(f"b_x1g{i}", [128, TQ], F32) for i in range(2)]
        ps_nm = R.ps_sum
        ps_bm = R.ps_bc
        p.dma("sync", ln2c[:, :], io["ln2w"][:, :])
        scrh_v = scr_h.rearrange("(dc p) t -> p dc t", p=128)

        cfg = {
            "NEr": 7, "NHT": 8, "nsub": 1, "router": "sigmoid",
            "scaling": 8.0, "gate_w": io["router_w"], "rb_bc": io["rb_bc"],
            "wgl": io["r_wgl"], "wul": io["r_wul"], "wdl": io["r_wdl"],
        }

        def mk_xb(q, xb):
            tq = slice(q * TQ, (q + 1) * TQ)
            for dc in range(NDC):
                par = dc % 2
                p.dma("sync", hf[par][:, :], scrh_v[:, dc, tq], ikey="bhall")
                p.act(sqm[:, :], hf[par][:, :], AF.Square)
                p.mm(ps_nm[0:1, :], ones[:, 0:1], sqm[:, :],
                     start=(dc == 0), stop=(dc == NDC - 1))
            p.ts(msr2[0:1, :], ps_nm[0:1, :], 1.0 / 2048.0, ALU.mult)
            p.act(msr2[0:1, :], msr2[0:1, :], AF.Ln, bias=C["eps"][0:1, 0:1])
            p.act(msr2[0:1, :], msr2[0:1, :], AF.Exp, scale=-0.5)
            p.mm(ps_bm[:, :], ones[0:1, 0:128], msr2[0:1, :])
            p.scopy(rstdb2[:, :], ps_bm[:, :])
            for dc in range(NDC):
                par = dc % 2
                p.dma("sync", hf[par][:, :], scrh_v[:, dc, tq], ikey="bhall")
                p.tt(hr[:, :], hf[par][:, :], rstdb2[:, :], ALU.mult)
                p.ts(xb[:, dc, :], hr[:, :], ln2c[:, dc:dc + 1], ALU.mult)

        def resid(q, dt, x4, ps):
            tq = slice(q * TQ, (q + 1) * TQ)
            par = dt % 2
            p.dma("sync", x1g[par][:, :],
                  x1own_v[:, dt, slice(q * TQ, (q + 1) * TQ)])
            p.dma("sync", hf[par][:, :], scrh_v[:, dt, tq], ikey="bhall")
            p.tt(x4[:, :], ps[:, :], x1g[par][:, :], ALU.add)
            p.tt(x4[:, :], x4[:, :], hf[par][:, :], ALU.add)

        emit_moe(p, C, cfg, mk_xb, resid, out_v, True, "bm")

    p.end_waits()


def _rope_tables():
    inv = 1.0 / 10000.0 ** (np.arange(0, RD2, 2, dtype=np.float64) / RD2)
    ang = np.outer(np.arange(4096, dtype=np.float64), inv)
    emb = np.concatenate([ang, ang], axis=-1)
    cos = np.tile(np.cos(emb).T, (2, 1))
    sin = np.tile(np.sin(emb).T, (2, 1))
    return cos.astype(BF), sin.astype(BF)


def _pswapT():
    P = np.zeros((128, 128), dtype=np.float32)
    for base in (0, 64):
        for d in range(32):
            P[base + d, base + d + 32] = -1.0
            P[base + d + 32, base + d] = 1.0
    return np.ascontiguousarray(P.T).astype(BF)


def _masks_b(core_i):
    bm = np.zeros((8, 128, 512), dtype=np.float32)
    kk = np.arange(128)[:, None]
    qq = np.arange(128)[None, :]
    for r in range(8):
        for sblk in range(4):
            rel = 2 * sblk + core_i
            if r < rel:
                bm[r, :, sblk * 128:(sblk + 1) * 128] = 1.0
            elif r == rel:
                bm[r, :, sblk * 128:(sblk + 1) * 128] = (kk <= qq)
    return bm.astype(BF)


def build_nc_phase_b():
    nc = bass.Bass("TRN2", target_bir_lowering=False, debug=False)
    io = {}

    def inp(name, shape, dt=F32):
        io[name] = nc.dram_tensor(name, shape, dt, kind="ExternalInput").ap()
    inp("x1T", [2048, 4096])
    inp("x1ownT", [2048, 2048])
    inp("ln1w", [128, 16])
    inp("ln2w", [128, 16])
    inp("kvdw", [2048, 512], BF16)
    inp("qdw", [2048, 512], BF16)
    inp("kuw", [512, 1024], BF16)
    inp("vuw", [512, 2048], BF16)
    inp("quw", [512, 1024], BF16)
    inp("ropekl", [8, 128, 16, 128], BF16)
    inp("ropeqw", [512, 1024], BF16)
    inp("owl", [16, 128, 16, 128], BF16)
    inp("cs_cos", [128, 4096], BF16)
    inp("cs_sin", [128, 4096], BF16)
    inp("cs_cos_own", [128, 2048], BF16)
    inp("cs_sin_own", [128, 2048], BF16)
    inp("pswapT", [128, 128], BF16)
    inp("bmask", [8, 128, 512], BF16)
    inp("router_w", [2048, 7], BF16)
    inp("rb_bc", [128, 7])
    inp("r_wgl", [8, 8, 128, 16, 128], BF16)
    inp("r_wul", [8, 8, 128, 16, 128], BF16)
    inp("r_wdl", [8, 16, 128, 8, 128], BF16)
    inp("identity", [128, 128])
    inp("ones", [128, 128])
    inp("epsln", [1, 1])
    io["x2"] = nc.dram_tensor("x2", [2048, 2048], F32, kind="ExternalOutput").ap()
    with ExitStack() as stack:
        p = Prog(nc, stack)
        C = emit_consts(p, io)
        C["ln"] = LNRes(p, "ln", C)
        emit_phase_b(p, io, C)
        p.install()
    return nc


_B_MOE_TILED = None


def _phase_b_prep(weights):
    global _B_MOE_TILED
    if _B_MOE_TILED is None:
        _B_MOE_TILED = _moe_pretile(
            weights["L0_ffn_r_gate"], weights["L0_ffn_r_up"],
            weights["L0_ffn_r_down"], weights["L0_ffn_sh_gate"],
            weights["L0_ffn_sh_up"], weights["L0_ffn_sh_down"], 8)


def _phase_b_core_inputs(x1, weights, b, i):
    _phase_b_prep(weights)
    cos, sin = _rope_tables()
    x1b = x1[b]
    own = x1b.reshape(32, 128, 2048)[i::2].reshape(2048, 2048)
    colsel = np.arange(4096).reshape(32, 128)[i::2].reshape(-1)
    return {
        "x1T": np.ascontiguousarray(x1b.T),
        "x1ownT": np.ascontiguousarray(own.T),
        "ln1w": np.ascontiguousarray(
            weights["L0_ffn_ln1_w"].reshape(16, 128).T),
        "ln2w": np.ascontiguousarray(
            weights["L0_ffn_ln2_w"].reshape(16, 128).T),
        "kvdw": weights["L0_ffn_kv_d_w"].astype(BF),
        "qdw": weights["L0_ffn_q_d_w"].astype(BF),
        "kuw": weights["L0_ffn_k_u_w"].astype(BF),
        "vuw": weights["L0_ffn_v_u_w"].astype(BF),
        "quw": weights["L0_ffn_q_u_w"].astype(BF),
        "ropekl": _tile_gu(weights["L0_ffn_rope_k_w"]).astype(BF),
        "ropeqw": weights["L0_ffn_rope_q_w"].astype(BF),
        "owl": _tile_d(weights["L0_ffn_o_w"]).astype(BF),
        "cs_cos": cos, "cs_sin": sin,
        "cs_cos_own": np.ascontiguousarray(cos[:, colsel]),
        "cs_sin_own": np.ascontiguousarray(sin[:, colsel]),
        "pswapT": _pswapT(),
        "bmask": _masks_b(i),
        "router_w": weights["L0_ffn_router_w"].astype(BF),
        "rb_bc": np.tile(weights["L0_ffn_router_b"][None, :], (128, 1)),
        "r_wgl": _B_MOE_TILED[0].astype(BF),
        "r_wul": _B_MOE_TILED[1].astype(BF),
        "r_wdl": _B_MOE_TILED[2].astype(BF),
        **_consts(),
    }


def run_phase_b(x1, weights, core_ids=None):
    if core_ids is None:
        core_ids = list(range(8))
    nc = build_nc_phase_b()
    in_maps = [
        _phase_b_core_inputs(x1, weights, c // 2, c % 2) for c in core_ids]
    res = run_bass_kernel_spmd(nc, in_maps, core_ids=core_ids)
    x2 = np.zeros((4, 4096, 2048), dtype=np.float32)
    for j, c in enumerate(core_ids):
        b, i = c // 2, c % 2
        loc = res.results[j]["x2"].T.reshape(16, 128, 2048)
        x2[b].reshape(32, 128, 2048)[i::2] = loc
    return x2



def emit_phase_a2(p: Prog, io: dict, C):
    R = C["ln"]
    og_v = io["ogT"].rearrange("(hc p) t -> p hc t", p=128)
    x0_v = io["x0ownT"].rearrange("(dc p) t -> p dc t", p=128)
    out_v = io["x1own"].rearrange("(dc p) t -> p dc t", p=128)
    wo_v = io["woAl"]

    with p.scope():
        att_q = [p.sb(f"a2_og{i}", [128, 512], BF16) for i in range(16)]
        wob = [p.sb(f"a2_wob{i}", [128, NDC, 128], BF16) for i in range(2)]
        x0f = [p.sb(f"a2_x0f{i}", [128, TQ], F32) for i in range(2)]
        ps_ow = p.ps("a2_psow", [128, TQ])
        for q4 in range(4):
            tq = slice(q4 * TQ, (q4 + 1) * TQ)
            for hc in range(16):
                p.dma("sync", att_q[hc][:, :], og_v[:, hc, tq],
                      okey=(f"a2q{hc % 4}", f"a2_og{hc}"))

            def produce(dt, x4):
                par = dt % 2
                p.dma("sync", wob[par][:, :, :], wo_v[dt])
                for hc in range(NDC):
                    p.mm(ps_ow[:, :], wob[par][:, hc, :], att_q[hc][:, :],
                         start=(hc == 0), stop=(hc == NDC - 1))
                p.dma("sync", x0f[par][:, :], x0_v[:, dt, tq])
                p.tt(x4[:, :], ps_ow[:, :], x0f[par][:, :], ALU.add)

            emit_postln(p, R, produce, out_v, tq, qtag="a2")
    p.end_waits()


def build_nc_phase_a2():
    nc = bass.Bass("TRN2", target_bir_lowering=False, debug=False)
    io = {}
    io["ogT"] = nc.dram_tensor("ogT", [2048, 2048], BF16, kind="ExternalInput").ap()
    io["x0ownT"] = nc.dram_tensor("x0ownT", [2048, 2048], F32, kind="ExternalInput").ap()
    io["woAl"] = nc.dram_tensor("woAl", [16, 128, 16, 128], BF16, kind="ExternalInput").ap()
    io["identity"] = nc.dram_tensor("identity", [128, 128], F32, kind="ExternalInput").ap()
    io["ones"] = nc.dram_tensor("ones", [128, 128], F32, kind="ExternalInput").ap()
    io["epsln"] = nc.dram_tensor("epsln", [1, 1], F32, kind="ExternalInput").ap()
    io["x1own"] = nc.dram_tensor("x1own", [2048, 2048], F32, kind="ExternalOutput").ap()
    with ExitStack() as stack:
        p = Prog(nc, stack)
        C = emit_consts(p, io)
        C["ln"] = LNRes(p, "ln", C)
        emit_phase_a2(p, io, C)
        p.install()
    return nc



CA = 32
NCH = 128


def emit_phase_a1(p: Prog, io: dict, C):
    ones = C["ones"]
    x0_v = io["x0bT"].rearrange("(dc p) t -> p dc t", p=128)

    scr_q = p.dram("a_scrq", [8, 128, 4096], BF16)
    scr_k = p.dram("a_scrk", [8, 4, 128, 4096], BF16)
    scr_w = p.dram("a_scrw", [8, 4, 128, 4096], BF16)
    scr_v = p.dram("a_scrv", [8, 4, 128, 4096], BF16)
    scr_A = p.dram("a_scrA", [8, 128, 4096], F32)
    scr_g = p.dram("a_scrg", [8, 128, 4096], BF16)
    scr_b = p.dram("a_scrb", [32, 4096], F32)
    scr_o = p.dram("a_scro", [8, 128, 4096], F32)

    f1b = p.sb("a_f1b", [128, 4096], BF16)
    g1b = p.sb("a_g1b", [128, 4096], BF16)
    with p.scope():
        xb = p.sb("a_xb", [128, NDC, 512], BF16)
        wbs_t = p.sb("a_wbs", [128, NDC, 32], BF16)
        wf1_t = p.sb("a_wf1", [128, NDC, 128], BF16)
        wg1_t = p.sb("a_wg1", [128, NDC, 128], BF16)
        bg1c = p.sb("a_bg1", [128, 1], F32)
        bev = p.sb("a_bev", [32, 512], F32)
        ps_a = p.ps("a1_psa", [128, 512])
        ps_b = p.ps("a1_psb", [32, 512])
        p.dma("sync", wbs_t[:, :, :],
              io["wbs"].rearrange("(dc p) c -> p dc c", p=128))
        p.dma("sync", wf1_t[:, :, :],
              io["wf1T"].rearrange("(dc p) c -> p dc c", p=128))
        p.dma("sync", wg1_t[:, :, :],
              io["wg1T"].rearrange("(dc p) c -> p dc c", p=128))
        p.dma("sync", bg1c[:, :], io["bg1c"][:, :])
        for tt in range(8):
            tcol = slice(tt * 512, (tt + 1) * 512)
            p.dma("sync", xb[:, :, :], x0_v[:, :, tcol])
            for dc in range(NDC):
                p.mm(ps_b[:, :], wbs_t[:, dc, :], xb[:, dc, :],
                     start=(dc == 0), stop=(dc == NDC - 1))
            p.act(bev[:, :], ps_b[:, :], AF.Sigmoid)
            p.dma("sync", scr_b[:, tcol], bev[:, :], okey="s1b")
            for dc in range(NDC):
                p.mm(ps_a[:, :], wf1_t[:, dc, :], xb[:, dc, :],
                     start=(dc == 0), stop=(dc == NDC - 1))
            p.scopy(f1b[:, tcol], ps_a[:, :])
            for dc in range(NDC):
                p.mm(ps_a[:, :], wg1_t[:, dc, :], xb[:, dc, :],
                     start=(dc == 0), stop=(dc == NDC - 1))
            p.ts(g1b[:, tcol], ps_a[:, :], bg1c[:, 0:1], ALU.add)

    with p.scope():
        wf2_t = p.sb("a_wf2", [128, 8, 128], BF16)
        wg2_t = p.sb("a_wg2", [128, 8, 128], BF16)
        dtb_t = p.sb("a_dtb", [128, 8], F32)
        bg2_t = p.sb("a_bg2", [128, 8], F32)
        alog_t = p.sb("a_alog", [1, 8], F32)
        nega_t = p.sb("a_nega", [1, 8], F32)
        negab = p.sb("a_negab", [128, 8], F32)
        u_t = p.sb("a_u", [128, 512], F32)
        a_t = p.sb("a_a", [128, 512], F32)
        e_t = p.sb("a_e", [128, 512], F32)
        r_t = p.sb("a_r", [128, 512], F32)
        sp_t = p.sb("a_sp", [128, 512], F32)
        la_t = [p.sb(f"a_la{i}", [128, 512], F32) for i in range(2)]
        gev = p.sb("a_gev", [128, 512], BF16)
        ps_a = p.ps("a2_psa", [128, 512])
        ps_c = p.ps("a2_psc", [128, 8])
        p.dma("sync", wf2_t[:, :, :],
              io["wf2s"].rearrange("p (h c) -> p h c", c=128))
        p.dma("sync", wg2_t[:, :, :],
              io["wg2s"].rearrange("p (h c) -> p h c", c=128))
        p.dma("sync", dtb_t[:, :], io["dtb"][:, :])
        p.dma("sync", bg2_t[:, :], io["bg2c"][:, :])
        p.dma("sync", alog_t[:, :], io["alog"][:, :])
        p.act(nega_t[0:1, :], alog_t[0:1, :], AF.Exp, scale=1.0)
        p.ts(nega_t[0:1, :], nega_t[0:1, :], -1.0, ALU.mult)
        p.mm(ps_c[:, :], ones[0:1, 0:128], nega_t[0:1, :])
        p.vcopy(negab[:, :], ps_c[:, :])
        for ht in range(8):
            for tt in range(8):
                tcol = slice(tt * 512, (tt + 1) * 512)
                p.mm(ps_a[:, :], wf2_t[:, ht, :], f1b[:, tcol])
                p.ts(u_t[:, :], ps_a[:, :], dtb_t[:, ht:ht + 1], ALU.add)
                p.op("scalar", "activation", a_t[:, :], u_t[:, :], AF.Abs,
                     outs=(a_t[:, :],), ins=(u_t[:, :],))
                p.act(e_t[:, :], a_t[:, :], AF.Exp, scale=-1.0)
                p.act(e_t[:, :], e_t[:, :], AF.Ln, bias=ones[:, 0:1])
                p.op("vector", "tensor_relu", r_t[:, :], u_t[:, :],
                     outs=(r_t[:, :],), ins=(u_t[:, :],))
                p.tt(sp_t[:, :], r_t[:, :], e_t[:, :], ALU.add)
                p.ts(la_t[0][:, :], sp_t[:, :], negab[:, ht:ht + 1], ALU.mult)
                src, dst = la_t[0], la_t[1]
                for sh in (1, 2, 4, 8, 16):
                    sv = src[:, :].rearrange("p (c t) -> p c t", t=32)
                    dv = dst[:, :].rearrange("p (c t) -> p c t", t=32)
                    p.tt(dv[:, :, sh:32], sv[:, :, sh:32], sv[:, :, 0:32 - sh],
                         ALU.add)
                    p.vcopy(dv[:, :, 0:sh], sv[:, :, 0:sh], eng="gpsimd")
                    src, dst = dst, src
                p.dma("sync", scr_A[ht, :, tcol], src[:, :],
                      okey=f"s2A{tt % 2}")
                p.mm(ps_a[:, :], wg2_t[:, ht, :], g1b[:, tcol])
                p.ts(gev[:, :], ps_a[:, :], bg2_t[:, ht:ht + 1], ALU.add)
                p.dma("sync", scr_g[ht, :, tcol], gev[:, :],
                      okey=f"s2g{tt % 2}")

    with p.scope():
        xb = p.sb("a3_xb", [128, NDC, 512], BF16)
        wsl = [p.sb(f"a3_w{g}", [128, NDC, 128], BF16) for g in range(18)]
        cw_t = p.sb("a3_cw", [128, 72, 4], F32)
        tails = p.sb("a3_tails", [128, 72, 3], F32)
        zf = p.sb("a3_zf", [128, 515], F32)
        c1 = p.sb("a3_c1", [128, 512], F32)
        outb = [p.sb(f"a3_outb{i}", [128, 512], BF16) for i in range(2)]
        ps_a = p.ps("a3_psa", [128, 512])
        p.dma("sync", cw_t[:, :, :], io["convw"][:, :, :])
        p.op("vector", "memset", tails[:, :, :], 0.0,
             outs=(tails[:, :, :],))

        def cht_kind(cht):
            if cht < 8:
                return "q", None, cht
            if cht < 40:
                return "k", None, cht - 8
            return "v", None, cht - 40

        for grp in range(4):
            chts = list(range(grp * 18, (grp + 1) * 18))
            for gi, cht in enumerate(chts):
                p.dma("sync", wsl[gi][:, :, :], io["wqkvl"][cht])
            for tt in range(8):
                tcol = slice(tt * 512, (tt + 1) * 512)
                p.dma("sync", xb[:, :, :], x0_v[:, :, tcol])
                for gi, cht in enumerate(chts):
                    kind, wsrc, off = cht_kind(cht)
                    for dc in range(NDC):
                        p.mm(ps_a[:, :], wsl[gi][:, dc, :], xb[:, dc, :],
                             start=(dc == 0), stop=(dc == NDC - 1))
                    p.vcopy(zf[:, 0:3], tails[:, cht, :])
                    p.scopy(zf[:, 3:515], ps_a[:, :])
                    p.vcopy(tails[:, cht, :], zf[:, 512:515], eng="gpsimd")
                    p.ts(c1[:, :], zf[:, 0:512], cw_t[:, cht, 0:1], ALU.mult)
                    p.stt(c1[:, :], zf[:, 1:513], cw_t[:, cht, 1:2],
                          c1[:, :], ALU.mult, ALU.add)
                    p.stt(c1[:, :], zf[:, 2:514], cw_t[:, cht, 2:3],
                          c1[:, :], ALU.mult, ALU.add)
                    p.stt(c1[:, :], zf[:, 3:515], cw_t[:, cht, 3:4],
                          c1[:, :], ALU.mult, ALU.add)
                    opar = (gi + tt) % 2
                    p.act(outb[opar][:, :], c1[:, :], AF.Silu)
                    if kind == "v":
                        h8, r4 = off // 4, off % 4
                        dst = scr_v[h8, r4, :, tcol]
                    elif kind == "q":
                        dst = scr_q[off, :, tcol]
                    else:
                        h8, r4 = off // 4, off % 4
                        dst = scr_k[h8, r4, :, tcol]
                    p.dma("sync", dst, outb[opar][:, :],
                          okey=f"s3o{opar}")
    with p.scope():
        inb = [p.sb(f"an_in{i}", [128, 512], BF16) for i in range(2)]
        sq2 = [p.sb(f"an_sq{i}", [128, 512], F32) for i in range(2)]
        nrm = [p.sb(f"an_nrm{i}", [1, 512], F32) for i in range(2)]
        nrb = [p.sb(f"an_nrb{i}", [128, 512], F32) for i in range(2)]
        onb = [p.sb(f"an_out{i}", [128, 512], BF16) for i in range(2)]
        brow = [p.sb(f"an_brow{i}", [1, 512], F32) for i in range(2)]
        bbc = [p.sb(f"an_bbc{i}", [128, 512], F32) for i in range(2)]
        wev = [p.sb(f"an_wev{i}", [128, 512], BF16) for i in range(2)]
        ps_n = [p.ps(f"an_psn{i}", [1, 512]) for i in range(2)]
        ps_b = [p.ps(f"an_psb{i}", [128, 512]) for i in range(2)]
        for cht in range(40):
            kind = "q" if cht < 8 else "k"
            off = cht if cht < 8 else cht - 8
            for tt in range(8):
                par = (cht * 8 + tt) % 2
                tcol = slice(tt * 512, (tt + 1) * 512)
                if kind == "q":
                    src = scr_q[off, :, tcol]
                else:
                    h8, r4 = off // 4, off % 4
                    src = scr_k[h8, r4, :, tcol]
                p.dma("sync", inb[par][:, :], src, ikey=f"anin{par}")
                p.act(sq2[par][:, :], inb[par][:, :], AF.Square)
                p.mm(ps_n[par][0:1, :], ones[:, 0:1], sq2[par][:, :])
                p.act(nrm[par][0:1, :], ps_n[par][0:1, :], AF.Ln)
                p.act(nrm[par][0:1, :], nrm[par][0:1, :], AF.Exp, scale=0.5)
                p.ts(nrm[par][0:1, :], nrm[par][0:1, :], 1e-12, ALU.max)
                p.recip(nrm[par][0:1, :], nrm[par][0:1, :])
                if kind == "q":
                    p.ts(nrm[par][0:1, :], nrm[par][0:1, :],
                         float(1.0 / np.sqrt(128.0)), ALU.mult)
                p.mm(ps_b[par][:, :], ones[0:1, 0:128], nrm[par][0:1, :])
                p.scopy(nrb[par][:, :], ps_b[par][:, :])
                p.tt(onb[par][:, :], inb[par][:, :], nrb[par][:, :],
                     ALU.mult)
                p.dma("scalar", src, onb[par][:, :], okey=f"anon{par}")
                if kind == "k":
                    p.dma("sync", brow[par][0:1, :],
                          scr_b[off:off + 1, tcol], ikey=f"anb{par}")
                    p.mm(ps_b[par][:, :], ones[0:1, 0:128],
                         brow[par][0:1, :])
                    p.vcopy(bbc[par][:, :], ps_b[par][:, :])
                    p.tt(wev[par][:, :], onb[par][:, :], bbc[par][:, :],
                         ALU.mult)
                    p.dma("scalar", scr_w[h8, r4, :, tcol], wev[par][:, :],
                          okey=f"anw{par}")
    return dict(scr_q=scr_q, scr_k=scr_k, scr_w=scr_w, scr_v=scr_v,
                scr_A=scr_A, scr_g=scr_g, scr_o=scr_o)


def emit_phase_a1_mkda(p: Prog, io: dict, C, scr, nch=NCH):
    ones = C["ones"]
    scr_q, scr_k, scr_w = scr["scr_q"], scr["scr_k"], scr["scr_w"]
    scr_v, scr_A, scr_o = scr["scr_v"], scr["scr_A"], scr["scr_o"]

    with p.scope():
        q8 = [p.sb(f"am_q{i}", [128, 8, 32], BF16) for i in range(2)]
        k8 = [p.sb(f"am_k{i}", [128, 8, 4, 32], BF16) for i in range(2)]
        w8 = [p.sb(f"am_w{i}", [128, 8, 4, 32], BF16) for i in range(2)]
        vT8 = [p.sb(f"am_v{i}", [128, 8, 4, 32], BF16) for i in range(2)]
        A8 = [p.sb(f"am_A{i}", [128, 8, 32], F32) for i in range(2)]
        Arel = p.sb("am_Arel", [128, 8, 32], F32)
        EK = p.sb("am_EK", [128, 8, 32], F32)
        EW = p.sb("am_EW", [128, 8, 32], F32)
        eAc = p.sb("am_eAc", [128, 8], F32)
        eA31 = p.sb("am_eA31", [128, 8], F32)
        f31 = p.sb("am_f31", [128, 8], F32)
        Khat = p.sb("am_Khat", [128, 8, 128], BF16)
        What = p.sb("am_What", [128, 8, 128], BF16)
        kexp = p.sb("am_kexp", [128, 8, 128], BF16)
        kdec = p.sb("am_kdec", [128, 8, 128], BF16)
        qhc = p.sb("am_qhc", [128, 8, 32], BF16)
        qhf = p.sb("am_qhf", [128, 8, 32], BF16)
        Nt = p.sb("am_Nt", [128, 8, 128], BF16)
        vS = p.sb("am_vS", [128, 8, 128], BF16)
        kdT = p.sb("am_kdT", [128, 8, 128], BF16)
        attnT = p.sb("am_attnT", [128, 8, 32], BF16)
        Tp = [p.sb(f"am_T{l}", [128, 8, 128], BF16) for l in range(7)]
        Up = [p.sb(f"am_U{i}", [128, 8, 128], BF16) for i in range(2)]
        yv = [p.sb(f"am_y{i}", [128, 8, 128], BF16) for i in range(2)]
        Sst = p.sb("am_S", [128, 8, 128], F32)
        Sbf = p.sb("am_Sbf", [128, 8, 128], BF16)
        Sdec = p.sb("am_Sdec", [128, 8, 128], F32)
        o8 = p.sb("am_o8", [128, 8, 32], F32)
        mmask = p.sb("am_mmask", [128, 128], BF16)
        amask = p.sb("am_amask", [128, 32], BF16)

        psf = [p.ps(f"am_psf{i}", [128, 512]) for i in range(4)]
        psb = [p.ps(f"am_psb{i}", [128, 1024], BF16) for i in range(2)]
        ps_o = p.ps("am_pso", [128, 256])

        p.dma("sync", mmask[:, :], io["mmask"][:, :])
        p.dma("sync", amask[:, :], io["amask"][:, :])
        p.op("vector", "memset", Sst[:, :, :], 0.0, outs=(Sst[:, :, :],))
        p.op("gpsimd", "memset", Sbf[:, :, :], 0.0, outs=(Sbf[:, :, :],))

        def h4(j):
            return slice(j * 4, (j + 1) * 4)

        for n in range(nch):
            par = n % 2
            tcol = slice(n * CA, (n + 1) * CA)
            p.dma("sync", q8[par][:, :, :],
                  scr_q.rearrange("h p t -> p h t")[:, :, tcol])
            p.dma("sync", k8[par][:, :, :, :],
                  scr_k.rearrange("h i p t -> p h i t")[:, :, :, tcol])
            p.dma("sync", w8[par][:, :, :, :],
                  scr_w.rearrange("h i p t -> p h i t")[:, :, :, tcol])
            p.dma("sync", vT8[par][:, :, :, :],
                  scr_v.rearrange("h i p t -> p h i t")[:, :, :, tcol])
            p.dma("sync", A8[par][:, :, :],
                  scr_A.rearrange("h p t -> p h t")[:, :, tcol])
            A = A8[par]
            kv = k8[par].rearrange("p h i t -> p h (i t)")
            wv = w8[par].rearrange("p h i t -> p h (i t)")

            p.tt(Arel[:, :, :], A[:, :, :],
                 A[:, :, 15:16].broadcast_to([128, 8, 32]), ALU.subtract)
            p.ts(Arel[:, :, :], Arel[:, :, :], 85.0, ALU.min)
            p.ts(Arel[:, :, :], Arel[:, :, :], -85.0, ALU.max)
            p.act(EK[:, :, :], Arel[:, :, :], AF.Exp)
            p.act(EW[:, :, :], Arel[:, :, :], AF.Exp, scale=-1.0)
            a15 = A[:, :, 15:16].rearrange("p h o -> p (h o)")
            a31 = A[:, :, 31:32].rearrange("p h o -> p (h o)")
            p.act(eAc[:, :], a15, AF.Exp)
            p.act(eA31[:, :], a31, AF.Exp)
            p.tt(f31[:, :], a31, a15, ALU.subtract)
            p.act(f31[:, :], f31[:, :], AF.Exp)
            ek4 = EK[:, :, :].unsqueeze(2).broadcast_to([128, 8, 4, 32])
            ew4 = EW[:, :, :].unsqueeze(2).broadcast_to([128, 8, 4, 32])
            kh4 = Khat.rearrange("p h (i t) -> p h i t", t=32)
            wh4 = What.rearrange("p h (i t) -> p h i t", t=32)
            p.tt(kh4, k8[par][:, :, :, :], ek4, ALU.mult)
            p.tt(wh4, w8[par][:, :, :, :], ew4, ALU.mult)
            p.tt(kexp[:, :, :], Khat[:, :, :],
                 eAc[:, :].unsqueeze(2).broadcast_to([128, 8, 128]), ALU.mult)
            p.tt(kdec[:, :, :], What[:, :, :],
                 f31[:, :].unsqueeze(2).broadcast_to([128, 8, 128]), ALU.mult,
                 eng="gpsimd")
            p.tt(qhc[:, :, :], q8[par][:, :, :], EK[:, :, :], ALU.mult)
            p.tt(qhf[:, :, :], qhc[:, :, :],
                 eAc[:, :].unsqueeze(2).broadcast_to([128, 8, 32]), ALU.mult,
                 eng="gpsimd")

            for h in range(8):
                p.mm(psf[h // 4][:, (h % 4) * 128:(h % 4 + 1) * 128],
                     Khat[:, h, :], What[:, h, :])
            for j in range(2):
                p.tt(Nt[:, j * 4:(j + 1) * 4, :],
                     psf[j][:, :].rearrange("p (a c) -> p a c", c=128),
                     mmask[:, :].unsqueeze(1).broadcast_to([128, 4, 128]),
                     ALU.mult)
            for h in range(8):
                p.trans(psb[h // 4][:, (h % 4) * 128:(h % 4 + 1) * 128],
                        Nt[:, h, :], C["identb"][:, :])
            for j in range(2):
                p.scopy(Tp[0].rearrange("p h c -> p (h c)")[
                    :, j * 512:(j + 1) * 512], psb[j][:, 0:512])
            for h in range(8):
                p.trans(psb[h // 4][:, (h % 4) * 128:(h % 4 + 1) * 128],
                        vT8[par].rearrange("p h i t -> p h (i t)")[:, h, :],
                        C["identb"][:, :])
            for j in range(2):
                p.scopy(vS.rearrange("p h c -> p (h c)")[
                    :, j * 512:(j + 1) * 512], psb[j][:, 0:512])
            for h in range(8):
                p.mm(psf[h // 4][:, (h % 4) * 128:(h % 4 + 1) * 128],
                     kexp[:, h, :], Sbf[:, h, :])
            for j in range(2):
                p.tt(yv[0].rearrange("p h c -> p (h c)")[
                    :, j * 512:(j + 1) * 512],
                    vS.rearrange("p h c -> p (h c)")[:, j * 512:(j + 1) * 512],
                    psf[j][:, :], ALU.subtract)

            for lvl in range(6):
                ucur = Up[lvl % 2] if lvl > 0 else Nt
                tcur = Tp[lvl]
                unxt = Up[(lvl + 1) % 2]
                for h in range(8):
                    p.mm(psf[h // 4][:, (h % 4) * 128:(h % 4 + 1) * 128],
                         tcur[:, h, :], ucur[:, h, :])
                for h in range(8):
                    p.mm(psf[2 + h // 4][:, (h % 4) * 128:(h % 4 + 1) * 128],
                         ucur[:, h, :], tcur[:, h, :])
                for j in range(2):
                    p.scopy(unxt.rearrange("p h c -> p (h c)")[
                        :, j * 512:(j + 1) * 512], psf[j][:, :])
                for j in range(2):
                    p.scopy(Tp[lvl + 1].rearrange("p h c -> p (h c)")[
                        :, j * 512:(j + 1) * 512], psf[2 + j][:, :])
            ycur, ynxt = yv[0], yv[1]
            for lvl in range(6, -1, -1):
                for h in range(8):
                    p.mm(psf[h // 4][:, (h % 4) * 128:(h % 4 + 1) * 128],
                         Tp[lvl][:, h, :], ycur[:, h, :])
                alu = ALU.add if lvl > 0 else ALU.subtract
                for j in range(2):
                    p.tt(ynxt.rearrange("p h c -> p (h c)")[
                        :, j * 512:(j + 1) * 512],
                        ycur.rearrange("p h c -> p (h c)")[
                            :, j * 512:(j + 1) * 512],
                        psf[j][:, :], alu)
                ycur, ynxt = ynxt, ycur
            u_f = ycur

            for h in range(8):
                p.mm(psf[2 + h // 4][:, (h % 4) * 32:(h % 4 + 1) * 32],
                     What[:, h, :], qhc[:, h, :])
            for j in range(2):
                p.tt(attnT[:, j * 4:(j + 1) * 4, :],
                    psf[2 + j][:, 0:128].rearrange("p (a c) -> p a c", c=32),
                    amask[:, :].unsqueeze(1).broadcast_to([128, 4, 32]),
                    ALU.mult)
            for h in range(8):
                p.mm(ps_o[:, h * 32:(h + 1) * 32], Sbf[:, h, :],
                     qhf[:, h, :], start=True, stop=False,
                     skip_group_check=True)
                p.mm(ps_o[:, h * 32:(h + 1) * 32], u_f[:, h, :],
                     attnT[:, h, :], start=False, stop=True,
                     skip_group_check=True)
            p.vcopy(o8[:, :, :].rearrange("p h t -> p (h t)"), ps_o[:, :])
            p.dma("sync", scr_o.rearrange("h p t -> p h t")[:, :, tcol],
                  o8[:, :, :], okey=f"amo{n % 3}")

            for h in range(8):
                p.trans(psb[h // 4][:, (h % 4) * 128:(h % 4 + 1) * 128],
                        kdec[:, h, :], C["identb"][:, :])
            for j in range(2):
                p.scopy(kdT.rearrange("p h c -> p (h c)")[
                    :, j * 512:(j + 1) * 512], psb[j][:, 0:512])
            for h in range(8):
                p.mm(psf[h // 4][:, (h % 4) * 128:(h % 4 + 1) * 128],
                     kdT[:, h, :], u_f[:, h, :])
            p.tt(Sdec[:, :, :], Sst[:, :, :],
                 eA31[:, :].unsqueeze(2).broadcast_to([128, 8, 128]),
                 ALU.mult)
            for j in range(2):
                p.tt(Sst.rearrange("p h c -> p (h c)")[
                    :, j * 512:(j + 1) * 512],
                    Sdec.rearrange("p h c -> p (h c)")[
                        :, j * 512:(j + 1) * 512],
                    psf[j][:, :], ALU.add)
            p.scopy(Sbf.rearrange("p h c -> p (h c)"),
                    Sst.rearrange("p h c -> p (h c)"))


def emit_phase_a1_gate(p: Prog, io: dict, C, scr):
    ones = C["ones"]
    scr_o, scr_g = scr["scr_o"], scr["scr_g"]
    og_v = io["og"].rearrange("(h p) t -> p h t", p=128)
    with p.scope():
        of = [p.sb(f"ag_of{i}", [128, 512], F32) for i in range(2)]
        gf = [p.sb(f"ag_gf{i}", [128, 512], BF16) for i in range(2)]
        sq2 = [p.sb(f"ag_sq{i}", [128, 512], F32) for i in range(2)]
        ms = [p.sb(f"ag_ms{i}", [1, 512], F32) for i in range(2)]
        rsb = [p.sb(f"ag_rsb{i}", [128, 512], F32) for i in range(2)]
        sg_all = p.sb("ag_sgall", [128, 8, 4096], BF16)
        onc = p.sb("ag_onc", [128, 1], F32)
        ogo = [p.sb(f"ag_ogo{i}", [128, 512], BF16) for i in range(2)]
        ps_n = [p.ps(f"ag_psn{i}", [1, 512]) for i in range(2)]
        ps_b = [p.ps(f"ag_psb{i}", [128, 512]) for i in range(2)]
        p.dma("sync", onc[:, :], io["onormc"][:, :])
        for h in range(8):
            for tt in range(8):
                par = tt % 2
                tcol = slice(tt * 512, (tt + 1) * 512)
                p.dma("sync", gf[par][:, :], scr_g[h, :, tcol],
                      ikey=f"agg{par}")
                p.act(sg_all[:, h, tcol], gf[par][:, :], AF.Sigmoid)
        for h in range(8):
            for tt in range(8):
                par = tt % 2
                tcol = slice(tt * 512, (tt + 1) * 512)
                p.dma("sync", of[par][:, :], scr_o[h, :, tcol], ikey="amoall")
                p.act(sq2[par][:, :], of[par][:, :], AF.Square)
                p.mm(ps_n[par][0:1, :], ones[:, 0:1], sq2[par][:, :])
                p.ts(ms[par][0:1, :], ps_n[par][0:1, :], 1.0 / 128.0,
                     ALU.mult)
                p.act(ms[par][0:1, :], ms[par][0:1, :], AF.Ln,
                      bias=C["eps"][0:1, 0:1])
                p.act(ms[par][0:1, :], ms[par][0:1, :], AF.Exp, scale=-0.5)
                p.mm(ps_b[par][:, :], ones[0:1, 0:128], ms[par][0:1, :])
                p.scopy(rsb[par][:, :], ps_b[par][:, :])
                p.tt(ogo[par][:, :], of[par][:, :], rsb[par][:, :], ALU.mult)
                p.ts(ogo[par][:, :], ogo[par][:, :], onc[:, 0:1], ALU.mult)
                p.tt(ogo[par][:, :], ogo[par][:, :], sg_all[:, h, tcol],
                     ALU.mult)
                p.dma("sync", og_v[:, h, tcol], ogo[par][:, :],
                      okey=f"ago{tt % 2}")


def _mkda_masks():
    ii = np.arange(128) // 32
    tt_ = np.arange(128) % 32
    t1 = tt_[:, None]; i1 = ii[:, None]
    t2 = tt_[None, :]; i2 = ii[None, :]
    mm_ = (t1 > t2) | ((t1 == t2) & (i1 > i2))
    am = (tt_[:, None] <= np.arange(32)[None, :])
    return mm_.astype(BF), am.astype(BF)


def build_nc_phase_a1(nch=NCH):
    nc = bass.Bass("TRN2", target_bir_lowering=False, debug=False)
    io = {}

    def inp(name, shape, dt=F32):
        io[name] = nc.dram_tensor(name, shape, dt, kind="ExternalInput").ap()
    inp("x0T", [2048, 4096])
    inp("x0bT", [2048, 4096], BF16)
    inp("wqkvl", [72, 128, 16, 128], BF16)
    inp("wbs", [2048, 32], BF16)
    inp("wf1T", [2048, 128], BF16)
    inp("wf2s", [128, 1024], BF16)
    inp("wg1T", [2048, 128], BF16)
    inp("wg2s", [128, 1024], BF16)
    inp("convw", [128, 72, 4])
    inp("alog", [1, 8])
    inp("dtb", [128, 8])
    inp("bg1c", [128, 1])
    inp("bg2c", [128, 8])
    inp("onormc", [128, 1])
    inp("mmask", [128, 128], BF16)
    inp("amask", [128, 32], BF16)
    inp("identity", [128, 128])
    inp("ones", [128, 128])
    inp("epsln", [1, 1])
    io["og"] = nc.dram_tensor("og", [1024, 4096], BF16, kind="ExternalOutput").ap()
    with ExitStack() as stack:
        p = Prog(nc, stack)
        C = emit_consts(p, io)
        scr = emit_phase_a1(p, io, C)
        emit_phase_a1_mkda(p, io, C, scr, nch=nch)
        emit_phase_a1_gate(p, io, C, scr)
        p.end_waits()
        p.install()
    return nc


def _phase_a1_core_inputs(x0, weights, b, hg):
    hs = slice(hg * 8, (hg + 1) * 8)
    chs_q = slice(hg * 1024, (hg + 1) * 1024)
    chs_kv = slice(hg * 4096, (hg + 1) * 4096)
    cw = np.zeros((72, 128, 4), dtype=np.float32)
    cw[0:8] = weights["L0_attn_conv_q"][chs_q].reshape(8, 128, 4)
    cw[8:40] = weights["L0_attn_conv_k"][chs_kv].reshape(32, 128, 4)
    cw[40:72] = weights["L0_attn_conv_v"][chs_kv].reshape(32, 128, 4)
    mm_, am = _mkda_masks()
    return {
        "x0T": np.ascontiguousarray(x0[b].T),
        "x0bT": np.ascontiguousarray(x0[b].T).astype(BF),
        "wqkvl": np.concatenate([
            _tile_gu(np.ascontiguousarray(weights["L0_attn_Wq"][chs_q].T)),
            _tile_gu(np.ascontiguousarray(weights["L0_attn_Wk"][chs_kv].T)),
            _tile_gu(np.ascontiguousarray(weights["L0_attn_Wv"][chs_kv].T)),
        ], axis=0).astype(BF),
        "wbs": np.ascontiguousarray(
            weights["L0_attn_Wb"][hg * 32:(hg + 1) * 32].T).astype(BF),
        "wf1T": np.ascontiguousarray(weights["L0_attn_Wf1"].T).astype(BF),
        "wf2s": np.ascontiguousarray(
            weights["L0_attn_Wf2"][chs_q].T).astype(BF),
        "wg1T": np.ascontiguousarray(weights["L0_attn_Wg1"].T).astype(BF),
        "wg2s": np.ascontiguousarray(
            weights["L0_attn_Wg2"][chs_q].T).astype(BF),
        "convw": np.ascontiguousarray(cw.transpose(1, 0, 2)),
        "alog": np.ascontiguousarray(
            weights["L0_attn_A_log"][hs][None, :]),
        "dtb": np.ascontiguousarray(
            weights["L0_attn_dt_bias"][chs_q].reshape(8, 128).T),
        "bg1c": np.ascontiguousarray(weights["L0_attn_bg1"][:, None]),
        "bg2c": np.ascontiguousarray(
            weights["L0_attn_bg2"][chs_q].reshape(8, 128).T),
        "onormc": np.ascontiguousarray(weights["L0_attn_onorm_w"][:, None]),
        "mmask": mm_, "amask": am,
        **_consts(),
    }



def _run(nc, in_maps):
    return run_bass_kernel_spmd(nc, in_maps, core_ids=list(range(8)))


def kernel(**inputs):
    z = inputs
    x0 = np.asarray(z["L0_attn_hidden_states"])

    nc = build_nc_phase_a1()
    in_maps = [_phase_a1_core_inputs(x0, z, c // 2, c % 2) for c in range(8)]
    res = _run(nc, in_maps)
    og = [res.results[c]["og"] for c in range(8)]

    nc = build_nc_phase_a2()
    woAl = _tile_d(np.ascontiguousarray(z["L0_attn_Wo"].T)).astype(BF)
    consts = _consts()
    in_maps = []
    for c in range(8):
        b, i = c // 2, c % 2
        ogf = np.concatenate([og[2 * b], og[2 * b + 1]], axis=0)
        colsel = np.arange(4096).reshape(32, 128)[i::2].reshape(-1)
        x0own = x0[b].reshape(32, 128, 2048)[i::2].reshape(2048, 2048)
        in_maps.append({
            "ogT": np.ascontiguousarray(ogf[:, colsel]),
            "x0ownT": np.ascontiguousarray(x0own.T),
            "woAl": woAl, **consts,
        })
    res = _run(nc, in_maps)
    x1 = np.zeros((4, 4096, 2048), dtype=np.float32)
    for c in range(8):
        b, i = c // 2, c % 2
        x1[b].reshape(32, 128, 2048)[i::2] = \
            res.results[c]["x1own"].T.reshape(16, 128, 2048)

    x2 = run_phase_b(x1, z)

    x3 = run_phase_c(x2, z)

    y = run_phase_d(x3, z)
    return y
